# version 10
# baseline (speedup 1.0000x reference)
import numpy as np
import ml_dtypes
import concourse.bass as bass
import concourse.mybir as mybir
from concourse.bass_utils import run_bass_kernel_spmd
from contextlib import ExitStack

F32 = mybir.dt.float32
BF16 = mybir.dt.bfloat16
I32 = mybir.dt.int32
AF = mybir.ActivationFunctionType
ALU = mybir.AluOpType

import os
ABL_NO_PW = os.environ.get("ABL_NO_PW") == "1"
ABL_NO_LHLL = os.environ.get("ABL_NO_LHLL") == "1"
ABL_NO_SPREAD = os.environ.get("ABL_NO_SPREAD") == "1"

E = 2048
S = 2048
H = 5632
NEC = E // 128
NHC = H // 128
NSC = S // 128
SGRP = 512
NSG = S // SGRP
SB = 256
NSB = S // SB
EPS = 1e-5
DECAY = 0.95
SCALE2 = float(np.float32(np.sqrt(1.0 - DECAY))) ** 2


class Ctr:
    def __init__(self, handle):
        self.h = handle
        self.n = 0


class Prog:
    def __init__(self):
        self.streams = {k: [] for k in ("t", "v", "s", "y", "g", "g2")}
        self.auto = {}

    def emit(self, key, build, waits=(), inc=None, inc_n=1):
        waits = list(waits)
        if key in self.auto:
            ctr = self.auto[key]
            waits = [(ctr, ctr.n)] + waits
            if inc is None:
                inc = ctr
        if key == "g2":
            key = "g"
        w = [(c.h, v) for (c, v) in waits if v > 0]
        tgt = None
        if inc is not None:
            inc.n += inc_n
            tgt = (inc.h, inc_n)

        def run(eng, w=w, build=build, tgt=tgt):
            for h, v in w:
                eng.wait_ge(h, v)
            ins = build(eng)
            if tgt is not None:
                if ins is None:
                    ins = eng.nop()
                ins.then_inc(*tgt)
        self.streams[key].append(run)


def build_nc():
    nc = bass.Bass("TRN2", target_bir_lowering=False)

    xT = nc.dram_tensor("xT", [E, S], F32, kind="ExternalInput").ap()
    x_tm = nc.dram_tensor("x_tm", [S, E], F32, kind="ExternalInput").ap()
    wg = nc.dram_tensor("wg", [E, E], F32, kind="ExternalInput").ap()
    wv = nc.dram_tensor("wv", [E, E], F32, kind="ExternalInput").ap()
    wq = nc.dram_tensor("wq", [E, E], F32, kind="ExternalInput").ap()
    wo = nc.dram_tensor("wo", [E, E], F32, kind="ExternalInput").ap()
    bg2d = nc.dram_tensor("bg2d", [128, NEC], F32, kind="ExternalInput").ap()
    bq2d = nc.dram_tensor("bq2d", [128, NEC], F32, kind="ExternalInput").ap()
    wgT_i = nc.dram_tensor("wgT_i", [E, H], I32, kind="ExternalInput").ap()
    wuT_i = nc.dram_tensor("wuT_i", [E, H], I32, kind="ExternalInput").ap()
    wdT_i = nc.dram_tensor("wdT_i", [H, E], I32, kind="ExternalInput").ap()
    sg2d = nc.dram_tensor("sg2d", [128, NHC], F32, kind="ExternalInput").ap()
    su2d = nc.dram_tensor("su2d", [128, NHC], F32, kind="ExternalInput").ap()
    sd2d = nc.dram_tensor("sd2d", [128, NEC], F32, kind="ExternalInput").ap()
    dec_i = nc.dram_tensor("dec_i", [128, SGRP], F32, kind="ExternalInput").ap()
    ident_i = nc.dram_tensor("ident_i", [128, 128], F32, kind="ExternalInput").ap()
    onc_i = nc.dram_tensor("onc_i", [128, 1], BF16, kind="ExternalInput").ap()
    onr_i = nc.dram_tensor("onr_i", [1, 128], BF16, kind="ExternalInput").ap()

    out_fm = nc.dram_tensor("out_fm", [E, S], F32, kind="ExternalOutput").ap()
    load_hl_scr = nc.dram_tensor("load_hl_scr", [NSC, 2, 128, E], BF16).ap()
    wg_hl_scr = nc.dram_tensor("wg_hl_scr", [NEC, 2, 128, E], BF16).ap()
    wv_hl_scr = nc.dram_tensor("wv_hl_scr", [NEC, 2, 128, E], BF16).ap()
    wq_hl_scr = nc.dram_tensor("wq_hl_scr", [NEC, 2, 128, E], BF16).ap()
    wo_hl_scr = nc.dram_tensor("wo_hl_scr", [NEC, 2, 128, E], BF16).ap()
    yfh_scr = nc.dram_tensor("yfh_scr", [NEC, 128, S], BF16).ap()
    yfl_scr = nc.dram_tensor("yfl_scr", [NEC, 128, S], BF16).ap()
    wgT_bf = nc.dram_tensor("wgT_bf", [E, H], BF16).ap()
    wuT_bf = nc.dram_tensor("wuT_bf", [E, H], BF16).ap()
    wdT_bf = nc.dram_tensor("wdT_bf", [H, E], BF16).ap()

    prog = Prog()
    P = prog.emit

    with ExitStack() as g_stack:
        ge = g_stack.enter_context
        d_f = Ctr(ge(nc.semaphore("d_f")))
        d_w = [Ctr(ge(nc.semaphore(f"d_w{i}"))) for i in range(2)]
        d_x = [Ctr(ge(nc.semaphore(f"d_x{i}"))) for i in range(2)]
        d_c = Ctr(ge(nc.semaphore("d_c")))
        d_d = Ctr(ge(nc.semaphore("d_d")))
        d_d2 = Ctr(ge(nc.semaphore("d_d2")))
        d_e = [Ctr(ge(nc.semaphore(f"d_e{i}"))) for i in range(2)]
        d_as = [Ctr(ge(nc.semaphore(f"d_as{i}"))) for i in range(2)]
        d_st = [Ctr(ge(nc.semaphore(f"d_st{i}"))) for i in range(3)]
        d_pw = [Ctr(ge(nc.semaphore(f"d_pw{i}"))) for i in range(4)]
        d_wo = [Ctr(ge(nc.semaphore(f"d_wo{i}"))) for i in range(2)]
        t_p = Ctr(ge(nc.semaphore("t_p")))
        v_p = Ctr(ge(nc.semaphore("v_p")))
        s_p = Ctr(ge(nc.semaphore("s_p")))
        g_p = Ctr(ge(nc.semaphore("g_p")))
        prog.auto = {"v": v_p, "s": s_p, "g2": g_p}
        ALL_DMA = [d_f, d_w[0], d_w[1], d_x[0], d_x[1], d_c, d_d, d_d2, d_e[0], d_e[1],
                   d_as[0], d_as[1], d_st[0], d_st[1], d_st[2],
                   d_pw[0], d_pw[1], d_pw[2], d_pw[3], d_wo[0], d_wo[1]]

        def barrier():
            for key, ctr in (("t", t_p), ("v", v_p), ("s", s_p)):
                P(key, lambda e: e.nop(), waits=[(ctr, ctr.n)], inc=ctr)
            marks = [(t_p, t_p.n), (v_p, v_p.n), (s_p, s_p.n)] + \
                    [(c, c.n) for c in ALL_DMA]
            for key in ("t", "v", "s", "y", "g"):
                def run(eng, marks=tuple(marks)):
                    for c, v in marks:
                        if v > 0:
                            eng.wait_ge(c.h, v)
                prog.streams[key].append(run)

        bg_t = ge(nc.sbuf_tensor("bg_t", [128, NEC], F32))
        bq_t = ge(nc.sbuf_tensor("bq_t", [128, NEC], F32))
        sg_t = ge(nc.sbuf_tensor("sg_t", [128, NHC], F32))
        su_t = ge(nc.sbuf_tensor("su_t", [128, NHC], F32))
        sd_t = ge(nc.sbuf_tensor("sd_t", [128, NEC], F32))
        dec_t = ge(nc.sbuf_tensor("dec_t", [128, SGRP], F32))
        ident = ge(nc.sbuf_tensor("ident", [128, 128], F32))
        onc = ge(nc.sbuf_tensor("onc", [128, 1], BF16))
        onr = ge(nc.sbuf_tensor("onr", [1, 128], BF16))
        carry = ge(nc.sbuf_tensor("carry", [128, NEC], F32))
        stats = ge(nc.sbuf_tensor("stats", [128, 16], F32))
        srow = ge(nc.sbuf_tensor("srow", [1, 4 * SB], F32))
        srow_bf = ge(nc.sbuf_tensor("srow_bf", [1, 4 * SB], BF16))

        for dst, src in ((bg_t, bg2d), (bq_t, bq2d), (sg_t, sg2d), (su_t, su2d),
                         (sd_t, sd2d), (dec_t, dec_i), (ident, ident_i),
                         (onc, onc_i), (onr, onr_i)):
            P("y", lambda e, d=dst, s=src: e.dma_start(out=d[:, :], in_=s),
              inc=d_f, inc_n=16)
        glob_ld = d_f.n

        p1_stack = ExitStack()
        xgh = [p1_stack.enter_context(nc.sbuf_tensor(f"xgh{i}", [128, NEC * SGRP], BF16)) for i in range(2)]
        xgl = [p1_stack.enter_context(nc.sbuf_tensor(f"xgl{i}", [128, NEC * SGRP], BF16)) for i in range(2)]
        xstg = [p1_stack.enter_context(nc.sbuf_tensor(f"xstg{i}", [128, SGRP], F32)) for i in range(2)]
        wgs = [p1_stack.enter_context(nc.sbuf_tensor(f"wgs{i}", [128, 2 * E], BF16)) for i in range(2)]
        wvs = [p1_stack.enter_context(nc.sbuf_tensor(f"wvs{i}", [128, 2 * E], BF16)) for i in range(2)]
        wqs = [p1_stack.enter_context(nc.sbuf_tensor(f"wqs{i}", [128, 2 * E], BF16)) for i in range(2)]
        wtmp = [p1_stack.enter_context(nc.sbuf_tensor(f"wtmp{i}", [128, E], F32)) for i in range(3)]
        gate_sb = [p1_stack.enter_context(nc.sbuf_tensor(f"gate{i}", [128, SGRP], F32)) for i in range(2)]
        que_sb = [p1_stack.enter_context(nc.sbuf_tensor(f"que{i}", [128, SGRP], F32)) for i in range(2)]
        store_sb = [p1_stack.enter_context(nc.sbuf_tensor(f"store{i}", [128, SGRP], F32)) for i in range(2)]
        mem_sb = [p1_stack.enter_context(nc.sbuf_tensor(f"mem{i}", [128, SGRP], F32)) for i in range(2)]
        load_sb = [p1_stack.enter_context(nc.sbuf_tensor(f"load{i}", [128, SGRP], F32)) for i in range(2)]
        lh_sb = [p1_stack.enter_context(nc.sbuf_tensor(f"lh{i}", [128, SGRP], BF16)) for i in range(2)]
        ll_sb = [p1_stack.enter_context(nc.sbuf_tensor(f"ll{i}", [128, SGRP], BF16)) for i in range(2)]
        PWW = 704
        wint = [p1_stack.enter_context(nc.sbuf_tensor(f"wint{i}", [128, PWW], I32)) for i in range(4)]
        wbf = [p1_stack.enter_context(nc.sbuf_tensor(f"wbf{i}", [128, PWW], BF16)) for i in range(2)]
        wof = p1_stack.enter_context(nc.sbuf_tensor("wof", [128, E], F32))
        woh_sl = p1_stack.enter_context(nc.sbuf_tensor("woh_sl", [128, E], BF16))
        wol_sl = p1_stack.enter_context(nc.sbuf_tensor("wol_sl", [128, E], BF16))

        p1_psum = ExitStack()
        p_gate = [p1_psum.enter_context(nc.psum_tensor(f"pg{i}", [128, SGRP], F32)) for i in range(2)]
        p_v = [p1_psum.enter_context(nc.psum_tensor(f"pv{i}", [128, SGRP], F32)) for i in range(2)]
        p_que = [p1_psum.enter_context(nc.psum_tensor(f"pq{i}", [128, SGRP], F32)) for i in range(2)]

        pw_items = []
        for k in range(NEC):
            for h2 in range(8):
                pw_items.append(("ffn", wgT_i[k * 128:(k + 1) * 128, h2 * PWW:(h2 + 1) * PWW],
                                 wgT_bf[k * 128:(k + 1) * 128, h2 * PWW:(h2 + 1) * PWW], PWW))
        for k in range(NEC):
            for h2 in range(8):
                pw_items.append(("ffn", wuT_i[k * 128:(k + 1) * 128, h2 * PWW:(h2 + 1) * PWW],
                                 wuT_bf[k * 128:(k + 1) * 128, h2 * PWW:(h2 + 1) * PWW], PWW))
        for k in range(NHC):
            for h2 in range(4):
                pw_items.append(("ffn", wdT_i[k * 128:(k + 1) * 128, h2 * 512:(h2 + 1) * 512],
                                 wdT_bf[k * 128:(k + 1) * 128, h2 * 512:(h2 + 1) * 512], 512))
        wo_items = [("wo", wo[k * 128:(k + 1) * 128, :],
                     (wo_hl_scr[k, 0], wo_hl_scr[k, 1]), E) for k in range(NEC)]
        for k in range(NEC):
            pw_items.insert(80 + k * 8, wo_items[k])
        if ABL_NO_PW:
            pw_items = []
        pw_state = {"emitted": 0, "in": [], "conv": [], "outs": []}

        def emit_pw_step():
            k = pw_state["emitted"]
            if k >= len(pw_items):
                return
            pw_state["emitted"] += 1
            kind, src_, dst, width = pw_items[k]
            if kind == "ffn":
                fk = pw_state.setdefault("fk", 0)
                pw_state["fk"] = fk + 1
                p4 = fk % 4
                p2 = fk % 2
                ch_in = d_pw[p4]
                w_in = []
                cm = pw_state.get(("fconv", p4))
                if cm is not None:
                    w_in.append((s_p, cm))
                P("y", lambda e, s=src_, wd=width, b=wint[p4]: e.dma_start(
                    out=b[:, :wd], in_=s), waits=w_in, inc=ch_in, inc_n=16)
                in_mark = ch_in.n
                w_cv = [(ch_in, in_mark)]
                om = pw_state.get(("fout", p2))
                if om is not None:
                    w_cv.append(((d_d if p2 == 0 else d_d2), om))
                P("s", lambda e, wd=width, b=wint[p4], o=wbf[p2]: e.activation(
                    o[:, :wd], b[:, :wd], AF.Copy, bias=-128.0),
                  waits=w_cv, inc=s_p)
                pw_state[("fconv", p4)] = s_p.n
                ch_out = d_d if p2 == 0 else d_d2
                P("g", lambda e, d=dst, wd=width, o=wbf[p2]: e.dma_start(
                    out=d, in_=o[:, :wd]),
                  waits=[(s_p, s_p.n)], inc=ch_out, inc_n=16)
                pw_state[("fout", p2)] = ch_out.n
            else:
                w_in = []
                if pw_state.get("wo_sub") is not None:
                    w_in.append((v_p, pw_state["wo_sub"]))
                P("g", lambda e, s=src_: e.dma_start(out=wof[:, :], in_=s),
                  waits=w_in, inc=d_wo[0], inc_n=16)
                in_mark = d_wo[0].n
                w_cv = [(d_wo[0], in_mark)]
                if pw_state.get("wo_out") is not None:
                    w_cv.append((d_wo[1], pw_state["wo_out"]))
                P("s", lambda e: e.activation(woh_sl[:, :], wof[:, :], AF.Copy),
                  waits=w_cv, inc=s_p)
                hm = s_p.n
                P("v", lambda e: e.tensor_tensor(wol_sl[:, :], wof[:, :], woh_sl[:, :], ALU.subtract),
                  waits=[(s_p, hm)])
                lm = v_p.n
                pw_state["wo_sub"] = lm
                P("g", lambda e, d=dst[0]: e.dma_start(out=d, in_=woh_sl[:, :]),
                  waits=[(s_p, hm)], inc=d_wo[1], inc_n=16)
                P("g", lambda e, d=dst[1]: e.dma_start(out=d, in_=wol_sl[:, :]),
                  waits=[(v_p, lm)], inc=d_wo[1], inc_n=16)
                pw_state["wo_out"] = d_wo[1].n

        def w_slice_view(w, ms):
            return w.rearrange("(j p) m -> p j m", p=128)[:, :, ms * 128:(ms + 1) * 128]

        xg_mark = [0, 0]
        xs_mark = {}
        w_mark = [0, 0]
        w_split = {}
        proj_done = {}
        evac_done = {}
        store_done = {}
        N_ITER = NSG * NEC
        AS_FINAL = 6 * (NEC // 2) * 16

        def emit_xsplit_dma(sg, j):
            par = j % 2
            key = (sg, j - 2) if j >= 2 else ((sg - 1, 14 + j) if sg >= 1 else None)
            w = [(v_p, sub_marks[key])] if key in sub_marks else []
            P("y", lambda e, j=j, sg=sg, stg=xstg[par]: e.dma_start(
                out=stg[:, :],
                in_=xT[j * 128:(j + 1) * 128, sg * SGRP:(sg + 1) * SGRP]),
              waits=w, inc=d_x[par], inc_n=16)
            xdma_marks[(sg, j)] = d_x[par].n

        def emit_xsplit_ops(sg, j):
            par = j % 2
            b = sg % 2
            guard = []
            if sg >= 2 and j == 0:
                guard = [(t_p, proj_done[((sg - 2) * NEC + NEC - 1, "q")])]
            dm = xdma_marks[(sg, j)]
            P("s", lambda e, j=j, stg=xstg[par], o=xgh[b]: e.activation(
                o[:, j * SGRP:(j + 1) * SGRP], stg[:, :], AF.Copy),
              waits=[(d_x[par], dm)] + guard)
            cast_m = s_p.n
            P("v", lambda e, j=j, stg=xstg[par], h=xgh[b], o=xgl[b]: e.tensor_tensor(
                o[:, j * SGRP:(j + 1) * SGRP], stg[:, :],
                h[:, j * SGRP:(j + 1) * SGRP], ALU.subtract),
              waits=[(s_p, cast_m)])
            sub_marks[(sg, j)] = v_p.n
            if j == NEC - 1:
                xs_mark[sg] = (cast_m, v_p.n)

        def emit_xsplit_chunk(sg, j):
            emit_xsplit_dma(sg, j)
            emit_xsplit_ops(sg, j)
        xdma_marks = {}

        sub_marks = {}
        wtmp_state = {}
        as_mark = [0, 0]

        def emit_p1_dmas(i):
            sg, ms = divmod(i, NEC)
            par = i % 2
            if sg == 0:
                marks = []
                for k, wt in enumerate((wg, wv, wq)):
                    wi = wtmp_state.get("idx", 0)
                    wtmp_state["idx"] = wi + 1
                    tb = wtmp[wi % 3]
                    w = []
                    prev = wtmp_state.get(("sub", wi % 3))
                    if prev is not None:
                        w.append((v_p, prev))
                    if i >= 2 and k == 0:
                        w.append((t_p, proj_done[(i - 2, "q")]))
                        w.append((d_as[par], as_mark[par]))
                    ch = d_st[wi % 3]
                    P("y", lambda e, wt=wt, ms=ms, tb=tb: e.dma_start(
                        out=tb[:, :].rearrange("p (j m) -> p j m", m=128),
                        in_=w_slice_view(wt, ms)), waits=w, inc=ch, inc_n=16)
                    marks.append((wi, ch.n))
                wtmp_state[("dmas", i)] = marks
            else:
                w = [(d_as[0], AS_FINAL), (d_as[1], AS_FINAL)] if sg == 1 else []
                if i >= 2:
                    w.append((t_p, proj_done[(i - 2, "q")]))
                for scr, buf in ((wg_hl_scr, wgs[par]), (wv_hl_scr, wvs[par]),
                                 (wq_hl_scr, wqs[par])):
                    P("y", lambda e, scr=scr, ms=ms, b=buf: e.dma_start(
                        out=b[:, :].rearrange("p (hl m) -> p hl m", m=E),
                        in_=scr[ms].rearrange("hl p m -> p hl m")),
                      waits=w, inc=d_w[par], inc_n=16)
                    w = []
            w_mark[par] = d_w[par].n

        def emit_p1_splits(i):
            sg, ms = divmod(i, NEC)
            par = i % 2
            if sg != 0:
                return
            marks = wtmp_state.pop(("dmas", i))
            for (wi, dm), (proj, buf, scr) in zip(marks, (
                    ("g", wgs[par], wg_hl_scr), ("v", wvs[par], wv_hl_scr),
                    ("q", wqs[par], wq_hl_scr))):
                ch = d_st[wi % 3]
                tb = wtmp[wi % 3]
                P("s", lambda e, buf=buf, tb=tb: e.activation(buf[:, :E], tb[:, :], AF.Copy),
                  waits=[(ch, dm)])
                hm = s_p.n
                P("v", lambda e, buf=buf, tb=tb: e.tensor_tensor(
                    buf[:, E:2 * E], tb[:, :], buf[:, :E], ALU.subtract),
                  waits=[(s_p, hm)])
                lm = v_p.n
                wtmp_state[("sub", wi % 3)] = lm
                w_split[(i, proj)] = (hm, lm)
                P("g", lambda e, scr=scr, ms=ms, buf=buf: e.dma_start(
                    out=scr[ms, 0], in_=buf[:, :E]),
                  waits=[(s_p, hm)], inc=d_as[par], inc_n=16)
                P("g", lambda e, scr=scr, ms=ms, buf=buf: e.dma_start(
                    out=scr[ms, 1], in_=buf[:, E:2 * E]),
                  waits=[(v_p, lm)], inc=d_as[par], inc_n=16)
            as_mark[par] = d_as[par].n

        for i in range(N_ITER):
            sg, ms = divmod(i, NEC)
            par = i % 2
            if i == 0:
                for j in range(NEC):
                    emit_xsplit_chunk(0, j)
                emit_p1_dmas(0)
                emit_p1_splits(0)
            if sg + 1 < NSG and 6 <= ms < 14:
                for j in range(2 * (ms - 6), 2 * (ms - 6) + 2):
                    emit_xsplit_dma(sg + 1, j)
            if i + 1 < N_ITER:
                emit_p1_dmas(i + 1)
                if (i + 1) < NEC:
                    emit_p1_splits(i + 1)

            for proj, wb, ps in (("g", wgs, p_gate), ("v", wvs, p_v), ("q", wqs, p_que)):
                waits = [(d_f, glob_ld),
                         (s_p, xs_mark[sg][0]), (v_p, xs_mark[sg][1])]
                if sg == 0:
                    hm, lm = w_split[(i, proj)]
                    waits += [(s_p, hm), (v_p, lm)]
                else:
                    waits.append((d_w[par], w_mark[par]))
                if i >= 2:
                    kind = {"g": "gate_evac", "v": "store_mult", "q": "que_evac"}[proj]
                    ctr = s_p if proj in ("g", "q") else v_p
                    waits.append((ctr, evac_done[(i - 2, kind)]))

                def mm_run(eng, wb=wb[par], ps=ps[par],
                           xh=xgh[sg % 2], xl=xgl[sg % 2]):
                    ins = None
                    for j in range(NEC):
                        ins = eng.matmul(ps[:, :], wb[:, j * 128:(j + 1) * 128],
                                         xh[:, j * SGRP:(j + 1) * SGRP],
                                         start=(j == 0), stop=False)
                        ins = eng.matmul(ps[:, :], wb[:, E + j * 128:E + (j + 1) * 128],
                                         xh[:, j * SGRP:(j + 1) * SGRP],
                                         start=False, stop=False)
                        ins = eng.matmul(ps[:, :], wb[:, j * 128:(j + 1) * 128],
                                         xl[:, j * SGRP:(j + 1) * SGRP],
                                         start=False, stop=(j == NEC - 1))
                    return ins
                P("t", mm_run, waits=waits, inc=t_p)
                proj_done[(i, proj)] = t_p.n

            gw = [(t_p, proj_done[(i, "g")])]
            if i >= 2:
                gw.append((v_p, evac_done[(i - 2, "store_mult")]))
            P("s", lambda e, ps=p_gate[par], o=gate_sb[par], ms=ms: e.activation(
                o[:, :], ps[:, :], AF.Sigmoid, bias=bg_t[:, ms:ms + 1]),
              waits=gw, inc=s_p)
            evac_done[(i, "gate_evac")] = s_p.n
            qw = [(t_p, proj_done[(i, "q")])]
            if i >= 2:
                qw.append((v_p, evac_done[(i - 2, "load")]))
            P("s", lambda e, ps=p_que[par], o=que_sb[par], ms=ms: e.activation(
                o[:, :], ps[:, :], AF.Sigmoid, bias=bq_t[:, ms:ms + 1]),
              waits=qw, inc=s_p)
            evac_done[(i, "que_evac")] = s_p.n

            P("v", lambda e, ps=p_v[par], g=gate_sb[par], o=store_sb[par]: e.tensor_tensor(
                o[:, :], ps[:, :], g[:, :], ALU.mult),
              waits=[(t_p, proj_done[(i, "v")]), (s_p, evac_done[(i, "gate_evac")])],
              inc=v_p)
            evac_done[(i, "store_mult")] = v_p.n
            fence_pre = v_p.n
            if sg == 0:
                init = 0.0
            else:
                init = carry[:, ms:ms + 1]
            P("v", lambda e, st=store_sb[par], m=mem_sb[par], init=init: e.tensor_tensor_scan(
                m[:, :], dec_t[:, :], st[:, :], init, ALU.mult, ALU.add),
              waits=[(v_p, fence_pre)], inc=v_p)
            fence_post = v_p.n
            P("v", lambda e, m=mem_sb[par], ms=ms: e.tensor_copy(
                carry[:, ms:ms + 1], m[:, SGRP - 1:SGRP]),
              waits=[(v_p, fence_post)], inc=v_p)
            lw = [(v_p, fence_post), (s_p, evac_done[(i, "que_evac")])]
            if i >= 2 and evac_done.get((i - 2, "lh_cast"), 0):
                lw.append((g_p, evac_done[(i - 2, "lh_cast")]))
            P("v", lambda e, m=mem_sb[par], q=que_sb[par], o=load_sb[par]: e.scalar_tensor_tensor(
                o[:, :], m[:, :], 0.05, q[:, :], ALU.mult, ALU.mult),
              waits=lw, inc=v_p)
            load_ready = v_p.n
            evac_done[(i, "load")] = load_ready

            if not ABL_NO_LHLL:
              sw = [(v_p, load_ready)]
              if i >= 2:
                sw.append((d_e[par], store_done[i - 2]))
              P("g2", lambda e, o=load_sb[par], h=lh_sb[par]: e.tensor_copy(
                h[:, :], o[:, :]), waits=sw)
              evac_done[(i, "lh_cast")] = g_p.n
              P("g2", lambda e, o=load_sb[par], h=lh_sb[par], l=ll_sb[par]: e.tensor_tensor(
                l[:, :], o[:, :], h[:, :], ALU.subtract))
              ll_mark = g_p.n

              P("g", lambda e, o=lh_sb[par], ms=ms, sg=sg: e.dma_start(
                out=load_hl_scr[sg * 4:(sg + 1) * 4, 0, :, ms * 128:(ms + 1) * 128]
                    .rearrange("c p s -> p c s"),
                in_=o[:, :].rearrange("p (c s) -> p c s", s=128)),
                waits=[(g_p, evac_done[(i, "lh_cast")])], inc=d_e[par], inc_n=16)
              P("g", lambda e, o=ll_sb[par], ms=ms, sg=sg: e.dma_start(
                out=load_hl_scr[sg * 4:(sg + 1) * 4, 1, :, ms * 128:(ms + 1) * 128]
                    .rearrange("c p s -> p c s"),
                in_=o[:, :].rearrange("p (c s) -> p c s", s=128)),
                waits=[(g_p, ll_mark)], inc=d_e[par], inc_n=16)
              store_done[i] = d_e[par].n

            if sg + 1 < NSG and 6 <= ms < 14:
                for j in range(2 * (ms - 6), 2 * (ms - 6) + 2):
                    emit_xsplit_ops(sg + 1, j)
            if i + 1 >= NEC and i + 1 < N_ITER:
                emit_p1_splits(i + 1)

            while pw_state["emitted"] * N_ITER < len(pw_items) * (i + 1):
                emit_pw_step()

        while pw_state["emitted"] < len(pw_items):
            emit_pw_step()

        p1_stack.close()
        p1_psum.close()
        barrier()

        p2_stack = ExitStack()
        wo_s = p2_stack.enter_context(nc.sbuf_tensor("wo_s", [128, 2 * NEC * E], BF16))
        lc = [p2_stack.enter_context(nc.sbuf_tensor(f"lc{i}", [128, 2 * E], BF16)) for i in range(2)]
        xc = p2_stack.enter_context(nc.sbuf_tensor("xc", [128, E], F32))
        ry = p2_stack.enter_context(nc.sbuf_tensor("ry", [128, E], F32))
        yfh = p2_stack.enter_context(nc.sbuf_tensor("yfh", [128, E], BF16))
        yfl = p2_stack.enter_context(nc.sbuf_tensor("yfl", [128, E], BF16))
        garb = p2_stack.enter_context(nc.sbuf_tensor("garb", [128, E], BF16))
        p2_psum = ExitStack()
        p_attn = p2_psum.enter_context(nc.psum_tensor("p_attn", [128, E], F32))
        p_yT = p2_psum.enter_context(nc.psum_tensor("p_yT", [128, E], F32))

        WOL = NEC * E
        for hl in range(2):
            P("y", lambda e, hl=hl: e.dma_start(
                out=wo_s[:, hl * WOL:(hl + 1) * WOL].rearrange("p (j e) -> p j e", e=E),
                in_=wo_hl_scr[:, hl].rearrange("j p e -> p j e")), inc=d_f, inc_n=16)
        wo_mark = d_f.n

        lc_mark = [0, 0]
        xc_mark = [0]
        attn_done = {}
        tr_done = {}
        r_done = {}
        y_done = {}
        yfh_done = {}
        yfl_done = {}
        yf_dma = {}

        def emit_p2_loads(c):
            par = c % 2
            w = []
            if c >= 2:
                w.append((t_p, attn_done[c - 2]))
            P("y", lambda e, c=c, b=lc[par]: e.dma_start(
                out=b[:, :].rearrange("p (hl m) -> p hl m", m=E),
                in_=load_hl_scr[c].rearrange("hl p m -> p hl m")),
              waits=w, inc=d_w[par], inc_n=16)
            lc_mark[par] = d_w[par].n
            w = []
            if c >= 1:
                w.append((v_p, r_done[c - 1]))
            P("y", lambda e, c=c: e.dma_start(
                out=xc[:, :], in_=x_tm[c * 128:(c + 1) * 128, :]),
              waits=w, inc=d_x[0], inc_n=16)
            xc_mark[0] = d_x[0].n

        emit_p2_loads(0)

        def emit_p2_transposes(c):
            w = [(v_p, y_done[c])]
            if c >= 1:
                w += [(s_p, yfh_done[c - 1]), (v_p, yfl_done[c - 1])]

            def tr_run(eng):
                ins = None
                for j in range(NEC):
                    ins = eng.transpose(p_yT[:, j * 128:(j + 1) * 128],
                                        ry[:, j * 128:(j + 1) * 128], ident[:, :])
                return ins
            P("t", tr_run, waits=w, inc=t_p)
            tr_done[c] = t_p.n
            wv_ = [(t_p, tr_done[c])]
            if c >= 1:
                wv_.append((d_e[0], yf_dma.get(c - 1, 0)))
            P("s", lambda e: e.activation(yfh[:, :], p_yT[:, :], AF.Copy),
              waits=wv_, inc=s_p)
            yfh_done[c] = s_p.n
            P("v", lambda e: e.tensor_tensor(yfl[:, :], p_yT[:, :], yfh[:, :], ALU.subtract),
              waits=[(s_p, yfh_done[c])], inc=v_p)
            yfl_done[c] = v_p.n
            P("g", lambda e, c=c: e.dma_start(
                out=yfh_scr.rearrange("j p s -> p j s")[:, :, c * 128:(c + 1) * 128],
                in_=yfh[:, :].rearrange("p (j s) -> p j s", s=128)),
              waits=[(s_p, yfh_done[c])], inc=d_e[0], inc_n=16)
            P("g", lambda e, c=c: e.dma_start(
                out=yfl_scr.rearrange("j p s -> p j s")[:, :, c * 128:(c + 1) * 128],
                in_=yfl[:, :].rearrange("p (j s) -> p j s", s=128)),
              waits=[(v_p, yfl_done[c])], inc=d_e[0], inc_n=16)
            yf_dma[c] = d_e[0].n

        for c in range(NSC):
            par = c % 2
            waits = [(d_w[par], lc_mark[par]), (d_f, wo_mark)]
            if c >= 1:
                waits.append((v_p, r_done[c - 1]))

            def attn_run(eng, b=lc[par]):
                ins = None
                WOL = NEC * E
                for et in range(4):
                    for j in range(NEC):
                        slh = slice(j * E + et * 512, j * E + (et + 1) * 512)
                        sll = slice(WOL + j * E + et * 512, WOL + j * E + (et + 1) * 512)
                        ins = eng.matmul(p_attn[:, et * 512:(et + 1) * 512],
                                         b[:, j * 128:(j + 1) * 128], wo_s[:, slh],
                                         start=(j == 0), stop=False)
                        ins = eng.matmul(p_attn[:, et * 512:(et + 1) * 512],
                                         b[:, E + j * 128:E + (j + 1) * 128], wo_s[:, slh],
                                         start=False, stop=False)
                        ins = eng.matmul(p_attn[:, et * 512:(et + 1) * 512],
                                         b[:, j * 128:(j + 1) * 128], wo_s[:, sll],
                                         start=False, stop=(j == NEC - 1))
                return ins
            P("t", attn_run, waits=waits, inc=t_p)
            attn_done[c] = t_p.n

            if c >= 1:
                emit_p2_transposes(c - 1)

            rw = [(t_p, attn_done[c]), (d_x[0], xc_mark[0])]
            if c >= 1:
                rw.append((t_p, tr_done[c - 1]))
            P("v", lambda e: e.scalar_tensor_tensor(
                ry[:, :], p_attn[:, :], 1.0, xc[:, :], ALU.mult, ALU.add,
                accum_out=stats[:, 0:1]),
              waits=rw, inc=v_p)
            r_done[c] = v_p.n
            if c + 1 < NSC:
                emit_p2_loads(c + 1)
            P("s", lambda e: e.activation(garb[:, :], ry[:, :], AF.Square,
                                          accum_out=stats[:, 1:2]),
              waits=[(v_p, r_done[c])], inc=s_p)
            sq_mark = s_p.n
            P("v", lambda e: e.tensor_scalar_mul(stats[:, 2:4], stats[:, 0:2], 1.0 / E),
              waits=[(s_p, sq_mark)])
            P("v", lambda e: e.tensor_tensor(stats[:, 4:5], stats[:, 2:3], stats[:, 2:3], ALU.mult))
            P("v", lambda e: e.tensor_scalar(stats[:, 4:5], stats[:, 4:5], -1.0, EPS,
                                             ALU.mult, ALU.add))
            P("v", lambda e: e.tensor_tensor(stats[:, 5:6], stats[:, 4:5], stats[:, 3:4], ALU.add))
            P("s", lambda e: e.activation(stats[:, 7:8], stats[:, 5:6], AF.Sqrt),
              waits=[(v_p, v_p.n)])
            sqrt_mark = s_p.n
            P("v", lambda e: e.reciprocal(stats[:, 11:12], stats[:, 7:8]),
              waits=[(s_p, sqrt_mark)])
            P("v", lambda e: e.tensor_scalar(ry[:, :], ry[:, :], stats[:, 2:3], stats[:, 11:12],
                                             ALU.subtract, ALU.mult))
            y_done[c] = v_p.n

        emit_p2_transposes(NSC - 1)
        p2_stack.close()
        p2_psum.close()
        barrier()

        p3_stack = ExitStack()
        yfh_s = [p3_stack.enter_context(nc.sbuf_tensor(f"yfh_s{i}", [128, NEC * SB], BF16)) for i in range(2)]
        yfl_s = [p3_stack.enter_context(nc.sbuf_tensor(f"yfl_s{i}", [128, NEC * SB], BF16)) for i in range(2)]
        hh_h = p3_stack.enter_context(nc.sbuf_tensor("hh_h", [128, NHC * SB], BF16))
        hh_l = p3_stack.enter_context(nc.sbuf_tensor("hh_l", [128, NHC * SB], BF16))
        wgt_s = [p3_stack.enter_context(nc.sbuf_tensor(f"wgt{i}", [128, E], BF16)) for i in range(2)]
        wut_s = [p3_stack.enter_context(nc.sbuf_tensor(f"wut{i}", [128, E], BF16)) for i in range(2)]
        wdt_s = [p3_stack.enter_context(nc.sbuf_tensor(f"wdt{i}", [128, NHC * 128], BF16)) for i in range(2)]
        gsil = [p3_stack.enter_context(nc.sbuf_tensor(f"gsil{i}", [128, SB], F32)) for i in range(2)]
        hhf = [p3_stack.enter_context(nc.sbuf_tensor(f"hhf{i}", [128, SB], F32)) for i in range(2)]
        r2 = p3_stack.enter_context(nc.sbuf_tensor("r2", [128, NEC * SB], F32))
        r2h = p3_stack.enter_context(nc.sbuf_tensor("r2h", [128, NEC * SB], BF16))
        r2l = p3_stack.enter_context(nc.sbuf_tensor("r2l", [128, NEC * SB], BF16))
        r2sq = p3_stack.enter_context(nc.sbuf_tensor("r2sq", [128, NEC * SB], BF16))
        bc = p3_stack.enter_context(nc.sbuf_tensor("bc", [128, 2 * SB], F32))
        p3_psum = ExitStack()
        p_g = [p3_psum.enter_context(nc.psum_tensor(f"p3g{i}", [128, SB], F32)) for i in range(2)]
        p_u = [p3_psum.enter_context(nc.psum_tensor(f"p3u{i}", [128, SB], F32)) for i in range(2)]
        p_f = [p3_psum.enter_context(nc.psum_tensor(f"p3f{i}", [128, SB], F32)) for i in range(2)]
        p_stat = p3_psum.enter_context(nc.psum_tensor("p_stat", [128, 2 * SB], F32))
        p_bc = p3_psum.enter_context(nc.psum_tensor("p_bc", [128, 2 * SB], F32))

        yf_mark = [0]
        wgu_mark = [0, 0]
        wdt_mark = [0, 0]
        gu_done = {}
        silu_done = {}
        hhf_done = {}
        hhh_done = {}
        hhl_done = {}
        down_done = {}
        r2_done = {}
        out_dma = {}
        stat_done = {}

        def emit_ln2(sb):
                last_r2 = r2_done[(sb, NEC - 1)]
                P("s", lambda e: e.activation(r2h[:, :], r2[:, :], AF.Copy),
                  waits=[(v_p, last_r2)], inc=s_p)
                r2h_mark = s_p.n
                P("v", lambda e: e.tensor_tensor(r2l[:, :], r2[:, :], r2h[:, :], ALU.subtract),
                  waits=[(s_p, r2h_mark)], inc=v_p)
                P("v", lambda e: e.tensor_tensor(r2sq[:, :], r2[:, :], r2[:, :], ALU.mult),
                  inc=v_p)
                split_mark = v_p.n

                def stat_run(eng):
                    first = True
                    for dc in range(NEC):
                        for part in (r2h, r2l):
                            eng.matmul(p_stat[0:1, 0:SB], onc[:, :],
                                       part[:, dc * SB:(dc + 1) * SB],
                                       start=first, stop=(dc == NEC - 1 and part is r2l))
                            first = False
                    ins = None
                    for dc in range(NEC):
                        ins = eng.matmul(p_stat[0:1, SB:2 * SB], onc[:, :],
                                         r2sq[:, dc * SB:(dc + 1) * SB],
                                         start=(dc == 0), stop=(dc == NEC - 1))
                    return ins
                P("t", stat_run, waits=[(v_p, split_mark)], inc=t_p)
                stat_done[sb] = t_p.n

                P("v", lambda e: e.tensor_scalar_mul(srow[0:1, 0:2 * SB], p_stat[0:1, 0:2 * SB], 1.0 / E),
                  waits=[(t_p, stat_done[sb])])
                P("v", lambda e: e.tensor_tensor(srow[0:1, 2 * SB:3 * SB], srow[0:1, 0:SB], srow[0:1, 0:SB], ALU.mult))
                P("v", lambda e: e.tensor_scalar(srow[0:1, 2 * SB:3 * SB], srow[0:1, 2 * SB:3 * SB],
                                                 -1.0, EPS, ALU.mult, ALU.add))
                P("v", lambda e: e.tensor_tensor(srow[0:1, SB:2 * SB], srow[0:1, 2 * SB:3 * SB],
                                                 srow[0:1, SB:2 * SB], ALU.add))
                P("s", lambda e: e.activation(srow[0:1, 2 * SB:3 * SB], srow[0:1, SB:2 * SB], AF.Sqrt),
                  waits=[(v_p, v_p.n)])
                P("v", lambda e: e.reciprocal(srow[0:1, SB:2 * SB], srow[0:1, 2 * SB:3 * SB]),
                  waits=[(s_p, s_p.n)])
                P("v", lambda e: e.tensor_copy(srow_bf[0:1, 0:2 * SB], srow[0:1, 0:2 * SB]))
                P("v", lambda e: e.tensor_tensor(srow[0:1, 2 * SB:4 * SB], srow[0:1, 0:2 * SB],
                                                 srow_bf[0:1, 0:2 * SB], ALU.subtract))
                P("v", lambda e: e.tensor_copy(srow_bf[0:1, 2 * SB:4 * SB], srow[0:1, 2 * SB:4 * SB]))
                rows_mark = v_p.n

                def bcast_run(eng):
                    eng.matmul(p_bc[:, 0:SB], onr[0:1, :], srow_bf[0:1, 0:SB], start=True, stop=False)
                    eng.matmul(p_bc[:, 0:SB], onr[0:1, :], srow_bf[0:1, 2 * SB:3 * SB], start=False, stop=True)
                    eng.matmul(p_bc[:, SB:2 * SB], onr[0:1, :], srow_bf[0:1, SB:2 * SB], start=True, stop=False)
                    return eng.matmul(p_bc[:, SB:2 * SB], onr[0:1, :], srow_bf[0:1, 3 * SB:4 * SB], start=False, stop=True)
                P("t", bcast_run, waits=[(v_p, rows_mark)], inc=t_p)
                bc_mark = t_p.n

                P("v", lambda e: e.tensor_copy(bc[:, :], p_bc[:, :]),
                  waits=[(t_p, bc_mark)], inc=v_p)
                bc_ev = v_p.n

                for dc in range(NEC):
                    P("v", lambda e, dc=dc: e.tensor_tensor(
                        r2[:, dc * SB:(dc + 1) * SB], r2[:, dc * SB:(dc + 1) * SB],
                        bc[:, 0:SB], ALU.subtract))
                    P("v", lambda e, dc=dc: e.tensor_tensor(
                        r2[:, dc * SB:(dc + 1) * SB], r2[:, dc * SB:(dc + 1) * SB],
                        bc[:, SB:2 * SB], ALU.mult))
                norm_mark = v_p.n

                P("g", lambda e, sb=sb: e.dma_start(
                    out=out_fm.rearrange("(j p) s -> p j s", p=128)[:, :, sb * SB:(sb + 1) * SB],
                    in_=r2[:, :].rearrange("p (j s) -> p j s", s=SB)),
                  waits=[(v_p, norm_mark)], inc=d_e[1], inc_n=16)
                out_dma[sb] = d_e[1].n

        pending_ln2 = []

        yf_marks = {}
        wgu_marks = {}

        def emit_yf_loads(sb):
            sp3 = sb % 2
            w = []
            if sb >= 2:
                w.append((t_p, gu_done[(sb - 2, NHC - 1, "u")]))
                w.append((v_p, r2_done[(sb - 2, NEC - 1)]))
            for scr, dst in ((yfh_scr, yfh_s[sp3]), (yfl_scr, yfl_s[sp3])):
                P("y", lambda e, scr=scr, dst=dst, sb=sb: e.dma_start(
                    out=dst[:, :].rearrange("p (j s) -> p j s", s=SB),
                    in_=scr.rearrange("j p s -> p j s")[:, :, sb * SB:(sb + 1) * SB]),
                  waits=w, inc=d_x[sp3], inc_n=16)
                w = []
            yf_marks[sb] = d_x[sp3].n

        def emit_wgu_load(sb, hc):
            par = hc % 2
            w = []
            if hc >= 2:
                w.append((t_p, gu_done[(sb, hc - 2, "u")]))
            elif sb >= 1:
                w.append((t_p, gu_done[(sb - 1, NHC - 2 + hc, "u")]))
            for scr, buf in ((wgT_bf, wgt_s[par]), (wuT_bf, wut_s[par])):
                P("y", lambda e, scr=scr, b=buf, hc=hc: e.dma_start(
                    out=b[:, :].rearrange("p (j m) -> p j m", m=128),
                    in_=scr.rearrange("(j p) h -> p j h", p=128)[:, :, hc * 128:(hc + 1) * 128]),
                  waits=w, inc=d_w[par], inc_n=16)
                w = []
            wgu_marks[(sb, hc)] = d_w[par].n

        for sb in range(NSB):
            sp3 = sb % 2
            if sb not in yf_marks:
                emit_yf_loads(sb)

            for hc in range(NHC):
                par = hc % 2
                if (sb, hc) not in wgu_marks:
                    emit_wgu_load(sb, hc)

                for proj, wbuf, ps in (("g", wgt_s, p_g), ("u", wut_s, p_u)):
                    waits = [(d_w[par], wgu_marks[(sb, hc)]), (d_x[sp3], yf_marks[sb])]
                    if hc >= 2:
                        ctr, key = ((s_p, "silu") if proj == "g" else (v_p, "hhf"))
                        mark = silu_done[(sb, hc - 2)] if proj == "g" else hhf_done[(sb, hc - 2)]
                        waits.append((ctr, mark))

                    def gu_run(eng, wbuf=wbuf[par], ps=ps[par],
                               yfh=yfh_s[sp3], yfl=yfl_s[sp3]):
                        first = True
                        ins = None
                        for j in range(NEC):
                            for yb in (yfh, yfl):
                                ins = eng.matmul(ps[:, :], wbuf[:, j * 128:(j + 1) * 128],
                                                 yb[:, j * SB:(j + 1) * SB],
                                                 start=first, stop=(j == NEC - 1 and yb is yfl))
                                first = False
                        return ins
                    P("t", gu_run, waits=waits, inc=t_p)
                    gu_done[(sb, hc, proj)] = t_p.n

                P("s", lambda e, ps=p_g[par], o=gsil[par], hc=hc: e.activation(
                    o[:, :], ps[:, :], AF.Silu, scale=sg_t[:, hc:hc + 1]),
                  waits=[(t_p, gu_done[(sb, hc, "g")])], inc=s_p)
                silu_done[(sb, hc)] = s_p.n
                P("v", lambda e, ps=p_u[par], g=gsil[par], o=hhf[par], hc=hc: e.scalar_tensor_tensor(
                    o[:, :], ps[:, :], su_t[:, hc:hc + 1], g[:, :], ALU.mult, ALU.mult),
                  waits=[(t_p, gu_done[(sb, hc, "u")]), (s_p, silu_done[(sb, hc)])],
                  inc=v_p)
                hhf_done[(sb, hc)] = v_p.n
                w_extra = []
                if sb >= 1 and hc == 0:
                    w_extra = [(t_p, down_done[(sb - 1, 15)])]
                P("s", lambda e, src=hhf[par], hc=hc: e.activation(
                    hh_h[:, hc * SB:(hc + 1) * SB], src[:, :], AF.Copy),
                  waits=[(v_p, hhf_done[(sb, hc)])] + w_extra, inc=s_p)
                hhh_done[(sb, hc)] = s_p.n
                P("v", lambda e, src=hhf[par], hc=hc: e.tensor_tensor(
                    hh_l[:, hc * SB:(hc + 1) * SB], src[:, :],
                    hh_h[:, hc * SB:(hc + 1) * SB], ALU.subtract),
                  waits=[(s_p, hhh_done[(sb, hc)])], inc=v_p)
                hhl_done[(sb, hc)] = v_p.n
                if hc == 1 and pending_ln2:
                    emit_ln2(pending_ln2.pop(0))

            if sb + 1 < NSB:
                emit_yf_loads(sb + 1)
                emit_wgu_load(sb + 1, 0)
                emit_wgu_load(sb + 1, 1)
            for dc in range(NEC):
                par = dc % 2
                w = []
                if dc >= 2:
                    w.append((t_p, down_done[(sb, dc - 2)]))
                elif sb >= 1:
                    w.append((t_p, down_done[(sb - 1, dc + NEC - 2)]))
                dch = d_st[par]
                P("y", lambda e, b=wdt_s[par], dc=dc: e.dma_start(
                    out=b[:, :].rearrange("p (j m) -> p j m", m=128),
                    in_=wdT_bf.rearrange("(j p) d -> p j d", p=128)[:, :, dc * 128:(dc + 1) * 128]),
                  waits=w, inc=dch, inc_n=16)
                wdt_mark[par] = dch.n

                waits = [(dch, wdt_mark[par]),
                         (v_p, hhl_done[(sb, NHC - 1)])]
                if dc >= 2:
                    waits.append((v_p, r2_done[(sb, dc - 2)]))

                def down_run(eng, b=wdt_s[par], ps=p_f[par]):
                    first = True
                    ins = None
                    for j in range(NHC):
                        for hb in (hh_h, hh_l):
                            ins = eng.matmul(ps[:, :], b[:, j * 128:(j + 1) * 128],
                                             hb[:, j * SB:(j + 1) * SB],
                                             start=first, stop=(j == NHC - 1 and hb is hh_l))
                            first = False
                    return ins
                P("t", down_run, waits=waits, inc=t_p)
                down_done[(sb, dc)] = t_p.n

                w = [(t_p, down_done[(sb, dc)])]
                if sb >= 1 and dc == 0:
                    w.append((d_e[1], out_dma[sb - 1]))
                P("v", lambda e, ps=p_f[par], dc=dc, yy=yfh_s[sp3]: e.scalar_tensor_tensor(
                    r2[:, dc * SB:(dc + 1) * SB], ps[:, :], sd_t[:, dc:dc + 1],
                    yy[:, dc * SB:(dc + 1) * SB], ALU.mult, ALU.add),
                  waits=w, inc=v_p)
                P("v", lambda e, dc=dc, yy=yfl_s[sp3]: e.tensor_tensor(
                    r2[:, dc * SB:(dc + 1) * SB], r2[:, dc * SB:(dc + 1) * SB],
                    yy[:, dc * SB:(dc + 1) * SB], ALU.add),
                  inc=v_p)
                r2_done[(sb, dc)] = v_p.n

            pending_ln2.append(sb)


        while pending_ln2:
            emit_ln2(pending_ln2.pop(0))

        p3_stack.close()
        p3_psum.close()

        final_out = d_e[1].n
        prog.streams["g"].append(lambda e, v=final_out: e.wait_ge(d_e[1].h, v))

        block = g_stack.enter_context(nc.Block())

        @block.tensor
        def _(eng):
            for f in prog.streams["t"]:
                f(eng)

        @block.vector
        def _(eng):
            for f in prog.streams["v"]:
                f(eng)

        @block.scalar
        def _(eng):
            for f in prog.streams["s"]:
                f(eng)

        @block.sync
        def _(eng):
            for f in prog.streams["y"]:
                f(eng)

        @block.gpsimd
        def _(eng):
            for f in prog.streams["g"]:
                f(eng)

    return nc


_NC_CACHE = None


def kernel(L0_attn_x, L0_attn_Wv, L0_attn_Wg, L0_attn_bg, L0_attn_Wq, L0_attn_bq,
           L0_attn_Wo, L0_ffn_wq_gate, L0_ffn_s_gate, L0_ffn_wq_up, L0_ffn_s_up,
           L0_ffn_wq_down, L0_ffn_s_down):
    global _NC_CACHE
    x = np.asarray(L0_attn_x, dtype=np.float32)
    B = x.shape[0]
    assert x.shape == (8, S, E)

    shared = dict(
        wg=np.ascontiguousarray(np.asarray(L0_attn_Wg, np.float32)),
        wv=np.ascontiguousarray(np.asarray(L0_attn_Wv, np.float32)),
        wq=np.ascontiguousarray(np.asarray(L0_attn_Wq, np.float32)),
        wo=np.ascontiguousarray(np.asarray(L0_attn_Wo, np.float32)),
        bg2d=np.ascontiguousarray(np.asarray(L0_attn_bg, np.float32).reshape(NEC, 128).T),
        bq2d=np.ascontiguousarray(np.asarray(L0_attn_bq, np.float32).reshape(NEC, 128).T),
        wgT_i=np.ascontiguousarray(np.asarray(L0_ffn_wq_gate, np.int32).T),
        wuT_i=np.ascontiguousarray(np.asarray(L0_ffn_wq_up, np.int32).T),
        wdT_i=np.ascontiguousarray(np.asarray(L0_ffn_wq_down, np.int32).T),
        sg2d=np.ascontiguousarray(np.asarray(L0_ffn_s_gate, np.float32).reshape(NHC, 128).T),
        su2d=np.ascontiguousarray(np.asarray(L0_ffn_s_up, np.float32).reshape(NHC, 128).T),
        sd2d=np.ascontiguousarray(np.asarray(L0_ffn_s_down, np.float32).reshape(NEC, 128).T),
        dec_i=np.full((128, SGRP), np.float32(DECAY), dtype=np.float32),
        ident_i=np.eye(128, dtype=np.float32),
        onc_i=np.ones((128, 1), dtype=ml_dtypes.bfloat16),
        onr_i=np.ones((1, 128), dtype=ml_dtypes.bfloat16),
    )
    in_maps = []
    for b in range(B):
        m = dict(shared)
        m["x_tm"] = np.ascontiguousarray(x[b])
        m["xT"] = np.ascontiguousarray(x[b].T)
        in_maps.append(m)

    if _NC_CACHE is None:
        _NC_CACHE = build_nc()
    res = run_bass_kernel_spmd(_NC_CACHE, in_maps, core_ids=list(range(8)))
    out = np.stack([np.asarray(r["out_fm"]).T for r in res.results], axis=0)
    return out


# version 11
# speedup vs baseline: 1.0444x; 1.0444x over previous
import numpy as np
import ml_dtypes
import concourse.bass as bass
import concourse.mybir as mybir
from concourse.bass_utils import run_bass_kernel_spmd
from contextlib import ExitStack

F32 = mybir.dt.float32
BF16 = mybir.dt.bfloat16
I32 = mybir.dt.int32
AF = mybir.ActivationFunctionType
ALU = mybir.AluOpType

import os
ABL_NO_PW = os.environ.get("ABL_NO_PW") == "1"
ABL_NO_LHLL = os.environ.get("ABL_NO_LHLL") == "1"
ABL_NO_SPREAD = os.environ.get("ABL_NO_SPREAD") == "1"

E = 2048
S = 2048
H = 5632
NEC = E // 128
NHC = H // 128
NSC = S // 128
SGRP = 512
NSG = S // SGRP
SB = 256
NSB = S // SB
EPS = 1e-5
DECAY = 0.95
SCALE2 = float(np.float32(np.sqrt(1.0 - DECAY))) ** 2


class Ctr:
    def __init__(self, handle):
        self.h = handle
        self.n = 0


class Prog:
    def __init__(self):
        self.streams = {k: [] for k in ("t", "v", "s", "y", "g", "g2")}
        self.auto = {}

    def emit(self, key, build, waits=(), inc=None, inc_n=1):
        waits = list(waits)
        if key in self.auto:
            ctr = self.auto[key]
            waits = [(ctr, ctr.n)] + waits
            if inc is None:
                inc = ctr
        if key == "g2":
            key = "g"
        w = [(c.h, v) for (c, v) in waits if v > 0]
        tgt = None
        if inc is not None:
            inc.n += inc_n
            tgt = (inc.h, inc_n)

        def run(eng, w=w, build=build, tgt=tgt):
            for h, v in w:
                eng.wait_ge(h, v)
            ins = build(eng)
            if tgt is not None:
                if ins is None:
                    ins = eng.nop()
                ins.then_inc(*tgt)
        self.streams[key].append(run)


def build_nc():
    nc = bass.Bass("TRN2", target_bir_lowering=False)

    xT = nc.dram_tensor("xT", [E, S], F32, kind="ExternalInput").ap()
    x_tm = nc.dram_tensor("x_tm", [S, E], F32, kind="ExternalInput").ap()
    wg = nc.dram_tensor("wg", [E, E], F32, kind="ExternalInput").ap()
    wv = nc.dram_tensor("wv", [E, E], F32, kind="ExternalInput").ap()
    wq = nc.dram_tensor("wq", [E, E], F32, kind="ExternalInput").ap()
    wo = nc.dram_tensor("wo", [E, E], F32, kind="ExternalInput").ap()
    bg2d = nc.dram_tensor("bg2d", [128, NEC], F32, kind="ExternalInput").ap()
    bq2d = nc.dram_tensor("bq2d", [128, NEC], F32, kind="ExternalInput").ap()
    wgT_i = nc.dram_tensor("wgT_i", [E, H], I32, kind="ExternalInput").ap()
    wuT_i = nc.dram_tensor("wuT_i", [E, H], I32, kind="ExternalInput").ap()
    wdT_i = nc.dram_tensor("wdT_i", [H, E], I32, kind="ExternalInput").ap()
    sg2d = nc.dram_tensor("sg2d", [128, NHC], F32, kind="ExternalInput").ap()
    su2d = nc.dram_tensor("su2d", [128, NHC], F32, kind="ExternalInput").ap()
    sd2d = nc.dram_tensor("sd2d", [128, NEC], F32, kind="ExternalInput").ap()
    dec_i = nc.dram_tensor("dec_i", [128, SGRP], F32, kind="ExternalInput").ap()
    ident_i = nc.dram_tensor("ident_i", [128, 128], F32, kind="ExternalInput").ap()
    onc_i = nc.dram_tensor("onc_i", [128, 1], BF16, kind="ExternalInput").ap()
    onr_i = nc.dram_tensor("onr_i", [1, 128], BF16, kind="ExternalInput").ap()

    out_fm = nc.dram_tensor("out_fm", [E, S], F32, kind="ExternalOutput").ap()
    load_hl_scr = nc.dram_tensor("load_hl_scr", [NSC, 2, 128, E], BF16).ap()
    wg_hl_scr = nc.dram_tensor("wg_hl_scr", [NEC, 2, 128, E], BF16).ap()
    wv_hl_scr = nc.dram_tensor("wv_hl_scr", [NEC, 2, 128, E], BF16).ap()
    wq_hl_scr = nc.dram_tensor("wq_hl_scr", [NEC, 2, 128, E], BF16).ap()
    wo_hl_scr = nc.dram_tensor("wo_hl_scr", [NEC, 2, 128, E], BF16).ap()
    yfh_scr = nc.dram_tensor("yfh_scr", [NEC, 128, S], BF16).ap()
    yfl_scr = nc.dram_tensor("yfl_scr", [NEC, 128, S], BF16).ap()
    wgT_sw = nc.dram_tensor("wgT_sw", [NHC, 128, E], BF16).ap()
    wuT_sw = nc.dram_tensor("wuT_sw", [NHC, 128, E], BF16).ap()
    wdT_sw = nc.dram_tensor("wdT_sw", [NEC, 128, NHC * 128], BF16).ap()

    prog = Prog()
    P = prog.emit

    with ExitStack() as g_stack:
        ge = g_stack.enter_context
        d_f = Ctr(ge(nc.semaphore("d_f")))
        d_w = [Ctr(ge(nc.semaphore(f"d_w{i}"))) for i in range(2)]
        d_x = [Ctr(ge(nc.semaphore(f"d_x{i}"))) for i in range(2)]
        d_c = Ctr(ge(nc.semaphore("d_c")))
        d_d = Ctr(ge(nc.semaphore("d_d")))
        d_d2 = Ctr(ge(nc.semaphore("d_d2")))
        d_e = [Ctr(ge(nc.semaphore(f"d_e{i}"))) for i in range(2)]
        d_as = [Ctr(ge(nc.semaphore(f"d_as{i}"))) for i in range(2)]
        d_st = [Ctr(ge(nc.semaphore(f"d_st{i}"))) for i in range(3)]
        d_pw = [Ctr(ge(nc.semaphore(f"d_pw{i}"))) for i in range(4)]
        d_wo = [Ctr(ge(nc.semaphore(f"d_wo{i}"))) for i in range(2)]
        t_p = Ctr(ge(nc.semaphore("t_p")))
        v_p = Ctr(ge(nc.semaphore("v_p")))
        s_p = Ctr(ge(nc.semaphore("s_p")))
        g_p = Ctr(ge(nc.semaphore("g_p")))
        prog.auto = {"v": v_p, "s": s_p, "g2": g_p}
        ALL_DMA = [d_f, d_w[0], d_w[1], d_x[0], d_x[1], d_c, d_d, d_d2, d_e[0], d_e[1],
                   d_as[0], d_as[1], d_st[0], d_st[1], d_st[2],
                   d_pw[0], d_pw[1], d_pw[2], d_pw[3], d_wo[0], d_wo[1]]

        def barrier():
            for key, ctr in (("t", t_p), ("v", v_p), ("s", s_p)):
                P(key, lambda e: e.nop(), waits=[(ctr, ctr.n)], inc=ctr)
            marks = [(t_p, t_p.n), (v_p, v_p.n), (s_p, s_p.n)] + \
                    [(c, c.n) for c in ALL_DMA]
            for key in ("t", "v", "s", "y", "g"):
                def run(eng, marks=tuple(marks)):
                    for c, v in marks:
                        if v > 0:
                            eng.wait_ge(c.h, v)
                prog.streams[key].append(run)

        bg_t = ge(nc.sbuf_tensor("bg_t", [128, NEC], F32))
        bq_t = ge(nc.sbuf_tensor("bq_t", [128, NEC], F32))
        sg_t = ge(nc.sbuf_tensor("sg_t", [128, NHC], F32))
        su_t = ge(nc.sbuf_tensor("su_t", [128, NHC], F32))
        sd_t = ge(nc.sbuf_tensor("sd_t", [128, NEC], F32))
        dec_t = ge(nc.sbuf_tensor("dec_t", [128, SGRP], F32))
        ident = ge(nc.sbuf_tensor("ident", [128, 128], F32))
        onc = ge(nc.sbuf_tensor("onc", [128, 1], BF16))
        onr = ge(nc.sbuf_tensor("onr", [1, 128], BF16))
        carry = ge(nc.sbuf_tensor("carry", [128, NEC], F32))
        stats = ge(nc.sbuf_tensor("stats", [128, 16], F32))
        srow = ge(nc.sbuf_tensor("srow", [1, 4 * SB], F32))
        srow_bf = ge(nc.sbuf_tensor("srow_bf", [1, 4 * SB], BF16))

        for dst, src in ((bg_t, bg2d), (bq_t, bq2d), (sg_t, sg2d), (su_t, su2d),
                         (sd_t, sd2d), (dec_t, dec_i), (ident, ident_i),
                         (onc, onc_i), (onr, onr_i)):
            P("y", lambda e, d=dst, s=src: e.dma_start(out=d[:, :], in_=s),
              inc=d_f, inc_n=16)
        glob_ld = d_f.n

        p1_stack = ExitStack()
        xgh = [p1_stack.enter_context(nc.sbuf_tensor(f"xgh{i}", [128, NEC * SGRP], BF16)) for i in range(2)]
        xgl = [p1_stack.enter_context(nc.sbuf_tensor(f"xgl{i}", [128, NEC * SGRP], BF16)) for i in range(2)]
        xstg = [p1_stack.enter_context(nc.sbuf_tensor(f"xstg{i}", [128, SGRP], F32)) for i in range(2)]
        wgs = [p1_stack.enter_context(nc.sbuf_tensor(f"wgs{i}", [128, 2 * E], BF16)) for i in range(2)]
        wvs = [p1_stack.enter_context(nc.sbuf_tensor(f"wvs{i}", [128, 2 * E], BF16)) for i in range(2)]
        wqs = [p1_stack.enter_context(nc.sbuf_tensor(f"wqs{i}", [128, 2 * E], BF16)) for i in range(2)]
        wtmp = [p1_stack.enter_context(nc.sbuf_tensor(f"wtmp{i}", [128, E], F32)) for i in range(3)]
        gate_sb = [p1_stack.enter_context(nc.sbuf_tensor(f"gate{i}", [128, SGRP], F32)) for i in range(2)]
        que_sb = [p1_stack.enter_context(nc.sbuf_tensor(f"que{i}", [128, SGRP], F32)) for i in range(2)]
        store_sb = [p1_stack.enter_context(nc.sbuf_tensor(f"store{i}", [128, SGRP], F32)) for i in range(2)]
        mem_sb = [p1_stack.enter_context(nc.sbuf_tensor(f"mem{i}", [128, SGRP], F32)) for i in range(2)]
        load_sb = [p1_stack.enter_context(nc.sbuf_tensor(f"load{i}", [128, SGRP], F32)) for i in range(2)]
        lh_sb = [p1_stack.enter_context(nc.sbuf_tensor(f"lh{i}", [128, SGRP], BF16)) for i in range(2)]
        ll_sb = [p1_stack.enter_context(nc.sbuf_tensor(f"ll{i}", [128, SGRP], BF16)) for i in range(2)]
        PWW = 1408
        wint = [p1_stack.enter_context(nc.sbuf_tensor(f"wint{i}", [128, PWW], I32)) for i in range(3)]
        wbf = [p1_stack.enter_context(nc.sbuf_tensor(f"wbf{i}", [128, PWW], BF16)) for i in range(2)]
        wof = p1_stack.enter_context(nc.sbuf_tensor("wof", [128, 1024], F32))
        woh_sl = p1_stack.enter_context(nc.sbuf_tensor("woh_sl", [128, 1024], BF16))
        wol_sl = p1_stack.enter_context(nc.sbuf_tensor("wol_sl", [128, 1024], BF16))

        p1_psum = ExitStack()
        p_gate = [p1_psum.enter_context(nc.psum_tensor(f"pg{i}", [128, SGRP], F32)) for i in range(2)]
        p_v = [p1_psum.enter_context(nc.psum_tensor(f"pv{i}", [128, SGRP], F32)) for i in range(2)]
        p_que = [p1_psum.enter_context(nc.psum_tensor(f"pq{i}", [128, SGRP], F32)) for i in range(2)]

        pw_items = []
        WG_W = 1408
        for k in range(NEC):
            for h2 in range(4):
                dst = wgT_sw[h2 * 11:(h2 + 1) * 11, :, k * 128:(k + 1) * 128] \
                    .rearrange("hc p m -> p hc m")
                pw_items.append(("ffn", wgT_i[k * 128:(k + 1) * 128, h2 * WG_W:(h2 + 1) * WG_W],
                                 dst, WG_W))
        for k in range(NEC):
            for h2 in range(4):
                dst = wuT_sw[h2 * 11:(h2 + 1) * 11, :, k * 128:(k + 1) * 128] \
                    .rearrange("hc p m -> p hc m")
                pw_items.append(("ffn", wuT_i[k * 128:(k + 1) * 128, h2 * WG_W:(h2 + 1) * WG_W],
                                 dst, WG_W))
        WD_W = 1024
        for k in range(NHC):
            for h2 in range(2):
                dst = wdT_sw[h2 * 8:(h2 + 1) * 8, :, k * 128:(k + 1) * 128] \
                    .rearrange("dc p m -> p dc m")
                pw_items.append(("ffn", wdT_i[k * 128:(k + 1) * 128, h2 * WD_W:(h2 + 1) * WD_W],
                                 dst, WD_W))
        wo_items = [("wo", wo[k * 128:(k + 1) * 128, h2 * 1024:(h2 + 1) * 1024],
                     (wo_hl_scr[k, 0, :, h2 * 1024:(h2 + 1) * 1024],
                      wo_hl_scr[k, 1, :, h2 * 1024:(h2 + 1) * 1024]), 1024)
                    for k in range(NEC) for h2 in range(2)]
        for k in range(2 * NEC):
            pw_items.insert(60 + k * 5, wo_items[k])
        if ABL_NO_PW:
            pw_items = []
        pw_state = {"emitted": 0, "in": [], "conv": [], "outs": []}

        def emit_pw_step():
            k = pw_state["emitted"]
            if k >= len(pw_items):
                return
            pw_state["emitted"] += 1
            kind, src_, dst, width = pw_items[k]
            if kind == "ffn":
                fk = pw_state.setdefault("fk", 0)
                pw_state["fk"] = fk + 1
                p4 = fk % 3
                p2 = fk % 2
                ch_in = d_pw[p4]
                w_in = []
                cm = pw_state.get(("fconv", p4))
                if cm is not None:
                    w_in.append((s_p, cm))
                P("y", lambda e, s=src_, wd=width, b=wint[p4]: e.dma_start(
                    out=b[:, :wd], in_=s), waits=w_in, inc=ch_in, inc_n=16)
                in_mark = ch_in.n
                w_cv = [(ch_in, in_mark)]
                om = pw_state.get(("fout", p2))
                if om is not None:
                    w_cv.append(((d_d if p2 == 0 else d_d2), om))
                P("s", lambda e, wd=width, b=wint[p4], o=wbf[p2]: e.activation(
                    o[:, :wd], b[:, :wd], AF.Copy, bias=-128.0),
                  waits=w_cv, inc=s_p)
                pw_state[("fconv", p4)] = s_p.n
                ch_out = d_d if p2 == 0 else d_d2
                P("g", lambda e, d=dst, wd=width, o=wbf[p2]: e.dma_start(
                    out=d, in_=o[:, :wd].rearrange("p (c m) -> p c m", m=128)),
                  waits=[(s_p, s_p.n)], inc=ch_out, inc_n=16)
                pw_state[("fout", p2)] = ch_out.n
            else:
                w_in = []
                if pw_state.get("wo_sub") is not None:
                    w_in.append((v_p, pw_state["wo_sub"]))
                P("g", lambda e, s=src_: e.dma_start(out=wof[:, :], in_=s),
                  waits=w_in, inc=d_wo[0], inc_n=16)
                in_mark = d_wo[0].n
                w_cv = [(d_wo[0], in_mark)]
                if pw_state.get("wo_out") is not None:
                    w_cv.append((d_wo[1], pw_state["wo_out"]))
                P("s", lambda e: e.activation(woh_sl[:, :], wof[:, :], AF.Copy),
                  waits=w_cv, inc=s_p)
                hm = s_p.n
                P("v", lambda e: e.tensor_tensor(wol_sl[:, :], wof[:, :], woh_sl[:, :], ALU.subtract),
                  waits=[(s_p, hm)])
                lm = v_p.n
                pw_state["wo_sub"] = lm
                P("g", lambda e, d=dst[0]: e.dma_start(out=d, in_=woh_sl[:, :]),
                  waits=[(s_p, hm)], inc=d_wo[1], inc_n=16)
                P("g", lambda e, d=dst[1]: e.dma_start(out=d, in_=wol_sl[:, :]),
                  waits=[(v_p, lm)], inc=d_wo[1], inc_n=16)
                pw_state["wo_out"] = d_wo[1].n

        def w_slice_view(w, ms):
            return w.rearrange("(j p) m -> p j m", p=128)[:, :, ms * 128:(ms + 1) * 128]

        xg_mark = [0, 0]
        xs_mark = {}
        w_mark = [0, 0]
        w_split = {}
        proj_done = {}
        evac_done = {}
        store_done = {}
        N_ITER = NSG * NEC
        AS_FINAL = 6 * (NEC // 2) * 16

        def emit_xsplit_dma(sg, j):
            par = j % 2
            key = (sg, j - 2) if j >= 2 else ((sg - 1, 14 + j) if sg >= 1 else None)
            w = [(v_p, sub_marks[key])] if key in sub_marks else []
            P("y", lambda e, j=j, sg=sg, stg=xstg[par]: e.dma_start(
                out=stg[:, :],
                in_=xT[j * 128:(j + 1) * 128, sg * SGRP:(sg + 1) * SGRP]),
              waits=w, inc=d_x[par], inc_n=16)
            xdma_marks[(sg, j)] = d_x[par].n

        def emit_xsplit_ops(sg, j):
            par = j % 2
            b = sg % 2
            guard = []
            if sg >= 2 and j == 0:
                guard = [(t_p, proj_done[((sg - 2) * NEC + NEC - 1, "q")])]
            dm = xdma_marks[(sg, j)]
            P("s", lambda e, j=j, stg=xstg[par], o=xgh[b]: e.activation(
                o[:, j * SGRP:(j + 1) * SGRP], stg[:, :], AF.Copy),
              waits=[(d_x[par], dm)] + guard)
            cast_m = s_p.n
            P("v", lambda e, j=j, stg=xstg[par], h=xgh[b], o=xgl[b]: e.tensor_tensor(
                o[:, j * SGRP:(j + 1) * SGRP], stg[:, :],
                h[:, j * SGRP:(j + 1) * SGRP], ALU.subtract),
              waits=[(s_p, cast_m)])
            sub_marks[(sg, j)] = v_p.n
            if j == NEC - 1:
                xs_mark[sg] = (cast_m, v_p.n)

        def emit_xsplit_chunk(sg, j):
            emit_xsplit_dma(sg, j)
            emit_xsplit_ops(sg, j)
        xdma_marks = {}

        sub_marks = {}
        wtmp_state = {}
        as_mark = [0, 0]

        def emit_p1_dmas(i):
            sg, ms = divmod(i, NEC)
            par = i % 2
            if sg == 0:
                marks = []
                for k, wt in enumerate((wg, wv, wq)):
                    wi = wtmp_state.get("idx", 0)
                    wtmp_state["idx"] = wi + 1
                    tb = wtmp[wi % 3]
                    w = []
                    prev = wtmp_state.get(("sub", wi % 3))
                    if prev is not None:
                        w.append((v_p, prev))
                    if i >= 2 and k == 0:
                        w.append((t_p, proj_done[(i - 2, "q")]))
                        w.append((d_as[par], as_mark[par]))
                    ch = d_st[wi % 3]
                    P("y", lambda e, wt=wt, ms=ms, tb=tb: e.dma_start(
                        out=tb[:, :].rearrange("p (j m) -> p j m", m=128),
                        in_=w_slice_view(wt, ms)), waits=w, inc=ch, inc_n=16)
                    marks.append((wi, ch.n))
                wtmp_state[("dmas", i)] = marks
            else:
                w = [(d_as[0], AS_FINAL), (d_as[1], AS_FINAL)] if sg == 1 else []
                if i >= 2:
                    w.append((t_p, proj_done[(i - 2, "q")]))
                for scr, buf in ((wg_hl_scr, wgs[par]), (wv_hl_scr, wvs[par]),
                                 (wq_hl_scr, wqs[par])):
                    P("y", lambda e, scr=scr, ms=ms, b=buf: e.dma_start(
                        out=b[:, :].rearrange("p (hl m) -> p hl m", m=E),
                        in_=scr[ms].rearrange("hl p m -> p hl m")),
                      waits=w, inc=d_w[par], inc_n=16)
                    w = []
            w_mark[par] = d_w[par].n

        def emit_p1_splits(i):
            sg, ms = divmod(i, NEC)
            par = i % 2
            if sg != 0:
                return
            marks = wtmp_state.pop(("dmas", i))
            for (wi, dm), (proj, buf, scr) in zip(marks, (
                    ("g", wgs[par], wg_hl_scr), ("v", wvs[par], wv_hl_scr),
                    ("q", wqs[par], wq_hl_scr))):
                ch = d_st[wi % 3]
                tb = wtmp[wi % 3]
                P("s", lambda e, buf=buf, tb=tb: e.activation(buf[:, :E], tb[:, :], AF.Copy),
                  waits=[(ch, dm)])
                hm = s_p.n
                P("v", lambda e, buf=buf, tb=tb: e.tensor_tensor(
                    buf[:, E:2 * E], tb[:, :], buf[:, :E], ALU.subtract),
                  waits=[(s_p, hm)])
                lm = v_p.n
                wtmp_state[("sub", wi % 3)] = lm
                w_split[(i, proj)] = (hm, lm)
                P("g", lambda e, scr=scr, ms=ms, buf=buf: e.dma_start(
                    out=scr[ms, 0], in_=buf[:, :E]),
                  waits=[(s_p, hm)], inc=d_as[par], inc_n=16)
                P("g", lambda e, scr=scr, ms=ms, buf=buf: e.dma_start(
                    out=scr[ms, 1], in_=buf[:, E:2 * E]),
                  waits=[(v_p, lm)], inc=d_as[par], inc_n=16)
            as_mark[par] = d_as[par].n

        for i in range(N_ITER):
            sg, ms = divmod(i, NEC)
            par = i % 2
            if i == 0:
                for j in range(NEC):
                    emit_xsplit_chunk(0, j)
                emit_p1_dmas(0)
                emit_p1_splits(0)
            if sg + 1 < NSG and 6 <= ms < 14:
                for j in range(2 * (ms - 6), 2 * (ms - 6) + 2):
                    emit_xsplit_dma(sg + 1, j)
            if i + 1 < N_ITER:
                emit_p1_dmas(i + 1)
                if (i + 1) < NEC:
                    emit_p1_splits(i + 1)

            for proj, wb, ps in (("g", wgs, p_gate), ("v", wvs, p_v), ("q", wqs, p_que)):
                waits = [(d_f, glob_ld),
                         (s_p, xs_mark[sg][0]), (v_p, xs_mark[sg][1])]
                if sg == 0:
                    hm, lm = w_split[(i, proj)]
                    waits += [(s_p, hm), (v_p, lm)]
                else:
                    waits.append((d_w[par], w_mark[par]))
                if i >= 2:
                    kind = {"g": "gate_evac", "v": "store_mult", "q": "que_evac"}[proj]
                    ctr = s_p if proj in ("g", "q") else v_p
                    waits.append((ctr, evac_done[(i - 2, kind)]))

                def mm_run(eng, wb=wb[par], ps=ps[par],
                           xh=xgh[sg % 2], xl=xgl[sg % 2]):
                    ins = None
                    for j in range(NEC):
                        ins = eng.matmul(ps[:, :], wb[:, j * 128:(j + 1) * 128],
                                         xh[:, j * SGRP:(j + 1) * SGRP],
                                         start=(j == 0), stop=False)
                        ins = eng.matmul(ps[:, :], wb[:, E + j * 128:E + (j + 1) * 128],
                                         xh[:, j * SGRP:(j + 1) * SGRP],
                                         start=False, stop=False)
                        ins = eng.matmul(ps[:, :], wb[:, j * 128:(j + 1) * 128],
                                         xl[:, j * SGRP:(j + 1) * SGRP],
                                         start=False, stop=(j == NEC - 1))
                    return ins
                P("t", mm_run, waits=waits, inc=t_p)
                proj_done[(i, proj)] = t_p.n

            gw = [(t_p, proj_done[(i, "g")])]
            if i >= 2:
                gw.append((v_p, evac_done[(i - 2, "store_mult")]))
            P("s", lambda e, ps=p_gate[par], o=gate_sb[par], ms=ms: e.activation(
                o[:, :], ps[:, :], AF.Sigmoid, bias=bg_t[:, ms:ms + 1]),
              waits=gw, inc=s_p)
            evac_done[(i, "gate_evac")] = s_p.n
            qw = [(t_p, proj_done[(i, "q")])]
            if i >= 2:
                qw.append((v_p, evac_done[(i - 2, "load")]))
            P("s", lambda e, ps=p_que[par], o=que_sb[par], ms=ms: e.activation(
                o[:, :], ps[:, :], AF.Sigmoid, bias=bq_t[:, ms:ms + 1]),
              waits=qw, inc=s_p)
            evac_done[(i, "que_evac")] = s_p.n

            P("v", lambda e, ps=p_v[par], g=gate_sb[par], o=store_sb[par]: e.tensor_tensor(
                o[:, :], ps[:, :], g[:, :], ALU.mult),
              waits=[(t_p, proj_done[(i, "v")]), (s_p, evac_done[(i, "gate_evac")])],
              inc=v_p)
            evac_done[(i, "store_mult")] = v_p.n
            fence_pre = v_p.n
            if sg == 0:
                init = 0.0
            else:
                init = carry[:, ms:ms + 1]
            P("v", lambda e, st=store_sb[par], m=mem_sb[par], init=init: e.tensor_tensor_scan(
                m[:, :], dec_t[:, :], st[:, :], init, ALU.mult, ALU.add),
              waits=[(v_p, fence_pre)], inc=v_p)
            fence_post = v_p.n
            P("v", lambda e, m=mem_sb[par], ms=ms: e.tensor_copy(
                carry[:, ms:ms + 1], m[:, SGRP - 1:SGRP]),
              waits=[(v_p, fence_post)], inc=v_p)
            lw = [(v_p, fence_post), (s_p, evac_done[(i, "que_evac")])]
            if i >= 2 and evac_done.get((i - 2, "lh_cast"), 0):
                lw.append((g_p, evac_done[(i - 2, "lh_cast")]))
            P("v", lambda e, m=mem_sb[par], q=que_sb[par], o=load_sb[par]: e.scalar_tensor_tensor(
                o[:, :], m[:, :], 0.05, q[:, :], ALU.mult, ALU.mult),
              waits=lw, inc=v_p)
            load_ready = v_p.n
            evac_done[(i, "load")] = load_ready

            if not ABL_NO_LHLL:
              sw = [(v_p, load_ready)]
              if i >= 2:
                sw.append((d_e[par], store_done[i - 2]))
              P("g2", lambda e, o=load_sb[par], h=lh_sb[par]: e.tensor_copy(
                h[:, :], o[:, :]), waits=sw)
              evac_done[(i, "lh_cast")] = g_p.n
              P("g2", lambda e, o=load_sb[par], h=lh_sb[par], l=ll_sb[par]: e.tensor_tensor(
                l[:, :], o[:, :], h[:, :], ALU.subtract))
              ll_mark = g_p.n

              P("g", lambda e, o=lh_sb[par], ms=ms, sg=sg: e.dma_start(
                out=load_hl_scr[sg * 4:(sg + 1) * 4, 0, :, ms * 128:(ms + 1) * 128]
                    .rearrange("c p s -> p c s"),
                in_=o[:, :].rearrange("p (c s) -> p c s", s=128)),
                waits=[(g_p, evac_done[(i, "lh_cast")])], inc=d_e[par], inc_n=16)
              P("g", lambda e, o=ll_sb[par], ms=ms, sg=sg: e.dma_start(
                out=load_hl_scr[sg * 4:(sg + 1) * 4, 1, :, ms * 128:(ms + 1) * 128]
                    .rearrange("c p s -> p c s"),
                in_=o[:, :].rearrange("p (c s) -> p c s", s=128)),
                waits=[(g_p, ll_mark)], inc=d_e[par], inc_n=16)
              store_done[i] = d_e[par].n

            if sg + 1 < NSG and 6 <= ms < 14:
                for j in range(2 * (ms - 6), 2 * (ms - 6) + 2):
                    emit_xsplit_ops(sg + 1, j)
            if i + 1 >= NEC and i + 1 < N_ITER:
                emit_p1_splits(i + 1)

            while pw_state["emitted"] * N_ITER < len(pw_items) * (i + 1):
                emit_pw_step()

        while pw_state["emitted"] < len(pw_items):
            emit_pw_step()

        p1_stack.close()
        p1_psum.close()
        barrier()

        p2_stack = ExitStack()
        wo_s = p2_stack.enter_context(nc.sbuf_tensor("wo_s", [128, 2 * NEC * E], BF16))
        lc = [p2_stack.enter_context(nc.sbuf_tensor(f"lc{i}", [128, 2 * E], BF16)) for i in range(2)]
        xc = p2_stack.enter_context(nc.sbuf_tensor("xc", [128, E], F32))
        ry = p2_stack.enter_context(nc.sbuf_tensor("ry", [128, E], F32))
        yfh = p2_stack.enter_context(nc.sbuf_tensor("yfh", [128, E], BF16))
        yfl = p2_stack.enter_context(nc.sbuf_tensor("yfl", [128, E], BF16))
        garb = p2_stack.enter_context(nc.sbuf_tensor("garb", [128, E], BF16))
        p2_psum = ExitStack()
        p_attn = p2_psum.enter_context(nc.psum_tensor("p_attn", [128, E], F32))
        p_yT = p2_psum.enter_context(nc.psum_tensor("p_yT", [128, E], F32))

        WOL = NEC * E
        for hl in range(2):
            P("y", lambda e, hl=hl: e.dma_start(
                out=wo_s[:, hl * WOL:(hl + 1) * WOL].rearrange("p (j e) -> p j e", e=E),
                in_=wo_hl_scr[:, hl].rearrange("j p e -> p j e")), inc=d_f, inc_n=16)
        wo_mark = d_f.n

        lc_mark = [0, 0]
        xc_mark = [0]
        attn_done = {}
        tr_done = {}
        r_done = {}
        y_done = {}
        yfh_done = {}
        yfl_done = {}
        yf_dma = {}

        def emit_p2_loads(c):
            par = c % 2
            w = []
            if c >= 2:
                w.append((t_p, attn_done[c - 2]))
            P("y", lambda e, c=c, b=lc[par]: e.dma_start(
                out=b[:, :].rearrange("p (hl m) -> p hl m", m=E),
                in_=load_hl_scr[c].rearrange("hl p m -> p hl m")),
              waits=w, inc=d_w[par], inc_n=16)
            lc_mark[par] = d_w[par].n
            w = []
            if c >= 1:
                w.append((v_p, r_done[c - 1]))
            P("y", lambda e, c=c: e.dma_start(
                out=xc[:, :], in_=x_tm[c * 128:(c + 1) * 128, :]),
              waits=w, inc=d_x[0], inc_n=16)
            xc_mark[0] = d_x[0].n

        emit_p2_loads(0)

        def emit_p2_transposes(c):
            w = [(v_p, y_done[c])]
            if c >= 1:
                w += [(s_p, yfh_done[c - 1]), (v_p, yfl_done[c - 1])]

            def tr_run(eng):
                ins = None
                for j in range(NEC):
                    ins = eng.transpose(p_yT[:, j * 128:(j + 1) * 128],
                                        ry[:, j * 128:(j + 1) * 128], ident[:, :])
                return ins
            P("t", tr_run, waits=w, inc=t_p)
            tr_done[c] = t_p.n
            wv_ = [(t_p, tr_done[c])]
            if c >= 1:
                wv_.append((d_e[0], yf_dma.get(c - 1, 0)))
            P("s", lambda e: e.activation(yfh[:, :], p_yT[:, :], AF.Copy),
              waits=wv_, inc=s_p)
            yfh_done[c] = s_p.n
            P("v", lambda e: e.tensor_tensor(yfl[:, :], p_yT[:, :], yfh[:, :], ALU.subtract),
              waits=[(s_p, yfh_done[c])], inc=v_p)
            yfl_done[c] = v_p.n
            P("g", lambda e, c=c: e.dma_start(
                out=yfh_scr.rearrange("j p s -> p j s")[:, :, c * 128:(c + 1) * 128],
                in_=yfh[:, :].rearrange("p (j s) -> p j s", s=128)),
              waits=[(s_p, yfh_done[c])], inc=d_e[0], inc_n=16)
            P("g", lambda e, c=c: e.dma_start(
                out=yfl_scr.rearrange("j p s -> p j s")[:, :, c * 128:(c + 1) * 128],
                in_=yfl[:, :].rearrange("p (j s) -> p j s", s=128)),
              waits=[(v_p, yfl_done[c])], inc=d_e[0], inc_n=16)
            yf_dma[c] = d_e[0].n

        for c in range(NSC):
            par = c % 2
            waits = [(d_w[par], lc_mark[par]), (d_f, wo_mark)]
            if c >= 1:
                waits.append((v_p, r_done[c - 1]))

            def attn_run(eng, b=lc[par]):
                ins = None
                WOL = NEC * E
                for et in range(4):
                    for j in range(NEC):
                        slh = slice(j * E + et * 512, j * E + (et + 1) * 512)
                        sll = slice(WOL + j * E + et * 512, WOL + j * E + (et + 1) * 512)
                        ins = eng.matmul(p_attn[:, et * 512:(et + 1) * 512],
                                         b[:, j * 128:(j + 1) * 128], wo_s[:, slh],
                                         start=(j == 0), stop=False)
                        ins = eng.matmul(p_attn[:, et * 512:(et + 1) * 512],
                                         b[:, E + j * 128:E + (j + 1) * 128], wo_s[:, slh],
                                         start=False, stop=False)
                        ins = eng.matmul(p_attn[:, et * 512:(et + 1) * 512],
                                         b[:, j * 128:(j + 1) * 128], wo_s[:, sll],
                                         start=False, stop=(j == NEC - 1))
                return ins
            P("t", attn_run, waits=waits, inc=t_p)
            attn_done[c] = t_p.n

            if c >= 1:
                emit_p2_transposes(c - 1)

            rw = [(t_p, attn_done[c]), (d_x[0], xc_mark[0])]
            if c >= 1:
                rw.append((t_p, tr_done[c - 1]))
            P("v", lambda e: e.scalar_tensor_tensor(
                ry[:, :], p_attn[:, :], 1.0, xc[:, :], ALU.mult, ALU.add,
                accum_out=stats[:, 0:1]),
              waits=rw, inc=v_p)
            r_done[c] = v_p.n
            if c + 1 < NSC:
                emit_p2_loads(c + 1)
            P("s", lambda e: e.activation(garb[:, :], ry[:, :], AF.Square,
                                          accum_out=stats[:, 1:2]),
              waits=[(v_p, r_done[c])], inc=s_p)
            sq_mark = s_p.n
            P("v", lambda e: e.tensor_scalar_mul(stats[:, 2:4], stats[:, 0:2], 1.0 / E),
              waits=[(s_p, sq_mark)])
            P("v", lambda e: e.tensor_tensor(stats[:, 4:5], stats[:, 2:3], stats[:, 2:3], ALU.mult))
            P("v", lambda e: e.tensor_scalar(stats[:, 4:5], stats[:, 4:5], -1.0, EPS,
                                             ALU.mult, ALU.add))
            P("v", lambda e: e.tensor_tensor(stats[:, 5:6], stats[:, 4:5], stats[:, 3:4], ALU.add))
            P("s", lambda e: e.activation(stats[:, 7:8], stats[:, 5:6], AF.Sqrt),
              waits=[(v_p, v_p.n)])
            sqrt_mark = s_p.n
            P("v", lambda e: e.reciprocal(stats[:, 11:12], stats[:, 7:8]),
              waits=[(s_p, sqrt_mark)])
            P("v", lambda e: e.tensor_scalar(ry[:, :], ry[:, :], stats[:, 2:3], stats[:, 11:12],
                                             ALU.subtract, ALU.mult))
            y_done[c] = v_p.n

        emit_p2_transposes(NSC - 1)
        p2_stack.close()
        p2_psum.close()
        barrier()

        p3_stack = ExitStack()
        yfh_s = [p3_stack.enter_context(nc.sbuf_tensor(f"yfh_s{i}", [128, NEC * SB], BF16)) for i in range(2)]
        yfl_s = [p3_stack.enter_context(nc.sbuf_tensor(f"yfl_s{i}", [128, NEC * SB], BF16)) for i in range(2)]
        hh_h = p3_stack.enter_context(nc.sbuf_tensor("hh_h", [128, NHC * SB], BF16))
        hh_l = p3_stack.enter_context(nc.sbuf_tensor("hh_l", [128, NHC * SB], BF16))
        wgt_s = [p3_stack.enter_context(nc.sbuf_tensor(f"wgt{i}", [128, E], BF16)) for i in range(2)]
        wut_s = [p3_stack.enter_context(nc.sbuf_tensor(f"wut{i}", [128, E], BF16)) for i in range(2)]
        wdt_s = [p3_stack.enter_context(nc.sbuf_tensor(f"wdt{i}", [128, NHC * 128], BF16)) for i in range(2)]
        gsil = [p3_stack.enter_context(nc.sbuf_tensor(f"gsil{i}", [128, SB], F32)) for i in range(2)]
        hhf = [p3_stack.enter_context(nc.sbuf_tensor(f"hhf{i}", [128, SB], F32)) for i in range(2)]
        r2 = p3_stack.enter_context(nc.sbuf_tensor("r2", [128, NEC * SB], F32))
        r2h = p3_stack.enter_context(nc.sbuf_tensor("r2h", [128, NEC * SB], BF16))
        r2l = p3_stack.enter_context(nc.sbuf_tensor("r2l", [128, NEC * SB], BF16))
        r2sq = p3_stack.enter_context(nc.sbuf_tensor("r2sq", [128, NEC * SB], BF16))
        bc = p3_stack.enter_context(nc.sbuf_tensor("bc", [128, 2 * SB], F32))
        p3_psum = ExitStack()
        p_g = [p3_psum.enter_context(nc.psum_tensor(f"p3g{i}", [128, SB], F32)) for i in range(2)]
        p_u = [p3_psum.enter_context(nc.psum_tensor(f"p3u{i}", [128, SB], F32)) for i in range(2)]
        p_f = [p3_psum.enter_context(nc.psum_tensor(f"p3f{i}", [128, SB], F32)) for i in range(2)]
        p_stat = p3_psum.enter_context(nc.psum_tensor("p_stat", [128, 2 * SB], F32))
        p_bc = p3_psum.enter_context(nc.psum_tensor("p_bc", [128, 2 * SB], F32))

        yf_mark = [0]
        wgu_mark = [0, 0]
        wdt_mark = [0, 0]
        gu_done = {}
        silu_done = {}
        hhf_done = {}
        hhh_done = {}
        hhl_done = {}
        down_done = {}
        r2_done = {}
        out_dma = {}
        stat_done = {}

        def emit_ln2(sb):
                last_r2 = r2_done[(sb, NEC - 1)]
                P("s", lambda e: e.activation(r2h[:, :], r2[:, :], AF.Copy),
                  waits=[(v_p, last_r2)], inc=s_p)
                r2h_mark = s_p.n
                P("v", lambda e: e.tensor_tensor(r2l[:, :], r2[:, :], r2h[:, :], ALU.subtract),
                  waits=[(s_p, r2h_mark)], inc=v_p)
                P("v", lambda e: e.tensor_tensor(r2sq[:, :], r2[:, :], r2[:, :], ALU.mult),
                  inc=v_p)
                split_mark = v_p.n

                def stat_run(eng):
                    first = True
                    for dc in range(NEC):
                        for part in (r2h, r2l):
                            eng.matmul(p_stat[0:1, 0:SB], onc[:, :],
                                       part[:, dc * SB:(dc + 1) * SB],
                                       start=first, stop=(dc == NEC - 1 and part is r2l))
                            first = False
                    ins = None
                    for dc in range(NEC):
                        ins = eng.matmul(p_stat[0:1, SB:2 * SB], onc[:, :],
                                         r2sq[:, dc * SB:(dc + 1) * SB],
                                         start=(dc == 0), stop=(dc == NEC - 1))
                    return ins
                P("t", stat_run, waits=[(v_p, split_mark)], inc=t_p)
                stat_done[sb] = t_p.n

                P("v", lambda e: e.tensor_scalar_mul(srow[0:1, 0:2 * SB], p_stat[0:1, 0:2 * SB], 1.0 / E),
                  waits=[(t_p, stat_done[sb])])
                P("v", lambda e: e.tensor_tensor(srow[0:1, 2 * SB:3 * SB], srow[0:1, 0:SB], srow[0:1, 0:SB], ALU.mult))
                P("v", lambda e: e.tensor_scalar(srow[0:1, 2 * SB:3 * SB], srow[0:1, 2 * SB:3 * SB],
                                                 -1.0, EPS, ALU.mult, ALU.add))
                P("v", lambda e: e.tensor_tensor(srow[0:1, SB:2 * SB], srow[0:1, 2 * SB:3 * SB],
                                                 srow[0:1, SB:2 * SB], ALU.add))
                P("s", lambda e: e.activation(srow[0:1, 2 * SB:3 * SB], srow[0:1, SB:2 * SB], AF.Sqrt),
                  waits=[(v_p, v_p.n)])
                P("v", lambda e: e.reciprocal(srow[0:1, SB:2 * SB], srow[0:1, 2 * SB:3 * SB]),
                  waits=[(s_p, s_p.n)])
                P("v", lambda e: e.tensor_copy(srow_bf[0:1, 0:2 * SB], srow[0:1, 0:2 * SB]))
                P("v", lambda e: e.tensor_tensor(srow[0:1, 2 * SB:4 * SB], srow[0:1, 0:2 * SB],
                                                 srow_bf[0:1, 0:2 * SB], ALU.subtract))
                P("v", lambda e: e.tensor_copy(srow_bf[0:1, 2 * SB:4 * SB], srow[0:1, 2 * SB:4 * SB]))
                rows_mark = v_p.n

                def bcast_run(eng):
                    eng.matmul(p_bc[:, 0:SB], onr[0:1, :], srow_bf[0:1, 0:SB], start=True, stop=False)
                    eng.matmul(p_bc[:, 0:SB], onr[0:1, :], srow_bf[0:1, 2 * SB:3 * SB], start=False, stop=True)
                    eng.matmul(p_bc[:, SB:2 * SB], onr[0:1, :], srow_bf[0:1, SB:2 * SB], start=True, stop=False)
                    return eng.matmul(p_bc[:, SB:2 * SB], onr[0:1, :], srow_bf[0:1, 3 * SB:4 * SB], start=False, stop=True)
                P("t", bcast_run, waits=[(v_p, rows_mark)], inc=t_p)
                bc_mark = t_p.n

                P("v", lambda e: e.tensor_copy(bc[:, :], p_bc[:, :]),
                  waits=[(t_p, bc_mark)], inc=v_p)
                bc_ev = v_p.n

                for dc in range(NEC):
                    P("v", lambda e, dc=dc: e.tensor_tensor(
                        r2[:, dc * SB:(dc + 1) * SB], r2[:, dc * SB:(dc + 1) * SB],
                        bc[:, 0:SB], ALU.subtract))
                    P("v", lambda e, dc=dc: e.tensor_tensor(
                        r2[:, dc * SB:(dc + 1) * SB], r2[:, dc * SB:(dc + 1) * SB],
                        bc[:, SB:2 * SB], ALU.mult))
                norm_mark = v_p.n

                P("g", lambda e, sb=sb: e.dma_start(
                    out=out_fm.rearrange("(j p) s -> p j s", p=128)[:, :, sb * SB:(sb + 1) * SB],
                    in_=r2[:, :].rearrange("p (j s) -> p j s", s=SB)),
                  waits=[(v_p, norm_mark)], inc=d_e[1], inc_n=16)
                out_dma[sb] = d_e[1].n

        pending_ln2 = []
        split_marks_ln2 = {}

        yf_marks = {}
        wgu_marks = {}

        def emit_yf_loads(sb):
            sp3 = sb % 2
            w = []
            if sb >= 2:
                w.append((t_p, gu_done[(sb - 2, NHC - 1, "u")]))
                w.append((v_p, r2_done[(sb - 2, NEC - 1)]))
            for scr, dst in ((yfh_scr, yfh_s[sp3]), (yfl_scr, yfl_s[sp3])):
                P("y", lambda e, scr=scr, dst=dst, sb=sb: e.dma_start(
                    out=dst[:, :].rearrange("p (j s) -> p j s", s=SB),
                    in_=scr.rearrange("j p s -> p j s")[:, :, sb * SB:(sb + 1) * SB]),
                  waits=w, inc=d_x[sp3], inc_n=16)
                w = []
            yf_marks[sb] = d_x[sp3].n

        def emit_wgu_load(sb, hc):
            par = hc % 2
            w = []
            if hc >= 2:
                w.append((t_p, gu_done[(sb, hc - 2, "u")]))
            elif sb >= 1:
                w.append((t_p, gu_done[(sb - 1, NHC - 2 + hc, "u")]))
            for scr, buf in ((wgT_sw, wgt_s[par]), (wuT_sw, wut_s[par])):
                P("y", lambda e, scr=scr, b=buf, hc=hc: e.dma_start(
                    out=b[:, :], in_=scr[hc]),
                  waits=w, inc=d_w[par], inc_n=16)
                w = []
            wgu_marks[(sb, hc)] = d_w[par].n

        for sb in range(NSB):
            sp3 = sb % 2
            if sb not in yf_marks:
                emit_yf_loads(sb)

            for hc in range(NHC):
                par = hc % 2
                if (sb, hc) not in wgu_marks:
                    emit_wgu_load(sb, hc)

                for proj, wbuf, ps in (("g", wgt_s, p_g), ("u", wut_s, p_u)):
                    waits = [(d_w[par], wgu_marks[(sb, hc)]), (d_x[sp3], yf_marks[sb])]
                    if hc >= 2:
                        ctr, key = ((s_p, "silu") if proj == "g" else (v_p, "hhf"))
                        mark = silu_done[(sb, hc - 2)] if proj == "g" else hhf_done[(sb, hc - 2)]
                        waits.append((ctr, mark))

                    def gu_run(eng, wbuf=wbuf[par], ps=ps[par],
                               yfh=yfh_s[sp3], yfl=yfl_s[sp3]):
                        first = True
                        ins = None
                        for j in range(NEC):
                            for yb in (yfh, yfl):
                                ins = eng.matmul(ps[:, :], wbuf[:, j * 128:(j + 1) * 128],
                                                 yb[:, j * SB:(j + 1) * SB],
                                                 start=first, stop=(j == NEC - 1 and yb is yfl))
                                first = False
                        return ins
                    P("t", gu_run, waits=waits, inc=t_p)
                    gu_done[(sb, hc, proj)] = t_p.n

                P("s", lambda e, ps=p_g[par], o=gsil[par], hc=hc: e.activation(
                    o[:, :], ps[:, :], AF.Silu, scale=sg_t[:, hc:hc + 1]),
                  waits=[(t_p, gu_done[(sb, hc, "g")])], inc=s_p)
                silu_done[(sb, hc)] = s_p.n
                P("v", lambda e, ps=p_u[par], g=gsil[par], o=hhf[par], hc=hc: e.scalar_tensor_tensor(
                    o[:, :], ps[:, :], su_t[:, hc:hc + 1], g[:, :], ALU.mult, ALU.mult),
                  waits=[(t_p, gu_done[(sb, hc, "u")]), (s_p, silu_done[(sb, hc)])],
                  inc=v_p)
                hhf_done[(sb, hc)] = v_p.n
                w_extra = []
                if sb >= 1 and hc == 0:
                    w_extra = [(t_p, down_done[(sb - 1, 15)])]
                P("s", lambda e, src=hhf[par], hc=hc: e.activation(
                    hh_h[:, hc * SB:(hc + 1) * SB], src[:, :], AF.Copy),
                  waits=[(v_p, hhf_done[(sb, hc)])] + w_extra, inc=s_p)
                hhh_done[(sb, hc)] = s_p.n
                P("v", lambda e, src=hhf[par], hc=hc: e.tensor_tensor(
                    hh_l[:, hc * SB:(hc + 1) * SB], src[:, :],
                    hh_h[:, hc * SB:(hc + 1) * SB], ALU.subtract),
                  waits=[(s_p, hhh_done[(sb, hc)])], inc=v_p)
                hhl_done[(sb, hc)] = v_p.n
                if hc == 3 and pending_ln2:
                    emit_ln2(pending_ln2.pop(0))

            if sb + 1 < NSB:
                emit_yf_loads(sb + 1)
                emit_wgu_load(sb + 1, 0)
                emit_wgu_load(sb + 1, 1)
            for dc in range(NEC):
                par = dc % 2
                w = []
                if dc >= 2:
                    w.append((t_p, down_done[(sb, dc - 2)]))
                elif sb >= 1:
                    w.append((t_p, down_done[(sb - 1, dc + NEC - 2)]))
                dch = d_st[par]
                P("y", lambda e, b=wdt_s[par], dc=dc: e.dma_start(
                    out=b[:, :], in_=wdT_sw[dc]),
                  waits=w, inc=dch, inc_n=16)
                wdt_mark[par] = dch.n

                waits = [(dch, wdt_mark[par]),
                         (v_p, hhl_done[(sb, NHC - 1)])]
                if dc >= 2:
                    waits.append((v_p, r2_done[(sb, dc - 2)]))

                def down_run(eng, b=wdt_s[par], ps=p_f[par]):
                    first = True
                    ins = None
                    for j in range(NHC):
                        for hb in (hh_h, hh_l):
                            ins = eng.matmul(ps[:, :], b[:, j * 128:(j + 1) * 128],
                                             hb[:, j * SB:(j + 1) * SB],
                                             start=first, stop=(j == NHC - 1 and hb is hh_l))
                            first = False
                    return ins
                P("t", down_run, waits=waits, inc=t_p)
                down_done[(sb, dc)] = t_p.n

                w = [(t_p, down_done[(sb, dc)])]
                if sb >= 1 and dc == 0:
                    w.append((d_e[1], out_dma[sb - 1]))
                P("v", lambda e, ps=p_f[par], dc=dc, yy=yfh_s[sp3]: e.scalar_tensor_tensor(
                    r2[:, dc * SB:(dc + 1) * SB], ps[:, :], sd_t[:, dc:dc + 1],
                    yy[:, dc * SB:(dc + 1) * SB], ALU.mult, ALU.add),
                  waits=w, inc=v_p)
                P("v", lambda e, dc=dc, yy=yfl_s[sp3]: e.tensor_tensor(
                    r2[:, dc * SB:(dc + 1) * SB], r2[:, dc * SB:(dc + 1) * SB],
                    yy[:, dc * SB:(dc + 1) * SB], ALU.add),
                  inc=v_p)
                r2_done[(sb, dc)] = v_p.n

            pending_ln2.append(sb)


        while pending_ln2:
            emit_ln2(pending_ln2.pop(0))

        p3_stack.close()
        p3_psum.close()

        final_out = d_e[1].n
        prog.streams["g"].append(lambda e, v=final_out: e.wait_ge(d_e[1].h, v))

        block = g_stack.enter_context(nc.Block())

        @block.tensor
        def _(eng):
            for f in prog.streams["t"]:
                f(eng)

        @block.vector
        def _(eng):
            for f in prog.streams["v"]:
                f(eng)

        @block.scalar
        def _(eng):
            for f in prog.streams["s"]:
                f(eng)

        @block.sync
        def _(eng):
            for f in prog.streams["y"]:
                f(eng)

        @block.gpsimd
        def _(eng):
            for f in prog.streams["g"]:
                f(eng)

    return nc


_NC_CACHE = None


def kernel(L0_attn_x, L0_attn_Wv, L0_attn_Wg, L0_attn_bg, L0_attn_Wq, L0_attn_bq,
           L0_attn_Wo, L0_ffn_wq_gate, L0_ffn_s_gate, L0_ffn_wq_up, L0_ffn_s_up,
           L0_ffn_wq_down, L0_ffn_s_down):
    global _NC_CACHE
    x = np.asarray(L0_attn_x, dtype=np.float32)
    B = x.shape[0]
    assert x.shape == (8, S, E)

    shared = dict(
        wg=np.ascontiguousarray(np.asarray(L0_attn_Wg, np.float32)),
        wv=np.ascontiguousarray(np.asarray(L0_attn_Wv, np.float32)),
        wq=np.ascontiguousarray(np.asarray(L0_attn_Wq, np.float32)),
        wo=np.ascontiguousarray(np.asarray(L0_attn_Wo, np.float32)),
        bg2d=np.ascontiguousarray(np.asarray(L0_attn_bg, np.float32).reshape(NEC, 128).T),
        bq2d=np.ascontiguousarray(np.asarray(L0_attn_bq, np.float32).reshape(NEC, 128).T),
        wgT_i=np.ascontiguousarray(np.asarray(L0_ffn_wq_gate, np.int32).T),
        wuT_i=np.ascontiguousarray(np.asarray(L0_ffn_wq_up, np.int32).T),
        wdT_i=np.ascontiguousarray(np.asarray(L0_ffn_wq_down, np.int32).T),
        sg2d=np.ascontiguousarray(np.asarray(L0_ffn_s_gate, np.float32).reshape(NHC, 128).T),
        su2d=np.ascontiguousarray(np.asarray(L0_ffn_s_up, np.float32).reshape(NHC, 128).T),
        sd2d=np.ascontiguousarray(np.asarray(L0_ffn_s_down, np.float32).reshape(NEC, 128).T),
        dec_i=np.full((128, SGRP), np.float32(DECAY), dtype=np.float32),
        ident_i=np.eye(128, dtype=np.float32),
        onc_i=np.ones((128, 1), dtype=ml_dtypes.bfloat16),
        onr_i=np.ones((1, 128), dtype=ml_dtypes.bfloat16),
    )
    in_maps = []
    for b in range(B):
        m = dict(shared)
        m["x_tm"] = np.ascontiguousarray(x[b])
        m["xT"] = np.ascontiguousarray(x[b].T)
        in_maps.append(m)

    if _NC_CACHE is None:
        _NC_CACHE = build_nc()
    res = run_bass_kernel_spmd(_NC_CACHE, in_maps, core_ids=list(range(8)))
    out = np.stack([np.asarray(r["out_fm"]).T for r in res.results], axis=0)
    return out


# version 12
# speedup vs baseline: 1.0642x; 1.0190x over previous
import numpy as np
import ml_dtypes
import concourse.bass as bass
import concourse.mybir as mybir
from concourse.bass_utils import run_bass_kernel_spmd
from contextlib import ExitStack

F32 = mybir.dt.float32
BF16 = mybir.dt.bfloat16
I32 = mybir.dt.int32
AF = mybir.ActivationFunctionType
ALU = mybir.AluOpType

import os
ABL_NO_PW = os.environ.get("ABL_NO_PW") == "1"
ABL_NO_LHLL = os.environ.get("ABL_NO_LHLL") == "1"
ABL_NO_SPREAD = os.environ.get("ABL_NO_SPREAD") == "1"

E = 2048
S = 2048
H = 5632
NEC = E // 128
NHC = H // 128
NSC = S // 128
SGRP = 512
NSG = S // SGRP
SB = 256
NSB = S // SB
EPS = 1e-5
DECAY = 0.95
SCALE2 = float(np.float32(np.sqrt(1.0 - DECAY))) ** 2


class Ctr:
    def __init__(self, handle):
        self.h = handle
        self.n = 0


class Prog:
    def __init__(self):
        self.streams = {k: [] for k in ("t", "v", "s", "y", "g", "g2")}
        self.auto = {}

    def emit(self, key, build, waits=(), inc=None, inc_n=1):
        waits = list(waits)
        if key in self.auto:
            ctr = self.auto[key]
            waits = [(ctr, ctr.n)] + waits
            if inc is None:
                inc = ctr
        if key == "g2":
            key = "g"
        w = [(c.h, v) for (c, v) in waits if v > 0]
        tgt = None
        if inc is not None:
            inc.n += inc_n
            tgt = (inc.h, inc_n)

        def run(eng, w=w, build=build, tgt=tgt):
            for h, v in w:
                eng.wait_ge(h, v)
            ins = build(eng)
            if tgt is not None:
                if ins is None:
                    ins = eng.nop()
                ins.then_inc(*tgt)
        self.streams[key].append(run)


def build_nc():
    nc = bass.Bass("TRN2", target_bir_lowering=False)

    xT = nc.dram_tensor("xT", [E, S], F32, kind="ExternalInput").ap()
    x_tm = nc.dram_tensor("x_tm", [S, E], F32, kind="ExternalInput").ap()
    wg = nc.dram_tensor("wg", [E, E], F32, kind="ExternalInput").ap()
    wv = nc.dram_tensor("wv", [E, E], F32, kind="ExternalInput").ap()
    wq = nc.dram_tensor("wq", [E, E], F32, kind="ExternalInput").ap()
    wo = nc.dram_tensor("wo", [E, E], F32, kind="ExternalInput").ap()
    bg2d = nc.dram_tensor("bg2d", [128, NEC], F32, kind="ExternalInput").ap()
    bq2d = nc.dram_tensor("bq2d", [128, NEC], F32, kind="ExternalInput").ap()
    wgT_i = nc.dram_tensor("wgT_i", [E, H], I32, kind="ExternalInput").ap()
    wuT_i = nc.dram_tensor("wuT_i", [E, H], I32, kind="ExternalInput").ap()
    wdT_i = nc.dram_tensor("wdT_i", [H, E], I32, kind="ExternalInput").ap()
    sg2d = nc.dram_tensor("sg2d", [128, NHC], F32, kind="ExternalInput").ap()
    su2d = nc.dram_tensor("su2d", [128, NHC], F32, kind="ExternalInput").ap()
    sd2d = nc.dram_tensor("sd2d", [128, NEC], F32, kind="ExternalInput").ap()
    dec_i = nc.dram_tensor("dec_i", [128, SGRP], F32, kind="ExternalInput").ap()
    ident_i = nc.dram_tensor("ident_i", [128, 128], F32, kind="ExternalInput").ap()
    onc_i = nc.dram_tensor("onc_i", [128, 1], BF16, kind="ExternalInput").ap()
    onr_i = nc.dram_tensor("onr_i", [1, 128], BF16, kind="ExternalInput").ap()

    out_fm = nc.dram_tensor("out_fm", [E, S], F32, kind="ExternalOutput").ap()
    load_hl_scr = nc.dram_tensor("load_hl_scr", [NSC, 2, 128, E], BF16).ap()
    wg_hl_scr = nc.dram_tensor("wg_hl_scr", [NEC, 2, 128, E], BF16).ap()
    wv_hl_scr = nc.dram_tensor("wv_hl_scr", [NEC, 2, 128, E], BF16).ap()
    wq_hl_scr = nc.dram_tensor("wq_hl_scr", [NEC, 2, 128, E], BF16).ap()
    wo_hl_scr = nc.dram_tensor("wo_hl_scr", [NEC, 2, 128, E], BF16).ap()
    yfh_scr = nc.dram_tensor("yfh_scr", [NEC, 128, S], BF16).ap()
    yfl_scr = nc.dram_tensor("yfl_scr", [NEC, 128, S], BF16).ap()
    wgT_sw = nc.dram_tensor("wgT_sw", [NHC, 128, E], BF16).ap()
    wuT_sw = nc.dram_tensor("wuT_sw", [NHC, 128, E], BF16).ap()
    wdT_sw = nc.dram_tensor("wdT_sw", [NEC, 128, NHC * 128], BF16).ap()

    prog = Prog()
    P = prog.emit

    with ExitStack() as g_stack:
        ge = g_stack.enter_context
        d_f = Ctr(ge(nc.semaphore("d_f")))
        d_w = [Ctr(ge(nc.semaphore(f"d_w{i}"))) for i in range(2)]
        d_x = [Ctr(ge(nc.semaphore(f"d_x{i}"))) for i in range(2)]
        d_c = Ctr(ge(nc.semaphore("d_c")))
        d_d = Ctr(ge(nc.semaphore("d_d")))
        d_d2 = Ctr(ge(nc.semaphore("d_d2")))
        d_e = [Ctr(ge(nc.semaphore(f"d_e{i}"))) for i in range(2)]
        d_as = [Ctr(ge(nc.semaphore(f"d_as{i}"))) for i in range(2)]
        d_st = [Ctr(ge(nc.semaphore(f"d_st{i}"))) for i in range(3)]
        d_pw = [Ctr(ge(nc.semaphore(f"d_pw{i}"))) for i in range(4)]
        d_wo = [Ctr(ge(nc.semaphore(f"d_wo{i}"))) for i in range(2)]
        t_p = Ctr(ge(nc.semaphore("t_p")))
        v_p = Ctr(ge(nc.semaphore("v_p")))
        s_p = Ctr(ge(nc.semaphore("s_p")))
        g_p = Ctr(ge(nc.semaphore("g_p")))
        prog.auto = {"v": v_p, "s": s_p, "g2": g_p}
        ALL_DMA = [d_f, d_w[0], d_w[1], d_x[0], d_x[1], d_c, d_d, d_d2, d_e[0], d_e[1],
                   d_as[0], d_as[1], d_st[0], d_st[1], d_st[2],
                   d_pw[0], d_pw[1], d_pw[2], d_pw[3], d_wo[0], d_wo[1]]

        def barrier():
            for key, ctr in (("t", t_p), ("v", v_p), ("s", s_p)):
                P(key, lambda e: e.nop(), waits=[(ctr, ctr.n)], inc=ctr)
            marks = [(t_p, t_p.n), (v_p, v_p.n), (s_p, s_p.n)] + \
                    [(c, c.n) for c in ALL_DMA]
            for key in ("t", "v", "s", "y", "g"):
                def run(eng, marks=tuple(marks)):
                    for c, v in marks:
                        if v > 0:
                            eng.wait_ge(c.h, v)
                prog.streams[key].append(run)

        bg_t = ge(nc.sbuf_tensor("bg_t", [128, NEC], F32))
        bq_t = ge(nc.sbuf_tensor("bq_t", [128, NEC], F32))
        sg_t = ge(nc.sbuf_tensor("sg_t", [128, NHC], F32))
        su_t = ge(nc.sbuf_tensor("su_t", [128, NHC], F32))
        sd_t = ge(nc.sbuf_tensor("sd_t", [128, NEC], F32))
        dec_t = ge(nc.sbuf_tensor("dec_t", [128, SGRP], F32))
        ident = ge(nc.sbuf_tensor("ident", [128, 128], F32))
        onc = ge(nc.sbuf_tensor("onc", [128, 1], BF16))
        onr = ge(nc.sbuf_tensor("onr", [1, 128], BF16))
        carry = ge(nc.sbuf_tensor("carry", [128, NEC], F32))
        stats = ge(nc.sbuf_tensor("stats", [128, 16], F32))
        srow = ge(nc.sbuf_tensor("srow", [1, 4 * SB], F32))
        srow_bf = ge(nc.sbuf_tensor("srow_bf", [1, 4 * SB], BF16))

        for dst, src in ((bg_t, bg2d), (bq_t, bq2d), (sg_t, sg2d), (su_t, su2d),
                         (sd_t, sd2d), (dec_t, dec_i), (ident, ident_i),
                         (onc, onc_i), (onr, onr_i)):
            P("y", lambda e, d=dst, s=src: e.dma_start(out=d[:, :], in_=s),
              inc=d_f, inc_n=16)
        glob_ld = d_f.n

        p1_stack = ExitStack()
        xgh = [p1_stack.enter_context(nc.sbuf_tensor(f"xgh{i}", [128, NEC * SGRP], BF16)) for i in range(2)]
        xgl = [p1_stack.enter_context(nc.sbuf_tensor(f"xgl{i}", [128, NEC * SGRP], BF16)) for i in range(2)]
        xstg = [p1_stack.enter_context(nc.sbuf_tensor(f"xstg{i}", [128, SGRP], F32)) for i in range(2)]
        wgs = [p1_stack.enter_context(nc.sbuf_tensor(f"wgs{i}", [128, 2 * E], BF16)) for i in range(2)]
        wvs = [p1_stack.enter_context(nc.sbuf_tensor(f"wvs{i}", [128, 2 * E], BF16)) for i in range(2)]
        wqs = [p1_stack.enter_context(nc.sbuf_tensor(f"wqs{i}", [128, 2 * E], BF16)) for i in range(2)]
        wtmp = [p1_stack.enter_context(nc.sbuf_tensor(f"wtmp{i}", [128, E], F32)) for i in range(3)]
        gate_sb = [p1_stack.enter_context(nc.sbuf_tensor(f"gate{i}", [128, SGRP], F32)) for i in range(2)]
        que_sb = [p1_stack.enter_context(nc.sbuf_tensor(f"que{i}", [128, SGRP], F32)) for i in range(2)]
        store_sb = [p1_stack.enter_context(nc.sbuf_tensor(f"store{i}", [128, SGRP], F32)) for i in range(2)]
        mem_sb = [p1_stack.enter_context(nc.sbuf_tensor(f"mem{i}", [128, SGRP], F32)) for i in range(2)]
        load_sb = [p1_stack.enter_context(nc.sbuf_tensor(f"load{i}", [128, SGRP], F32)) for i in range(2)]
        lh_sb = [p1_stack.enter_context(nc.sbuf_tensor(f"lh{i}", [128, SGRP], BF16)) for i in range(2)]
        ll_sb = [p1_stack.enter_context(nc.sbuf_tensor(f"ll{i}", [128, SGRP], BF16)) for i in range(2)]
        PWW = 1408
        wint = [p1_stack.enter_context(nc.sbuf_tensor(f"wint{i}", [128, PWW], I32)) for i in range(3)]
        wbf = [p1_stack.enter_context(nc.sbuf_tensor(f"wbf{i}", [128, PWW], BF16)) for i in range(2)]
        wof = p1_stack.enter_context(nc.sbuf_tensor("wof", [128, 1024], F32))
        woh_sl = p1_stack.enter_context(nc.sbuf_tensor("woh_sl", [128, 1024], BF16))
        wol_sl = p1_stack.enter_context(nc.sbuf_tensor("wol_sl", [128, 1024], BF16))

        p1_psum = ExitStack()
        p_gate = [p1_psum.enter_context(nc.psum_tensor(f"pg{i}", [128, SGRP], F32)) for i in range(2)]
        p_v = [p1_psum.enter_context(nc.psum_tensor(f"pv{i}", [128, SGRP], F32)) for i in range(2)]
        p_que = [p1_psum.enter_context(nc.psum_tensor(f"pq{i}", [128, SGRP], F32)) for i in range(2)]

        pw_items = []
        WG_W = 1408
        for k in range(NEC):
            for h2 in range(4):
                dst = wgT_sw[h2 * 11:(h2 + 1) * 11, :, k * 128:(k + 1) * 128] \
                    .rearrange("hc p m -> p hc m")
                pw_items.append(("ffn", wgT_i[k * 128:(k + 1) * 128, h2 * WG_W:(h2 + 1) * WG_W],
                                 dst, WG_W))
        for k in range(NEC):
            for h2 in range(4):
                dst = wuT_sw[h2 * 11:(h2 + 1) * 11, :, k * 128:(k + 1) * 128] \
                    .rearrange("hc p m -> p hc m")
                pw_items.append(("ffn", wuT_i[k * 128:(k + 1) * 128, h2 * WG_W:(h2 + 1) * WG_W],
                                 dst, WG_W))
        WD_W = 1024
        for k in range(NHC):
            for h2 in range(2):
                dst = wdT_sw[h2 * 8:(h2 + 1) * 8, :, k * 128:(k + 1) * 128] \
                    .rearrange("dc p m -> p dc m")
                pw_items.append(("ffn", wdT_i[k * 128:(k + 1) * 128, h2 * WD_W:(h2 + 1) * WD_W],
                                 dst, WD_W))
        wo_items = [("wo", wo[k * 128:(k + 1) * 128, h2 * 1024:(h2 + 1) * 1024],
                     (wo_hl_scr[k, 0, :, h2 * 1024:(h2 + 1) * 1024],
                      wo_hl_scr[k, 1, :, h2 * 1024:(h2 + 1) * 1024]), 1024)
                    for k in range(NEC) for h2 in range(2)]
        for k in range(2 * NEC):
            pw_items.insert(60 + k * 5, wo_items[k])
        if ABL_NO_PW:
            pw_items = []
        pw_state = {"emitted": 0, "in": [], "conv": [], "outs": []}

        def emit_pw_step():
            k = pw_state["emitted"]
            if k >= len(pw_items):
                return
            pw_state["emitted"] += 1
            kind, src_, dst, width = pw_items[k]
            if kind == "ffn":
                fk = pw_state.setdefault("fk", 0)
                pw_state["fk"] = fk + 1
                p4 = fk % 3
                p2 = fk % 2
                ch_in = d_pw[p4]
                w_in = []
                cm = pw_state.get(("fconv", p4))
                if cm is not None:
                    w_in.append((s_p, cm))
                P("y", lambda e, s=src_, wd=width, b=wint[p4]: e.dma_start(
                    out=b[:, :wd], in_=s), waits=w_in, inc=ch_in, inc_n=16)
                in_mark = ch_in.n
                w_cv = [(ch_in, in_mark)]
                om = pw_state.get(("fout", p2))
                if om is not None:
                    w_cv.append(((d_d if p2 == 0 else d_d2), om))
                P("s", lambda e, wd=width, b=wint[p4], o=wbf[p2]: e.activation(
                    o[:, :wd], b[:, :wd], AF.Copy, bias=-128.0),
                  waits=w_cv, inc=s_p)
                pw_state[("fconv", p4)] = s_p.n
                ch_out = d_d if p2 == 0 else d_d2
                P("g", lambda e, d=dst, wd=width, o=wbf[p2]: e.dma_start(
                    out=d, in_=o[:, :wd].rearrange("p (c m) -> p c m", m=128)),
                  waits=[(s_p, s_p.n)], inc=ch_out, inc_n=16)
                pw_state[("fout", p2)] = ch_out.n
            else:
                w_in = []
                if pw_state.get("wo_sub") is not None:
                    w_in.append((v_p, pw_state["wo_sub"]))
                P("g", lambda e, s=src_: e.dma_start(out=wof[:, :], in_=s),
                  waits=w_in, inc=d_wo[0], inc_n=16)
                in_mark = d_wo[0].n
                w_cv = [(d_wo[0], in_mark)]
                if pw_state.get("wo_out") is not None:
                    w_cv.append((d_wo[1], pw_state["wo_out"]))
                P("s", lambda e: e.activation(woh_sl[:, :], wof[:, :], AF.Copy),
                  waits=w_cv, inc=s_p)
                hm = s_p.n
                P("v", lambda e: e.tensor_tensor(wol_sl[:, :], wof[:, :], woh_sl[:, :], ALU.subtract),
                  waits=[(s_p, hm)])
                lm = v_p.n
                pw_state["wo_sub"] = lm
                P("g", lambda e, d=dst[0]: e.dma_start(out=d, in_=woh_sl[:, :]),
                  waits=[(s_p, hm)], inc=d_wo[1], inc_n=16)
                P("g", lambda e, d=dst[1]: e.dma_start(out=d, in_=wol_sl[:, :]),
                  waits=[(v_p, lm)], inc=d_wo[1], inc_n=16)
                pw_state["wo_out"] = d_wo[1].n

        def w_slice_view(w, ms):
            return w.rearrange("(j p) m -> p j m", p=128)[:, :, ms * 128:(ms + 1) * 128]

        xg_mark = [0, 0]
        xs_mark = {}
        w_mark = [0, 0]
        w_split = {}
        proj_done = {}
        evac_done = {}
        store_done = {}
        N_ITER = NSG * NEC
        AS_FINAL = 6 * (NEC // 2) * 16

        def emit_xsplit_dma(sg, j):
            par = j % 2
            key = (sg, j - 2) if j >= 2 else ((sg - 1, 14 + j) if sg >= 1 else None)
            w = [(v_p, sub_marks[key])] if key in sub_marks else []
            P("y", lambda e, j=j, sg=sg, stg=xstg[par]: e.dma_start(
                out=stg[:, :],
                in_=xT[j * 128:(j + 1) * 128, sg * SGRP:(sg + 1) * SGRP]),
              waits=w, inc=d_x[par], inc_n=16)
            xdma_marks[(sg, j)] = d_x[par].n

        def emit_xsplit_ops(sg, j):
            par = j % 2
            b = sg % 2
            guard = []
            if sg >= 2 and j == 0:
                guard = [(t_p, proj_done[((sg - 2) * NEC + NEC - 1, "q")])]
            dm = xdma_marks[(sg, j)]
            P("s", lambda e, j=j, stg=xstg[par], o=xgh[b]: e.activation(
                o[:, j * SGRP:(j + 1) * SGRP], stg[:, :], AF.Copy),
              waits=[(d_x[par], dm)] + guard)
            cast_m = s_p.n
            P("v", lambda e, j=j, stg=xstg[par], h=xgh[b], o=xgl[b]: e.tensor_tensor(
                o[:, j * SGRP:(j + 1) * SGRP], stg[:, :],
                h[:, j * SGRP:(j + 1) * SGRP], ALU.subtract),
              waits=[(s_p, cast_m)])
            sub_marks[(sg, j)] = v_p.n
            if j == NEC - 1:
                xs_mark[sg] = (cast_m, v_p.n)

        def emit_xsplit_chunk(sg, j):
            emit_xsplit_dma(sg, j)
            emit_xsplit_ops(sg, j)
        xdma_marks = {}

        sub_marks = {}
        wtmp_state = {}
        as_mark = [0, 0]

        def emit_p1_dmas(i):
            sg, ms = divmod(i, NEC)
            par = i % 2
            if sg == 0:
                marks = []
                for k, wt in enumerate((wg, wv, wq)):
                    wi = wtmp_state.get("idx", 0)
                    wtmp_state["idx"] = wi + 1
                    tb = wtmp[wi % 3]
                    w = []
                    prev = wtmp_state.get(("sub", wi % 3))
                    if prev is not None:
                        w.append((v_p, prev))
                    if i >= 2 and k == 0:
                        w.append((t_p, proj_done[(i - 2, "q")]))
                        w.append((d_as[par], as_mark[par]))
                    ch = d_st[wi % 3]
                    P("y", lambda e, wt=wt, ms=ms, tb=tb: e.dma_start(
                        out=tb[:, :].rearrange("p (j m) -> p j m", m=128),
                        in_=w_slice_view(wt, ms)), waits=w, inc=ch, inc_n=16)
                    marks.append((wi, ch.n))
                wtmp_state[("dmas", i)] = marks
            else:
                w = [(d_as[0], AS_FINAL), (d_as[1], AS_FINAL)] if sg == 1 else []
                if i >= 2:
                    w.append((t_p, proj_done[(i - 2, "q")]))
                for scr, buf in ((wg_hl_scr, wgs[par]), (wv_hl_scr, wvs[par]),
                                 (wq_hl_scr, wqs[par])):
                    P("y", lambda e, scr=scr, ms=ms, b=buf: e.dma_start(
                        out=b[:, :].rearrange("p (hl m) -> p hl m", m=E),
                        in_=scr[ms].rearrange("hl p m -> p hl m")),
                      waits=w, inc=d_w[par], inc_n=16)
                    w = []
            w_mark[par] = d_w[par].n

        def emit_p1_splits(i):
            sg, ms = divmod(i, NEC)
            par = i % 2
            if sg != 0:
                return
            marks = wtmp_state.pop(("dmas", i))
            for (wi, dm), (proj, buf, scr) in zip(marks, (
                    ("g", wgs[par], wg_hl_scr), ("v", wvs[par], wv_hl_scr),
                    ("q", wqs[par], wq_hl_scr))):
                ch = d_st[wi % 3]
                tb = wtmp[wi % 3]
                P("s", lambda e, buf=buf, tb=tb: e.activation(buf[:, :E], tb[:, :], AF.Copy),
                  waits=[(ch, dm)])
                hm = s_p.n
                P("v", lambda e, buf=buf, tb=tb: e.tensor_tensor(
                    buf[:, E:2 * E], tb[:, :], buf[:, :E], ALU.subtract),
                  waits=[(s_p, hm)])
                lm = v_p.n
                wtmp_state[("sub", wi % 3)] = lm
                w_split[(i, proj)] = (hm, lm)
                P("g", lambda e, scr=scr, ms=ms, buf=buf: e.dma_start(
                    out=scr[ms, 0], in_=buf[:, :E]),
                  waits=[(s_p, hm)], inc=d_as[par], inc_n=16)
                P("g", lambda e, scr=scr, ms=ms, buf=buf: e.dma_start(
                    out=scr[ms, 1], in_=buf[:, E:2 * E]),
                  waits=[(v_p, lm)], inc=d_as[par], inc_n=16)
            as_mark[par] = d_as[par].n

        for i in range(N_ITER):
            sg, ms = divmod(i, NEC)
            par = i % 2
            if i == 0:
                for j in range(NEC):
                    emit_xsplit_chunk(0, j)
                emit_p1_dmas(0)
                emit_p1_splits(0)
            if sg + 1 < NSG and 6 <= ms < 14:
                for j in range(2 * (ms - 6), 2 * (ms - 6) + 2):
                    emit_xsplit_dma(sg + 1, j)
            if i + 1 < N_ITER:
                emit_p1_dmas(i + 1)
                if (i + 1) < NEC:
                    emit_p1_splits(i + 1)

            for proj, wb, ps in (("g", wgs, p_gate), ("v", wvs, p_v), ("q", wqs, p_que)):
                waits = [(d_f, glob_ld),
                         (s_p, xs_mark[sg][0]), (v_p, xs_mark[sg][1])]
                if sg == 0:
                    hm, lm = w_split[(i, proj)]
                    waits += [(s_p, hm), (v_p, lm)]
                else:
                    waits.append((d_w[par], w_mark[par]))
                if i >= 2:
                    kind = {"g": "gate_evac", "v": "store_mult", "q": "que_evac"}[proj]
                    ctr = s_p if proj in ("g", "q") else v_p
                    waits.append((ctr, evac_done[(i - 2, kind)]))

                def mm_run(eng, wb=wb[par], ps=ps[par],
                           xh=xgh[sg % 2], xl=xgl[sg % 2]):
                    ins = None
                    for j in range(NEC):
                        ins = eng.matmul(ps[:, :], wb[:, j * 128:(j + 1) * 128],
                                         xh[:, j * SGRP:(j + 1) * SGRP],
                                         start=(j == 0), stop=False)
                        ins = eng.matmul(ps[:, :], wb[:, E + j * 128:E + (j + 1) * 128],
                                         xh[:, j * SGRP:(j + 1) * SGRP],
                                         start=False, stop=False)
                        ins = eng.matmul(ps[:, :], wb[:, j * 128:(j + 1) * 128],
                                         xl[:, j * SGRP:(j + 1) * SGRP],
                                         start=False, stop=(j == NEC - 1))
                    return ins
                P("t", mm_run, waits=waits, inc=t_p)
                proj_done[(i, proj)] = t_p.n

            gw = [(t_p, proj_done[(i, "g")])]
            if i >= 2:
                gw.append((v_p, evac_done[(i - 2, "store_mult")]))
            P("s", lambda e, ps=p_gate[par], o=gate_sb[par], ms=ms: e.activation(
                o[:, :], ps[:, :], AF.Sigmoid, bias=bg_t[:, ms:ms + 1]),
              waits=gw, inc=s_p)
            evac_done[(i, "gate_evac")] = s_p.n
            qw = [(t_p, proj_done[(i, "q")])]
            if i >= 2:
                qw.append((v_p, evac_done[(i - 2, "load")]))
            P("s", lambda e, ps=p_que[par], o=que_sb[par], ms=ms: e.activation(
                o[:, :], ps[:, :], AF.Sigmoid, bias=bq_t[:, ms:ms + 1]),
              waits=qw, inc=s_p)
            evac_done[(i, "que_evac")] = s_p.n

            P("v", lambda e, ps=p_v[par], g=gate_sb[par], o=store_sb[par]: e.tensor_tensor(
                o[:, :], ps[:, :], g[:, :], ALU.mult),
              waits=[(t_p, proj_done[(i, "v")]), (s_p, evac_done[(i, "gate_evac")])],
              inc=v_p)
            evac_done[(i, "store_mult")] = v_p.n
            fence_pre = v_p.n
            if sg == 0:
                init = 0.0
            else:
                init = carry[:, ms:ms + 1]
            P("v", lambda e, st=store_sb[par], m=mem_sb[par], init=init: e.tensor_tensor_scan(
                m[:, :], dec_t[:, :], st[:, :], init, ALU.mult, ALU.add),
              waits=[(v_p, fence_pre)], inc=v_p)
            fence_post = v_p.n
            P("v", lambda e, m=mem_sb[par], ms=ms: e.tensor_copy(
                carry[:, ms:ms + 1], m[:, SGRP - 1:SGRP]),
              waits=[(v_p, fence_post)], inc=v_p)
            lw = [(v_p, fence_post), (s_p, evac_done[(i, "que_evac")])]
            if i >= 2 and evac_done.get((i - 2, "lh_cast"), 0):
                lw.append((g_p, evac_done[(i - 2, "lh_cast")]))
            P("v", lambda e, m=mem_sb[par], q=que_sb[par], o=load_sb[par]: e.scalar_tensor_tensor(
                o[:, :], m[:, :], 0.05, q[:, :], ALU.mult, ALU.mult),
              waits=lw, inc=v_p)
            load_ready = v_p.n
            evac_done[(i, "load")] = load_ready

            if not ABL_NO_LHLL:
              sw = [(v_p, load_ready)]
              if i >= 2:
                sw.append((d_e[par], store_done[i - 2]))
              P("g2", lambda e, o=load_sb[par], h=lh_sb[par]: e.tensor_copy(
                h[:, :], o[:, :]), waits=sw)
              evac_done[(i, "lh_cast")] = g_p.n
              P("g2", lambda e, o=load_sb[par], h=lh_sb[par], l=ll_sb[par]: e.tensor_tensor(
                l[:, :], o[:, :], h[:, :], ALU.subtract))
              ll_mark = g_p.n

              P("g", lambda e, o=lh_sb[par], ms=ms, sg=sg: e.dma_start(
                out=load_hl_scr[sg * 4:(sg + 1) * 4, 0, :, ms * 128:(ms + 1) * 128]
                    .rearrange("c p s -> p c s"),
                in_=o[:, :].rearrange("p (c s) -> p c s", s=128)),
                waits=[(g_p, evac_done[(i, "lh_cast")])], inc=d_e[par], inc_n=16)
              P("g", lambda e, o=ll_sb[par], ms=ms, sg=sg: e.dma_start(
                out=load_hl_scr[sg * 4:(sg + 1) * 4, 1, :, ms * 128:(ms + 1) * 128]
                    .rearrange("c p s -> p c s"),
                in_=o[:, :].rearrange("p (c s) -> p c s", s=128)),
                waits=[(g_p, ll_mark)], inc=d_e[par], inc_n=16)
              store_done[i] = d_e[par].n

            if sg + 1 < NSG and 6 <= ms < 14:
                for j in range(2 * (ms - 6), 2 * (ms - 6) + 2):
                    emit_xsplit_ops(sg + 1, j)
            if i + 1 >= NEC and i + 1 < N_ITER:
                emit_p1_splits(i + 1)

            while pw_state["emitted"] * N_ITER < len(pw_items) * (i + 1):
                emit_pw_step()

        while pw_state["emitted"] < len(pw_items):
            emit_pw_step()

        p1_stack.close()
        p1_psum.close()
        barrier()

        p2_stack = ExitStack()
        wo_s = p2_stack.enter_context(nc.sbuf_tensor("wo_s", [128, 2 * NEC * E], BF16))
        lc = [p2_stack.enter_context(nc.sbuf_tensor(f"lc{i}", [128, 2 * E], BF16)) for i in range(2)]
        xc = p2_stack.enter_context(nc.sbuf_tensor("xc", [128, E], F32))
        ry = p2_stack.enter_context(nc.sbuf_tensor("ry", [128, E], F32))
        yfh = p2_stack.enter_context(nc.sbuf_tensor("yfh", [128, E], BF16))
        yfl = p2_stack.enter_context(nc.sbuf_tensor("yfl", [128, E], BF16))
        garb = p2_stack.enter_context(nc.sbuf_tensor("garb", [128, E], BF16))
        p2_psum = ExitStack()
        p_attn = p2_psum.enter_context(nc.psum_tensor("p_attn", [128, E], F32))
        p_yT = p2_psum.enter_context(nc.psum_tensor("p_yT", [128, E], F32))

        WOL = NEC * E
        for hl in range(2):
            P("y", lambda e, hl=hl: e.dma_start(
                out=wo_s[:, hl * WOL:(hl + 1) * WOL].rearrange("p (j e) -> p j e", e=E),
                in_=wo_hl_scr[:, hl].rearrange("j p e -> p j e")), inc=d_f, inc_n=16)
        wo_mark = d_f.n

        lc_mark = [0, 0]
        xc_mark = [0]
        attn_done = {}
        tr_done = {}
        r_done = {}
        y_done = {}
        yfh_done = {}
        yfl_done = {}
        yf_dma = {}

        def emit_p2_loads(c):
            par = c % 2
            w = []
            if c >= 2:
                w.append((t_p, attn_done[c - 2]))
            P("y", lambda e, c=c, b=lc[par]: e.dma_start(
                out=b[:, :].rearrange("p (hl m) -> p hl m", m=E),
                in_=load_hl_scr[c].rearrange("hl p m -> p hl m")),
              waits=w, inc=d_w[par], inc_n=16)
            lc_mark[par] = d_w[par].n
            w = []
            if c >= 1:
                w.append((v_p, r_done[c - 1]))
            P("y", lambda e, c=c: e.dma_start(
                out=xc[:, :], in_=x_tm[c * 128:(c + 1) * 128, :]),
              waits=w, inc=d_x[0], inc_n=16)
            xc_mark[0] = d_x[0].n

        emit_p2_loads(0)

        def emit_p2_transposes(c):
            w = [(v_p, y_done[c])]
            if c >= 1:
                w += [(s_p, yfh_done[c - 1]), (v_p, yfl_done[c - 1])]

            def tr_run(eng):
                ins = None
                for j in range(NEC):
                    ins = eng.transpose(p_yT[:, j * 128:(j + 1) * 128],
                                        ry[:, j * 128:(j + 1) * 128], ident[:, :])
                return ins
            P("t", tr_run, waits=w, inc=t_p)
            tr_done[c] = t_p.n
            wv_ = [(t_p, tr_done[c])]
            if c >= 1:
                wv_.append((d_e[0], yf_dma.get(c - 1, 0)))
            P("s", lambda e: e.activation(yfh[:, :], p_yT[:, :], AF.Copy),
              waits=wv_, inc=s_p)
            yfh_done[c] = s_p.n
            P("v", lambda e: e.tensor_tensor(yfl[:, :], p_yT[:, :], yfh[:, :], ALU.subtract),
              waits=[(s_p, yfh_done[c])], inc=v_p)
            yfl_done[c] = v_p.n
            P("g", lambda e, c=c: e.dma_start(
                out=yfh_scr.rearrange("j p s -> p j s")[:, :, c * 128:(c + 1) * 128],
                in_=yfh[:, :].rearrange("p (j s) -> p j s", s=128)),
              waits=[(s_p, yfh_done[c])], inc=d_e[0], inc_n=16)
            P("g", lambda e, c=c: e.dma_start(
                out=yfl_scr.rearrange("j p s -> p j s")[:, :, c * 128:(c + 1) * 128],
                in_=yfl[:, :].rearrange("p (j s) -> p j s", s=128)),
              waits=[(v_p, yfl_done[c])], inc=d_e[0], inc_n=16)
            yf_dma[c] = d_e[0].n

        for c in range(NSC):
            par = c % 2
            waits = [(d_w[par], lc_mark[par]), (d_f, wo_mark)]
            if c >= 1:
                waits.append((v_p, r_done[c - 1]))

            def attn_run(eng, b=lc[par]):
                ins = None
                WOL = NEC * E
                for et in range(4):
                    for j in range(NEC):
                        slh = slice(j * E + et * 512, j * E + (et + 1) * 512)
                        sll = slice(WOL + j * E + et * 512, WOL + j * E + (et + 1) * 512)
                        ins = eng.matmul(p_attn[:, et * 512:(et + 1) * 512],
                                         b[:, j * 128:(j + 1) * 128], wo_s[:, slh],
                                         start=(j == 0), stop=False)
                        ins = eng.matmul(p_attn[:, et * 512:(et + 1) * 512],
                                         b[:, E + j * 128:E + (j + 1) * 128], wo_s[:, slh],
                                         start=False, stop=False)
                        ins = eng.matmul(p_attn[:, et * 512:(et + 1) * 512],
                                         b[:, j * 128:(j + 1) * 128], wo_s[:, sll],
                                         start=False, stop=(j == NEC - 1))
                return ins
            P("t", attn_run, waits=waits, inc=t_p)
            attn_done[c] = t_p.n

            if c >= 1:
                emit_p2_transposes(c - 1)

            rw = [(t_p, attn_done[c]), (d_x[0], xc_mark[0])]
            if c >= 1:
                rw.append((t_p, tr_done[c - 1]))
            P("v", lambda e: e.scalar_tensor_tensor(
                ry[:, :], p_attn[:, :], 1.0, xc[:, :], ALU.mult, ALU.add,
                accum_out=stats[:, 0:1]),
              waits=rw, inc=v_p)
            r_done[c] = v_p.n
            if c + 1 < NSC:
                emit_p2_loads(c + 1)
            P("s", lambda e: e.activation(garb[:, :], ry[:, :], AF.Square,
                                          accum_out=stats[:, 1:2]),
              waits=[(v_p, r_done[c])], inc=s_p)
            sq_mark = s_p.n
            P("v", lambda e: e.tensor_scalar_mul(stats[:, 2:4], stats[:, 0:2], 1.0 / E),
              waits=[(s_p, sq_mark)])
            P("v", lambda e: e.tensor_tensor(stats[:, 4:5], stats[:, 2:3], stats[:, 2:3], ALU.mult))
            P("v", lambda e: e.tensor_scalar(stats[:, 4:5], stats[:, 4:5], -1.0, EPS,
                                             ALU.mult, ALU.add))
            P("v", lambda e: e.tensor_tensor(stats[:, 5:6], stats[:, 4:5], stats[:, 3:4], ALU.add))
            P("s", lambda e: e.activation(stats[:, 7:8], stats[:, 5:6], AF.Sqrt),
              waits=[(v_p, v_p.n)])
            sqrt_mark = s_p.n
            P("v", lambda e: e.reciprocal(stats[:, 11:12], stats[:, 7:8]),
              waits=[(s_p, sqrt_mark)])
            P("v", lambda e: e.tensor_scalar(ry[:, :], ry[:, :], stats[:, 2:3], stats[:, 11:12],
                                             ALU.subtract, ALU.mult))
            y_done[c] = v_p.n

        emit_p2_transposes(NSC - 1)
        p2_stack.close()
        p2_psum.close()
        barrier()

        p3_stack = ExitStack()
        yfh_s = [p3_stack.enter_context(nc.sbuf_tensor(f"yfh_s{i}", [128, NEC * SB], BF16)) for i in range(2)]
        yfl_s = [p3_stack.enter_context(nc.sbuf_tensor(f"yfl_s{i}", [128, NEC * SB], BF16)) for i in range(2)]
        hh_h = p3_stack.enter_context(nc.sbuf_tensor("hh_h", [128, NHC * SB], BF16))
        hh_l = p3_stack.enter_context(nc.sbuf_tensor("hh_l", [128, NHC * SB], BF16))
        wgt_s = [p3_stack.enter_context(nc.sbuf_tensor(f"wgt{i}", [128, E], BF16)) for i in range(2)]
        wut_s = [p3_stack.enter_context(nc.sbuf_tensor(f"wut{i}", [128, E], BF16)) for i in range(2)]
        wdt_s = [p3_stack.enter_context(nc.sbuf_tensor(f"wdt{i}", [128, NHC * 128], BF16)) for i in range(2)]
        gsil = [p3_stack.enter_context(nc.sbuf_tensor(f"gsil{i}", [128, SB], F32)) for i in range(2)]
        hhf = [p3_stack.enter_context(nc.sbuf_tensor(f"hhf{i}", [128, SB], F32)) for i in range(2)]
        r2 = p3_stack.enter_context(nc.sbuf_tensor("r2", [128, NEC * SB], F32))
        r2h = p3_stack.enter_context(nc.sbuf_tensor("r2h", [128, NEC * SB], BF16))
        r2l = p3_stack.enter_context(nc.sbuf_tensor("r2l", [128, NEC * SB], BF16))
        r2sq = p3_stack.enter_context(nc.sbuf_tensor("r2sq", [128, NEC * SB], BF16))
        bc = p3_stack.enter_context(nc.sbuf_tensor("bc", [128, 2 * SB], F32))
        p3_psum = ExitStack()
        p_g = [p3_psum.enter_context(nc.psum_tensor(f"p3g{i}", [128, SB], F32)) for i in range(2)]
        p_u = [p3_psum.enter_context(nc.psum_tensor(f"p3u{i}", [128, SB], F32)) for i in range(2)]
        p_f = [p3_psum.enter_context(nc.psum_tensor(f"p3f{i}", [128, SB], F32)) for i in range(2)]
        p_stat = p3_psum.enter_context(nc.psum_tensor("p_stat", [128, 2 * SB], F32))
        p_bc = p3_psum.enter_context(nc.psum_tensor("p_bc", [128, 2 * SB], F32))

        yf_mark = [0]
        wgu_mark = [0, 0]
        wdt_mark = [0, 0]
        gu_done = {}
        silu_done = {}
        hhf_done = {}
        hhh_done = {}
        hhl_done = {}
        down_done = {}
        r2_done = {}
        out_dma = {}
        stat_done = {}

        def emit_ln2(sb):
                split_mark = split_marks_ln2[sb]

                def stat_run(eng):
                    first = True
                    for dc in range(NEC):
                        for part in (r2h, r2l):
                            eng.matmul(p_stat[0:1, 0:SB], onc[:, :],
                                       part[:, dc * SB:(dc + 1) * SB],
                                       start=first, stop=(dc == NEC - 1 and part is r2l))
                            first = False
                    ins = None
                    for dc in range(NEC):
                        ins = eng.matmul(p_stat[0:1, SB:2 * SB], onc[:, :],
                                         r2sq[:, dc * SB:(dc + 1) * SB],
                                         start=(dc == 0), stop=(dc == NEC - 1))
                    return ins
                P("t", stat_run, waits=[(v_p, split_mark)], inc=t_p)
                stat_done[sb] = t_p.n

                P("v", lambda e: e.tensor_scalar_mul(srow[0:1, 0:2 * SB], p_stat[0:1, 0:2 * SB], 1.0 / E),
                  waits=[(t_p, stat_done[sb])])
                P("v", lambda e: e.tensor_tensor(srow[0:1, 2 * SB:3 * SB], srow[0:1, 0:SB], srow[0:1, 0:SB], ALU.mult))
                P("v", lambda e: e.tensor_scalar(srow[0:1, 2 * SB:3 * SB], srow[0:1, 2 * SB:3 * SB],
                                                 -1.0, EPS, ALU.mult, ALU.add))
                P("v", lambda e: e.tensor_tensor(srow[0:1, SB:2 * SB], srow[0:1, 2 * SB:3 * SB],
                                                 srow[0:1, SB:2 * SB], ALU.add))
                P("s", lambda e: e.activation(srow[0:1, 2 * SB:3 * SB], srow[0:1, SB:2 * SB], AF.Sqrt),
                  waits=[(v_p, v_p.n)])
                P("v", lambda e: e.reciprocal(srow[0:1, SB:2 * SB], srow[0:1, 2 * SB:3 * SB]),
                  waits=[(s_p, s_p.n)])
                P("v", lambda e: e.tensor_copy(srow_bf[0:1, 0:2 * SB], srow[0:1, 0:2 * SB]))
                P("v", lambda e: e.tensor_tensor(srow[0:1, 2 * SB:4 * SB], srow[0:1, 0:2 * SB],
                                                 srow_bf[0:1, 0:2 * SB], ALU.subtract))
                P("v", lambda e: e.tensor_copy(srow_bf[0:1, 2 * SB:4 * SB], srow[0:1, 2 * SB:4 * SB]))
                rows_mark = v_p.n

                def bcast_run(eng):
                    eng.matmul(p_bc[:, 0:SB], onr[0:1, :], srow_bf[0:1, 0:SB], start=True, stop=False)
                    eng.matmul(p_bc[:, 0:SB], onr[0:1, :], srow_bf[0:1, 2 * SB:3 * SB], start=False, stop=True)
                    eng.matmul(p_bc[:, SB:2 * SB], onr[0:1, :], srow_bf[0:1, SB:2 * SB], start=True, stop=False)
                    return eng.matmul(p_bc[:, SB:2 * SB], onr[0:1, :], srow_bf[0:1, 3 * SB:4 * SB], start=False, stop=True)
                P("t", bcast_run, waits=[(v_p, rows_mark)], inc=t_p)
                bc_mark = t_p.n

                P("v", lambda e: e.tensor_copy(bc[:, :], p_bc[:, :]),
                  waits=[(t_p, bc_mark)], inc=v_p)
                bc_ev = v_p.n

                for dc in range(NEC):
                    P("v", lambda e, dc=dc: e.tensor_tensor(
                        r2[:, dc * SB:(dc + 1) * SB], r2[:, dc * SB:(dc + 1) * SB],
                        bc[:, 0:SB], ALU.subtract))
                    P("v", lambda e, dc=dc: e.tensor_tensor(
                        r2[:, dc * SB:(dc + 1) * SB], r2[:, dc * SB:(dc + 1) * SB],
                        bc[:, SB:2 * SB], ALU.mult))
                norm_mark = v_p.n

                P("g", lambda e, sb=sb: e.dma_start(
                    out=out_fm.rearrange("(j p) s -> p j s", p=128)[:, :, sb * SB:(sb + 1) * SB],
                    in_=r2[:, :].rearrange("p (j s) -> p j s", s=SB)),
                  waits=[(v_p, norm_mark)], inc=d_e[1], inc_n=16)
                out_dma[sb] = d_e[1].n

        pending_ln2 = []
        split_marks_ln2 = {}

        yf_marks = {}
        wgu_marks = {}

        def emit_yf_loads(sb):
            sp3 = sb % 2
            w = []
            if sb >= 2:
                w.append((t_p, gu_done[(sb - 2, NHC - 1, "u")]))
                w.append((v_p, r2_done[(sb - 2, NEC - 1)]))
            for scr, dst in ((yfh_scr, yfh_s[sp3]), (yfl_scr, yfl_s[sp3])):
                P("y", lambda e, scr=scr, dst=dst, sb=sb: e.dma_start(
                    out=dst[:, :].rearrange("p (j s) -> p j s", s=SB),
                    in_=scr.rearrange("j p s -> p j s")[:, :, sb * SB:(sb + 1) * SB]),
                  waits=w, inc=d_x[sp3], inc_n=16)
                w = []
            yf_marks[sb] = d_x[sp3].n

        def emit_wgu_load(sb, hc):
            par = hc % 2
            w = []
            if hc >= 2:
                w.append((t_p, gu_done[(sb, hc - 2, "u")]))
            elif sb >= 1:
                w.append((t_p, gu_done[(sb - 1, NHC - 2 + hc, "u")]))
            for scr, buf in ((wgT_sw, wgt_s[par]), (wuT_sw, wut_s[par])):
                P("y", lambda e, scr=scr, b=buf, hc=hc: e.dma_start(
                    out=b[:, :], in_=scr[hc]),
                  waits=w, inc=d_w[par], inc_n=16)
                w = []
            wgu_marks[(sb, hc)] = d_w[par].n

        for sb in range(NSB):
            sp3 = sb % 2
            if sb not in yf_marks:
                emit_yf_loads(sb)

            for hc in range(NHC):
                par = hc % 2
                if (sb, hc) not in wgu_marks:
                    emit_wgu_load(sb, hc)

                for proj, wbuf, ps in (("g", wgt_s, p_g), ("u", wut_s, p_u)):
                    waits = [(d_w[par], wgu_marks[(sb, hc)]), (d_x[sp3], yf_marks[sb])]
                    if hc >= 2:
                        ctr, key = ((s_p, "silu") if proj == "g" else (v_p, "hhf"))
                        mark = silu_done[(sb, hc - 2)] if proj == "g" else hhf_done[(sb, hc - 2)]
                        waits.append((ctr, mark))

                    def gu_run(eng, wbuf=wbuf[par], ps=ps[par],
                               yfh=yfh_s[sp3], yfl=yfl_s[sp3]):
                        first = True
                        ins = None
                        for j in range(NEC):
                            for yb in (yfh, yfl):
                                ins = eng.matmul(ps[:, :], wbuf[:, j * 128:(j + 1) * 128],
                                                 yb[:, j * SB:(j + 1) * SB],
                                                 start=first, stop=(j == NEC - 1 and yb is yfl))
                                first = False
                        return ins
                    P("t", gu_run, waits=waits, inc=t_p)
                    gu_done[(sb, hc, proj)] = t_p.n

                P("s", lambda e, ps=p_g[par], o=gsil[par], hc=hc: e.activation(
                    o[:, :], ps[:, :], AF.Silu, scale=sg_t[:, hc:hc + 1]),
                  waits=[(t_p, gu_done[(sb, hc, "g")])], inc=s_p)
                silu_done[(sb, hc)] = s_p.n
                P("v", lambda e, ps=p_u[par], g=gsil[par], o=hhf[par], hc=hc: e.scalar_tensor_tensor(
                    o[:, :], ps[:, :], su_t[:, hc:hc + 1], g[:, :], ALU.mult, ALU.mult),
                  waits=[(t_p, gu_done[(sb, hc, "u")]), (s_p, silu_done[(sb, hc)])],
                  inc=v_p)
                hhf_done[(sb, hc)] = v_p.n
                w_extra = []
                if sb >= 1 and hc == 0:
                    w_extra = [(t_p, down_done[(sb - 1, 15)])]
                P("s", lambda e, src=hhf[par], hc=hc: e.activation(
                    hh_h[:, hc * SB:(hc + 1) * SB], src[:, :], AF.Copy),
                  waits=[(v_p, hhf_done[(sb, hc)])] + w_extra, inc=s_p)
                hhh_done[(sb, hc)] = s_p.n
                P("v", lambda e, src=hhf[par], hc=hc: e.tensor_tensor(
                    hh_l[:, hc * SB:(hc + 1) * SB], src[:, :],
                    hh_h[:, hc * SB:(hc + 1) * SB], ALU.subtract),
                  waits=[(s_p, hhh_done[(sb, hc)])], inc=v_p)
                hhl_done[(sb, hc)] = v_p.n
                if hc == 3 and pending_ln2:
                    emit_ln2(pending_ln2.pop(0))

            if sb + 1 < NSB:
                emit_yf_loads(sb + 1)
                emit_wgu_load(sb + 1, 0)
                emit_wgu_load(sb + 1, 1)
            for dc in range(NEC):
                par = dc % 2
                w = []
                if dc >= 2:
                    w.append((t_p, down_done[(sb, dc - 2)]))
                elif sb >= 1:
                    w.append((t_p, down_done[(sb - 1, dc + NEC - 2)]))
                dch = d_st[par]
                P("y", lambda e, b=wdt_s[par], dc=dc: e.dma_start(
                    out=b[:, :], in_=wdT_sw[dc]),
                  waits=w, inc=dch, inc_n=16)
                wdt_mark[par] = dch.n

                waits = [(dch, wdt_mark[par]),
                         (v_p, hhl_done[(sb, NHC - 1)])]
                if dc >= 2:
                    waits.append((v_p, r2_done[(sb, dc - 2)]))

                def down_run(eng, b=wdt_s[par], ps=p_f[par]):
                    first = True
                    ins = None
                    for j in range(NHC):
                        for hb in (hh_h, hh_l):
                            ins = eng.matmul(ps[:, :], b[:, j * 128:(j + 1) * 128],
                                             hb[:, j * SB:(j + 1) * SB],
                                             start=first, stop=(j == NHC - 1 and hb is hh_l))
                            first = False
                    return ins
                P("t", down_run, waits=waits, inc=t_p)
                down_done[(sb, dc)] = t_p.n

                w = [(t_p, down_done[(sb, dc)])]
                if sb >= 1 and dc == 0:
                    w.append((d_e[1], out_dma[sb - 1]))
                P("v", lambda e, ps=p_f[par], dc=dc, yy=yfh_s[sp3]: e.scalar_tensor_tensor(
                    r2[:, dc * SB:(dc + 1) * SB], ps[:, :], sd_t[:, dc:dc + 1],
                    yy[:, dc * SB:(dc + 1) * SB], ALU.mult, ALU.add),
                  waits=w, inc=v_p)
                P("v", lambda e, dc=dc, yy=yfl_s[sp3]: e.tensor_tensor(
                    r2[:, dc * SB:(dc + 1) * SB], r2[:, dc * SB:(dc + 1) * SB],
                    yy[:, dc * SB:(dc + 1) * SB], ALU.add),
                  inc=v_p)
                r2_done[(sb, dc)] = v_p.n
                P("v", lambda e, dc=dc: e.tensor_copy(
                    r2h[:, dc * SB:(dc + 1) * SB], r2[:, dc * SB:(dc + 1) * SB]))
                P("v", lambda e, dc=dc: e.tensor_tensor(
                    r2l[:, dc * SB:(dc + 1) * SB], r2[:, dc * SB:(dc + 1) * SB],
                    r2h[:, dc * SB:(dc + 1) * SB], ALU.subtract))
                P("v", lambda e, dc=dc: e.tensor_tensor(
                    r2sq[:, dc * SB:(dc + 1) * SB], r2[:, dc * SB:(dc + 1) * SB],
                    r2[:, dc * SB:(dc + 1) * SB], ALU.mult))
                if dc == NEC - 1:
                    split_marks_ln2[sb] = v_p.n

            pending_ln2.append(sb)


        while pending_ln2:
            emit_ln2(pending_ln2.pop(0))

        p3_stack.close()
        p3_psum.close()

        final_out = d_e[1].n
        prog.streams["g"].append(lambda e, v=final_out: e.wait_ge(d_e[1].h, v))

        block = g_stack.enter_context(nc.Block())

        @block.tensor
        def _(eng):
            for f in prog.streams["t"]:
                f(eng)

        @block.vector
        def _(eng):
            for f in prog.streams["v"]:
                f(eng)

        @block.scalar
        def _(eng):
            for f in prog.streams["s"]:
                f(eng)

        @block.sync
        def _(eng):
            for f in prog.streams["y"]:
                f(eng)

        @block.gpsimd
        def _(eng):
            for f in prog.streams["g"]:
                f(eng)

    return nc


_NC_CACHE = None


def kernel(L0_attn_x, L0_attn_Wv, L0_attn_Wg, L0_attn_bg, L0_attn_Wq, L0_attn_bq,
           L0_attn_Wo, L0_ffn_wq_gate, L0_ffn_s_gate, L0_ffn_wq_up, L0_ffn_s_up,
           L0_ffn_wq_down, L0_ffn_s_down):
    global _NC_CACHE
    x = np.asarray(L0_attn_x, dtype=np.float32)
    B = x.shape[0]
    assert x.shape == (8, S, E)

    shared = dict(
        wg=np.ascontiguousarray(np.asarray(L0_attn_Wg, np.float32)),
        wv=np.ascontiguousarray(np.asarray(L0_attn_Wv, np.float32)),
        wq=np.ascontiguousarray(np.asarray(L0_attn_Wq, np.float32)),
        wo=np.ascontiguousarray(np.asarray(L0_attn_Wo, np.float32)),
        bg2d=np.ascontiguousarray(np.asarray(L0_attn_bg, np.float32).reshape(NEC, 128).T),
        bq2d=np.ascontiguousarray(np.asarray(L0_attn_bq, np.float32).reshape(NEC, 128).T),
        wgT_i=np.ascontiguousarray(np.asarray(L0_ffn_wq_gate, np.int32).T),
        wuT_i=np.ascontiguousarray(np.asarray(L0_ffn_wq_up, np.int32).T),
        wdT_i=np.ascontiguousarray(np.asarray(L0_ffn_wq_down, np.int32).T),
        sg2d=np.ascontiguousarray(np.asarray(L0_ffn_s_gate, np.float32).reshape(NHC, 128).T),
        su2d=np.ascontiguousarray(np.asarray(L0_ffn_s_up, np.float32).reshape(NHC, 128).T),
        sd2d=np.ascontiguousarray(np.asarray(L0_ffn_s_down, np.float32).reshape(NEC, 128).T),
        dec_i=np.full((128, SGRP), np.float32(DECAY), dtype=np.float32),
        ident_i=np.eye(128, dtype=np.float32),
        onc_i=np.ones((128, 1), dtype=ml_dtypes.bfloat16),
        onr_i=np.ones((1, 128), dtype=ml_dtypes.bfloat16),
    )
    in_maps = []
    for b in range(B):
        m = dict(shared)
        m["x_tm"] = np.ascontiguousarray(x[b])
        m["xT"] = np.ascontiguousarray(x[b].T)
        in_maps.append(m)

    if _NC_CACHE is None:
        _NC_CACHE = build_nc()
    res = run_bass_kernel_spmd(_NC_CACHE, in_maps, core_ids=list(range(8)))
    out = np.stack([np.asarray(r["out_fm"]).T for r in res.results], axis=0)
    return out


# version 13
# speedup vs baseline: 1.1201x; 1.0525x over previous
import numpy as np
import ml_dtypes
import concourse.bass as bass
import concourse.mybir as mybir
from concourse.bass_utils import run_bass_kernel_spmd
from contextlib import ExitStack

F32 = mybir.dt.float32
BF16 = mybir.dt.bfloat16
I32 = mybir.dt.int32
AF = mybir.ActivationFunctionType
ALU = mybir.AluOpType

import os
ABL_NO_PW = os.environ.get("ABL_NO_PW") == "1"
ABL_NO_LHLL = os.environ.get("ABL_NO_LHLL") == "1"
ABL_NO_SPREAD = os.environ.get("ABL_NO_SPREAD") == "1"

E = 2048
S = 2048
H = 5632
NEC = E // 128
NHC = H // 128
NSC = S // 128
SGRP = 512
NSG = S // SGRP
SB = 256
NSB = S // SB
EPS = 1e-5
DECAY = 0.95
SCALE2 = float(np.float32(np.sqrt(1.0 - DECAY))) ** 2


class Ctr:
    def __init__(self, handle):
        self.h = handle
        self.n = 0


class Prog:
    def __init__(self):
        self.streams = {k: [] for k in ("t", "v", "s", "y", "g", "g2")}
        self.auto = {}

    def emit(self, key, build, waits=(), inc=None, inc_n=1):
        waits = list(waits)
        if key in self.auto:
            ctr = self.auto[key]
            waits = [(ctr, ctr.n)] + waits
            if inc is None:
                inc = ctr
        if key == "g2":
            key = "g"
        w = [(c.h, v) for (c, v) in waits if v > 0]
        tgt = None
        if inc is not None:
            inc.n += inc_n
            tgt = (inc.h, inc_n)

        def run(eng, w=w, build=build, tgt=tgt):
            for h, v in w:
                eng.wait_ge(h, v)
            ins = build(eng)
            if tgt is not None:
                if ins is None:
                    ins = eng.nop()
                ins.then_inc(*tgt)
        self.streams[key].append(run)


def build_nc():
    nc = bass.Bass("TRN2", target_bir_lowering=False)

    xT = nc.dram_tensor("xT", [E, S], F32, kind="ExternalInput").ap()
    x_tm = nc.dram_tensor("x_tm", [S, E], F32, kind="ExternalInput").ap()
    wg = nc.dram_tensor("wg", [E, E], F32, kind="ExternalInput").ap()
    wv = nc.dram_tensor("wv", [E, E], F32, kind="ExternalInput").ap()
    wq = nc.dram_tensor("wq", [E, E], F32, kind="ExternalInput").ap()
    wo = nc.dram_tensor("wo", [E, E], F32, kind="ExternalInput").ap()
    bg2d = nc.dram_tensor("bg2d", [128, NEC], F32, kind="ExternalInput").ap()
    bq2d = nc.dram_tensor("bq2d", [128, NEC], F32, kind="ExternalInput").ap()
    wgT_i = nc.dram_tensor("wgT_i", [E, H], I32, kind="ExternalInput").ap()
    wuT_i = nc.dram_tensor("wuT_i", [E, H], I32, kind="ExternalInput").ap()
    wdT_i = nc.dram_tensor("wdT_i", [H, E], I32, kind="ExternalInput").ap()
    sg2d = nc.dram_tensor("sg2d", [128, NHC], F32, kind="ExternalInput").ap()
    su2d = nc.dram_tensor("su2d", [128, NHC], F32, kind="ExternalInput").ap()
    sd2d = nc.dram_tensor("sd2d", [128, NEC], F32, kind="ExternalInput").ap()
    dec_i = nc.dram_tensor("dec_i", [128, SGRP], F32, kind="ExternalInput").ap()
    ident_i = nc.dram_tensor("ident_i", [128, 128], F32, kind="ExternalInput").ap()
    onc_i = nc.dram_tensor("onc_i", [128, 1], BF16, kind="ExternalInput").ap()
    onr_i = nc.dram_tensor("onr_i", [1, 128], BF16, kind="ExternalInput").ap()

    out_fm = nc.dram_tensor("out_fm", [E, S], F32, kind="ExternalOutput").ap()
    load_hl_scr = nc.dram_tensor("load_hl_scr", [NSC, 2, 128, E], BF16).ap()
    wg_hl_scr = nc.dram_tensor("wg_hl_scr", [NEC, 2, 128, E], BF16).ap()
    wv_hl_scr = nc.dram_tensor("wv_hl_scr", [NEC, 2, 128, E], BF16).ap()
    wq_hl_scr = nc.dram_tensor("wq_hl_scr", [NEC, 2, 128, E], BF16).ap()
    wo_hl_scr = nc.dram_tensor("wo_hl_scr", [NEC, 2, 128, E], BF16).ap()
    yfh_scr = nc.dram_tensor("yfh_scr", [NEC, 128, S], BF16).ap()
    yfl_scr = nc.dram_tensor("yfl_scr", [NEC, 128, S], BF16).ap()
    wgT_sw = nc.dram_tensor("wgT_sw", [NHC, 128, E], BF16).ap()
    wuT_sw = nc.dram_tensor("wuT_sw", [NHC, 128, E], BF16).ap()
    wdT_sw = nc.dram_tensor("wdT_sw", [NEC, 128, NHC * 128], BF16).ap()

    prog = Prog()
    P = prog.emit

    with ExitStack() as g_stack:
        ge = g_stack.enter_context
        d_f = Ctr(ge(nc.semaphore("d_f")))
        d_w = [Ctr(ge(nc.semaphore(f"d_w{i}"))) for i in range(2)]
        d_x = [Ctr(ge(nc.semaphore(f"d_x{i}"))) for i in range(2)]
        d_c = Ctr(ge(nc.semaphore("d_c")))
        d_d = Ctr(ge(nc.semaphore("d_d")))
        d_d2 = Ctr(ge(nc.semaphore("d_d2")))
        d_e = [Ctr(ge(nc.semaphore(f"d_e{i}"))) for i in range(2)]
        d_as = [Ctr(ge(nc.semaphore(f"d_as{i}"))) for i in range(2)]
        d_st = [Ctr(ge(nc.semaphore(f"d_st{i}"))) for i in range(3)]
        d_pw = [Ctr(ge(nc.semaphore(f"d_pw{i}"))) for i in range(4)]
        d_wo = [Ctr(ge(nc.semaphore(f"d_wo{i}"))) for i in range(2)]
        t_p = Ctr(ge(nc.semaphore("t_p")))
        v_p = Ctr(ge(nc.semaphore("v_p")))
        s_p = Ctr(ge(nc.semaphore("s_p")))
        g_p = Ctr(ge(nc.semaphore("g_p")))
        prog.auto = {"v": v_p, "s": s_p, "g2": g_p}
        ALL_DMA = [d_f, d_w[0], d_w[1], d_x[0], d_x[1], d_c, d_d, d_d2, d_e[0], d_e[1],
                   d_as[0], d_as[1], d_st[0], d_st[1], d_st[2],
                   d_pw[0], d_pw[1], d_pw[2], d_pw[3], d_wo[0], d_wo[1]]

        def barrier():
            for key, ctr in (("t", t_p), ("v", v_p), ("s", s_p)):
                P(key, lambda e: e.nop(), waits=[(ctr, ctr.n)], inc=ctr)
            marks = [(t_p, t_p.n), (v_p, v_p.n), (s_p, s_p.n)] + \
                    [(c, c.n) for c in ALL_DMA]
            for key in ("t", "v", "s", "y", "g"):
                def run(eng, marks=tuple(marks)):
                    for c, v in marks:
                        if v > 0:
                            eng.wait_ge(c.h, v)
                prog.streams[key].append(run)

        bg_t = ge(nc.sbuf_tensor("bg_t", [128, NEC], F32))
        bq_t = ge(nc.sbuf_tensor("bq_t", [128, NEC], F32))
        sg_t = ge(nc.sbuf_tensor("sg_t", [128, NHC], F32))
        su_t = ge(nc.sbuf_tensor("su_t", [128, NHC], F32))
        sd_t = ge(nc.sbuf_tensor("sd_t", [128, NEC], F32))
        dec_t = ge(nc.sbuf_tensor("dec_t", [128, SGRP], F32))
        ident = ge(nc.sbuf_tensor("ident", [128, 128], F32))
        onc = ge(nc.sbuf_tensor("onc", [128, 1], BF16))
        onr = ge(nc.sbuf_tensor("onr", [1, 128], BF16))
        carry = ge(nc.sbuf_tensor("carry", [128, NEC], F32))
        stats = ge(nc.sbuf_tensor("stats", [128, 16], F32))
        srow = ge(nc.sbuf_tensor("srow", [1, 4 * SB], F32))
        srow_bf = ge(nc.sbuf_tensor("srow_bf", [1, 4 * SB], BF16))

        for dst, src in ((bg_t, bg2d), (bq_t, bq2d), (sg_t, sg2d), (su_t, su2d),
                         (sd_t, sd2d), (dec_t, dec_i), (ident, ident_i),
                         (onc, onc_i), (onr, onr_i)):
            P("y", lambda e, d=dst, s=src: e.dma_start(out=d[:, :], in_=s),
              inc=d_f, inc_n=16)
        glob_ld = d_f.n

        p1_stack = ExitStack()
        xgh = [p1_stack.enter_context(nc.sbuf_tensor(f"xgh{i}", [128, NEC * SGRP], BF16)) for i in range(2)]
        xgl = [p1_stack.enter_context(nc.sbuf_tensor(f"xgl{i}", [128, NEC * SGRP], BF16)) for i in range(2)]
        xstg = [p1_stack.enter_context(nc.sbuf_tensor(f"xstg{i}", [128, SGRP], F32)) for i in range(2)]
        wgs = [p1_stack.enter_context(nc.sbuf_tensor(f"wgs{i}", [128, 2 * E], BF16)) for i in range(2)]
        wvs = [p1_stack.enter_context(nc.sbuf_tensor(f"wvs{i}", [128, 2 * E], BF16)) for i in range(2)]
        wqs = [p1_stack.enter_context(nc.sbuf_tensor(f"wqs{i}", [128, 2 * E], BF16)) for i in range(2)]
        wtmp = [p1_stack.enter_context(nc.sbuf_tensor(f"wtmp{i}", [128, E], F32)) for i in range(3)]
        gate_sb = [p1_stack.enter_context(nc.sbuf_tensor(f"gate{i}", [128, SGRP], F32)) for i in range(2)]
        que_sb = [p1_stack.enter_context(nc.sbuf_tensor(f"que{i}", [128, SGRP], F32)) for i in range(2)]
        store_sb = [p1_stack.enter_context(nc.sbuf_tensor(f"store{i}", [128, SGRP], F32)) for i in range(2)]
        mem_sb = [p1_stack.enter_context(nc.sbuf_tensor(f"mem{i}", [128, SGRP], F32)) for i in range(2)]
        load_sb = [p1_stack.enter_context(nc.sbuf_tensor(f"load{i}", [128, SGRP], F32)) for i in range(2)]
        lh_sb = [p1_stack.enter_context(nc.sbuf_tensor(f"lh{i}", [128, SGRP], BF16)) for i in range(2)]
        ll_sb = [p1_stack.enter_context(nc.sbuf_tensor(f"ll{i}", [128, SGRP], BF16)) for i in range(2)]
        PWW = 1408
        wint = [p1_stack.enter_context(nc.sbuf_tensor(f"wint{i}", [128, PWW], I32)) for i in range(3)]
        wbf = [p1_stack.enter_context(nc.sbuf_tensor(f"wbf{i}", [128, PWW], BF16)) for i in range(2)]
        wof = p1_stack.enter_context(nc.sbuf_tensor("wof", [128, 1024], F32))
        woh_sl = p1_stack.enter_context(nc.sbuf_tensor("woh_sl", [128, 1024], BF16))
        wol_sl = p1_stack.enter_context(nc.sbuf_tensor("wol_sl", [128, 1024], BF16))

        p1_psum = ExitStack()
        p_gate = [p1_psum.enter_context(nc.psum_tensor(f"pg{i}", [128, SGRP], F32)) for i in range(2)]
        p_v = [p1_psum.enter_context(nc.psum_tensor(f"pv{i}", [128, SGRP], F32)) for i in range(2)]
        p_que = [p1_psum.enter_context(nc.psum_tensor(f"pq{i}", [128, SGRP], F32)) for i in range(2)]

        pw_items = []
        WG_W = 1408
        for k in range(NEC):
            for h2 in range(4):
                dst = wgT_sw[h2 * 11:(h2 + 1) * 11, :, k * 128:(k + 1) * 128] \
                    .rearrange("hc p m -> p hc m")
                pw_items.append(("ffn", wgT_i[k * 128:(k + 1) * 128, h2 * WG_W:(h2 + 1) * WG_W],
                                 dst, WG_W))
        for k in range(NEC):
            for h2 in range(4):
                dst = wuT_sw[h2 * 11:(h2 + 1) * 11, :, k * 128:(k + 1) * 128] \
                    .rearrange("hc p m -> p hc m")
                pw_items.append(("ffn", wuT_i[k * 128:(k + 1) * 128, h2 * WG_W:(h2 + 1) * WG_W],
                                 dst, WG_W))
        WD_W = 1024
        for k in range(NHC):
            for h2 in range(2):
                dst = wdT_sw[h2 * 8:(h2 + 1) * 8, :, k * 128:(k + 1) * 128] \
                    .rearrange("dc p m -> p dc m")
                pw_items.append(("ffn", wdT_i[k * 128:(k + 1) * 128, h2 * WD_W:(h2 + 1) * WD_W],
                                 dst, WD_W))
        wo_items = [("wo", wo[k * 128:(k + 1) * 128, h2 * 1024:(h2 + 1) * 1024],
                     (wo_hl_scr[k, 0, :, h2 * 1024:(h2 + 1) * 1024],
                      wo_hl_scr[k, 1, :, h2 * 1024:(h2 + 1) * 1024]), 1024)
                    for k in range(NEC) for h2 in range(2)]
        for k in range(2 * NEC):
            pw_items.insert(60 + k * 5, wo_items[k])
        if ABL_NO_PW:
            pw_items = []
        pw_state = {"emitted": 0, "in": [], "conv": [], "outs": []}

        def emit_pw_step():
            k = pw_state["emitted"]
            if k >= len(pw_items):
                return
            pw_state["emitted"] += 1
            kind, src_, dst, width = pw_items[k]
            if kind == "ffn":
                fk = pw_state.setdefault("fk", 0)
                pw_state["fk"] = fk + 1
                p4 = fk % 3
                p2 = fk % 2
                ch_in = d_pw[p4]
                w_in = []
                cm = pw_state.get(("fconv", p4))
                if cm is not None:
                    w_in.append((s_p, cm))
                P("y", lambda e, s=src_, wd=width, b=wint[p4]: e.dma_start(
                    out=b[:, :wd], in_=s), waits=w_in, inc=ch_in, inc_n=16)
                in_mark = ch_in.n
                w_cv = [(ch_in, in_mark)]
                om = pw_state.get(("fout", p2))
                if om is not None:
                    w_cv.append(((d_d if p2 == 0 else d_d2), om))
                P("s", lambda e, wd=width, b=wint[p4], o=wbf[p2]: e.activation(
                    o[:, :wd], b[:, :wd], AF.Copy, bias=-128.0),
                  waits=w_cv, inc=s_p)
                pw_state[("fconv", p4)] = s_p.n
                ch_out = d_d if p2 == 0 else d_d2
                P("g", lambda e, d=dst, wd=width, o=wbf[p2]: e.dma_start(
                    out=d, in_=o[:, :wd].rearrange("p (c m) -> p c m", m=128)),
                  waits=[(s_p, s_p.n)], inc=ch_out, inc_n=16)
                pw_state[("fout", p2)] = ch_out.n
            else:
                w_in = []
                if pw_state.get("wo_sub") is not None:
                    w_in.append((v_p, pw_state["wo_sub"]))
                P("g", lambda e, s=src_: e.dma_start(out=wof[:, :], in_=s),
                  waits=w_in, inc=d_wo[0], inc_n=16)
                in_mark = d_wo[0].n
                w_cv = [(d_wo[0], in_mark)]
                if pw_state.get("wo_out") is not None:
                    w_cv.append((d_wo[1], pw_state["wo_out"]))
                P("s", lambda e: e.activation(woh_sl[:, :], wof[:, :], AF.Copy),
                  waits=w_cv, inc=s_p)
                hm = s_p.n
                P("v", lambda e: e.tensor_tensor(wol_sl[:, :], wof[:, :], woh_sl[:, :], ALU.subtract),
                  waits=[(s_p, hm)])
                lm = v_p.n
                pw_state["wo_sub"] = lm
                P("g", lambda e, d=dst[0]: e.dma_start(out=d, in_=woh_sl[:, :]),
                  waits=[(s_p, hm)], inc=d_wo[1], inc_n=16)
                P("g", lambda e, d=dst[1]: e.dma_start(out=d, in_=wol_sl[:, :]),
                  waits=[(v_p, lm)], inc=d_wo[1], inc_n=16)
                pw_state["wo_out"] = d_wo[1].n

        def w_slice_view(w, ms):
            return w.rearrange("(j p) m -> p j m", p=128)[:, :, ms * 128:(ms + 1) * 128]

        xg_mark = [0, 0]
        xs_mark = {}
        w_mark = [0, 0]
        w_split = {}
        proj_done = {}
        evac_done = {}
        store_done = {}
        N_ITER = NSG * NEC
        AS_FINAL = 6 * (NEC // 2) * 16

        def emit_xsplit_dma(sg, j):
            par = j % 2
            key = (sg, j - 2) if j >= 2 else ((sg - 1, 14 + j) if sg >= 1 else None)
            w = [(v_p, sub_marks[key])] if key in sub_marks else []
            P("y", lambda e, j=j, sg=sg, stg=xstg[par]: e.dma_start(
                out=stg[:, :],
                in_=xT[j * 128:(j + 1) * 128, sg * SGRP:(sg + 1) * SGRP]),
              waits=w, inc=d_x[par], inc_n=16)
            xdma_marks[(sg, j)] = d_x[par].n

        def emit_xsplit_ops(sg, j):
            par = j % 2
            b = sg % 2
            guard = []
            if sg >= 2 and j == 0:
                guard = [(t_p, proj_done[((sg - 2) * NEC + NEC - 1, "q")])]
            dm = xdma_marks[(sg, j)]
            P("s", lambda e, j=j, stg=xstg[par], o=xgh[b]: e.activation(
                o[:, j * SGRP:(j + 1) * SGRP], stg[:, :], AF.Copy),
              waits=[(d_x[par], dm)] + guard)
            cast_m = s_p.n
            P("v", lambda e, j=j, stg=xstg[par], h=xgh[b], o=xgl[b]: e.tensor_tensor(
                o[:, j * SGRP:(j + 1) * SGRP], stg[:, :],
                h[:, j * SGRP:(j + 1) * SGRP], ALU.subtract),
              waits=[(s_p, cast_m)])
            sub_marks[(sg, j)] = v_p.n
            if j == NEC - 1:
                xs_mark[sg] = (cast_m, v_p.n)

        def emit_xsplit_chunk(sg, j):
            emit_xsplit_dma(sg, j)
            emit_xsplit_ops(sg, j)
        xdma_marks = {}

        sub_marks = {}
        wtmp_state = {}
        as_mark = [0, 0]

        def emit_p1_dmas(i):
            sg, ms = divmod(i, NEC)
            par = i % 2
            if sg == 0:
                marks = []
                for k, wt in enumerate((wg, wv, wq)):
                    wi = wtmp_state.get("idx", 0)
                    wtmp_state["idx"] = wi + 1
                    tb = wtmp[wi % 3]
                    w = []
                    prev = wtmp_state.get(("sub", wi % 3))
                    if prev is not None:
                        w.append((v_p, prev))
                    if i >= 2 and k == 0:
                        w.append((t_p, proj_done[(i - 2, "q")]))
                        w.append((d_as[par], as_mark[par]))
                    ch = d_st[wi % 3]
                    P("y", lambda e, wt=wt, ms=ms, tb=tb: e.dma_start(
                        out=tb[:, :].rearrange("p (j m) -> p j m", m=128),
                        in_=w_slice_view(wt, ms)), waits=w, inc=ch, inc_n=16)
                    marks.append((wi, ch.n))
                wtmp_state[("dmas", i)] = marks
            else:
                w = [(d_as[0], AS_FINAL), (d_as[1], AS_FINAL)] if sg == 1 else []
                if i >= 2:
                    w.append((t_p, proj_done[(i - 2, "q")]))
                for scr, buf in ((wg_hl_scr, wgs[par]), (wv_hl_scr, wvs[par]),
                                 (wq_hl_scr, wqs[par])):
                    P("y", lambda e, scr=scr, ms=ms, b=buf: e.dma_start(
                        out=b[:, :].rearrange("p (hl m) -> p hl m", m=E),
                        in_=scr[ms].rearrange("hl p m -> p hl m")),
                      waits=w, inc=d_w[par], inc_n=16)
                    w = []
            w_mark[par] = d_w[par].n

        def emit_p1_splits(i):
            sg, ms = divmod(i, NEC)
            par = i % 2
            if sg != 0:
                return
            marks = wtmp_state.pop(("dmas", i))
            for (wi, dm), (proj, buf, scr) in zip(marks, (
                    ("g", wgs[par], wg_hl_scr), ("v", wvs[par], wv_hl_scr),
                    ("q", wqs[par], wq_hl_scr))):
                ch = d_st[wi % 3]
                tb = wtmp[wi % 3]
                P("s", lambda e, buf=buf, tb=tb: e.activation(buf[:, :E], tb[:, :], AF.Copy),
                  waits=[(ch, dm)])
                hm = s_p.n
                P("v", lambda e, buf=buf, tb=tb: e.tensor_tensor(
                    buf[:, E:2 * E], tb[:, :], buf[:, :E], ALU.subtract),
                  waits=[(s_p, hm)])
                lm = v_p.n
                wtmp_state[("sub", wi % 3)] = lm
                w_split[(i, proj)] = (hm, lm)
                P("g", lambda e, scr=scr, ms=ms, buf=buf: e.dma_start(
                    out=scr[ms, 0], in_=buf[:, :E]),
                  waits=[(s_p, hm)], inc=d_as[par], inc_n=16)
                P("g", lambda e, scr=scr, ms=ms, buf=buf: e.dma_start(
                    out=scr[ms, 1], in_=buf[:, E:2 * E]),
                  waits=[(v_p, lm)], inc=d_as[par], inc_n=16)
            as_mark[par] = d_as[par].n

        for i in range(N_ITER):
            sg, ms = divmod(i, NEC)
            par = i % 2
            if i == 0:
                for j in range(NEC):
                    emit_xsplit_chunk(0, j)
                emit_p1_dmas(0)
                emit_p1_splits(0)
            if sg + 1 < NSG and 6 <= ms < 14:
                for j in range(2 * (ms - 6), 2 * (ms - 6) + 2):
                    emit_xsplit_dma(sg + 1, j)
            if i + 1 < N_ITER:
                emit_p1_dmas(i + 1)
                if (i + 1) < NEC:
                    emit_p1_splits(i + 1)

            for proj, wb, ps in (("g", wgs, p_gate), ("v", wvs, p_v), ("q", wqs, p_que)):
                waits = [(d_f, glob_ld),
                         (s_p, xs_mark[sg][0]), (v_p, xs_mark[sg][1])]
                if sg == 0:
                    hm, lm = w_split[(i, proj)]
                    waits += [(s_p, hm), (v_p, lm)]
                else:
                    waits.append((d_w[par], w_mark[par]))
                if i >= 2:
                    kind = {"g": "gate_evac", "v": "store_mult", "q": "que_evac"}[proj]
                    ctr = s_p if proj in ("g", "q") else v_p
                    waits.append((ctr, evac_done[(i - 2, kind)]))

                def mm_run(eng, wb=wb[par], ps=ps[par],
                           xh=xgh[sg % 2], xl=xgl[sg % 2]):
                    ins = None
                    for j in range(NEC):
                        ins = eng.matmul(ps[:, :], wb[:, j * 128:(j + 1) * 128],
                                         xh[:, j * SGRP:(j + 1) * SGRP],
                                         start=(j == 0), stop=False)
                        ins = eng.matmul(ps[:, :], wb[:, E + j * 128:E + (j + 1) * 128],
                                         xh[:, j * SGRP:(j + 1) * SGRP],
                                         start=False, stop=False)
                        ins = eng.matmul(ps[:, :], wb[:, j * 128:(j + 1) * 128],
                                         xl[:, j * SGRP:(j + 1) * SGRP],
                                         start=False, stop=(j == NEC - 1))
                    return ins
                P("t", mm_run, waits=waits, inc=t_p)
                proj_done[(i, proj)] = t_p.n

            gw = [(t_p, proj_done[(i, "g")])]
            if i >= 2:
                gw.append((v_p, evac_done[(i - 2, "store_mult")]))
            P("s", lambda e, ps=p_gate[par], o=gate_sb[par], ms=ms: e.activation(
                o[:, :], ps[:, :], AF.Sigmoid, bias=bg_t[:, ms:ms + 1]),
              waits=gw, inc=s_p)
            evac_done[(i, "gate_evac")] = s_p.n
            qw = [(t_p, proj_done[(i, "q")])]
            if i >= 2:
                qw.append((v_p, evac_done[(i - 2, "load")]))
            P("s", lambda e, ps=p_que[par], o=que_sb[par], ms=ms: e.activation(
                o[:, :], ps[:, :], AF.Sigmoid, bias=bq_t[:, ms:ms + 1]),
              waits=qw, inc=s_p)
            evac_done[(i, "que_evac")] = s_p.n

            P("v", lambda e, ps=p_v[par], g=gate_sb[par], o=store_sb[par]: e.tensor_tensor(
                o[:, :], ps[:, :], g[:, :], ALU.mult),
              waits=[(t_p, proj_done[(i, "v")]), (s_p, evac_done[(i, "gate_evac")])],
              inc=v_p)
            evac_done[(i, "store_mult")] = v_p.n
            fence_pre = v_p.n
            if sg == 0:
                init = 0.0
            else:
                init = carry[:, ms:ms + 1]
            P("v", lambda e, st=store_sb[par], m=mem_sb[par], init=init: e.tensor_tensor_scan(
                m[:, :], dec_t[:, :], st[:, :], init, ALU.mult, ALU.add),
              waits=[(v_p, fence_pre)], inc=v_p)
            fence_post = v_p.n
            P("v", lambda e, m=mem_sb[par], ms=ms: e.tensor_copy(
                carry[:, ms:ms + 1], m[:, SGRP - 1:SGRP]),
              waits=[(v_p, fence_post)], inc=v_p)
            lw = [(v_p, fence_post), (s_p, evac_done[(i, "que_evac")])]
            if i >= 2 and evac_done.get((i - 2, "lh_cast"), 0):
                lw.append((g_p, evac_done[(i - 2, "lh_cast")]))
            P("v", lambda e, m=mem_sb[par], q=que_sb[par], o=load_sb[par]: e.scalar_tensor_tensor(
                o[:, :], m[:, :], 0.05, q[:, :], ALU.mult, ALU.mult),
              waits=lw, inc=v_p)
            load_ready = v_p.n
            evac_done[(i, "load")] = load_ready

            if not ABL_NO_LHLL:
              sw = [(v_p, load_ready)]
              if i >= 2:
                sw.append((d_e[par], store_done[i - 2]))
              P("g2", lambda e, o=load_sb[par], h=lh_sb[par]: e.tensor_copy(
                h[:, :], o[:, :]), waits=sw)
              evac_done[(i, "lh_cast")] = g_p.n
              P("g2", lambda e, o=load_sb[par], h=lh_sb[par], l=ll_sb[par]: e.tensor_tensor(
                l[:, :], o[:, :], h[:, :], ALU.subtract))
              ll_mark = g_p.n

              P("g", lambda e, o=lh_sb[par], ms=ms, sg=sg: e.dma_start(
                out=load_hl_scr[sg * 4:(sg + 1) * 4, 0, :, ms * 128:(ms + 1) * 128]
                    .rearrange("c p s -> p c s"),
                in_=o[:, :].rearrange("p (c s) -> p c s", s=128)),
                waits=[(g_p, evac_done[(i, "lh_cast")])], inc=d_e[par], inc_n=16)
              P("g", lambda e, o=ll_sb[par], ms=ms, sg=sg: e.dma_start(
                out=load_hl_scr[sg * 4:(sg + 1) * 4, 1, :, ms * 128:(ms + 1) * 128]
                    .rearrange("c p s -> p c s"),
                in_=o[:, :].rearrange("p (c s) -> p c s", s=128)),
                waits=[(g_p, ll_mark)], inc=d_e[par], inc_n=16)
              store_done[i] = d_e[par].n

            if sg + 1 < NSG and 6 <= ms < 14:
                for j in range(2 * (ms - 6), 2 * (ms - 6) + 2):
                    emit_xsplit_ops(sg + 1, j)
            if i + 1 >= NEC and i + 1 < N_ITER:
                emit_p1_splits(i + 1)

            while pw_state["emitted"] * N_ITER < len(pw_items) * (i + 1):
                emit_pw_step()

        while pw_state["emitted"] < len(pw_items):
            emit_pw_step()

        p1_stack.close()
        p1_psum.close()
        barrier()

        p2_stack = ExitStack()
        wo_s = p2_stack.enter_context(nc.sbuf_tensor("wo_s", [128, 2 * NEC * E], BF16))
        lc = [p2_stack.enter_context(nc.sbuf_tensor(f"lc{i}", [128, 2 * E], BF16)) for i in range(2)]
        xc = p2_stack.enter_context(nc.sbuf_tensor("xc", [128, E], F32))
        ry = p2_stack.enter_context(nc.sbuf_tensor("ry", [128, E], F32))
        yfh = p2_stack.enter_context(nc.sbuf_tensor("yfh", [128, E], BF16))
        yfl = p2_stack.enter_context(nc.sbuf_tensor("yfl", [128, E], BF16))
        garb = p2_stack.enter_context(nc.sbuf_tensor("garb", [128, E], BF16))
        p2_psum = ExitStack()
        p_attn2 = [p2_psum.enter_context(nc.psum_tensor(f"p_attn{i}", [128, E], F32)) for i in range(2)]

        WOL = NEC * E
        for hl in range(2):
            P("y", lambda e, hl=hl: e.dma_start(
                out=wo_s[:, hl * WOL:(hl + 1) * WOL].rearrange("p (j e) -> p j e", e=E),
                in_=wo_hl_scr[:, hl].rearrange("j p e -> p j e")), inc=d_f, inc_n=16)
        wo_mark = d_f.n

        lc_mark = [0, 0]
        xc_mark = [0]
        attn_done = {}
        tr_done = {}
        r_done = {}
        y_done = {}
        yfh_done = {}
        yfl_done = {}
        yf_dma = {}

        def emit_p2_loads(c):
            par = c % 2
            w = []
            if c >= 2:
                w.append((t_p, attn_done[c - 2]))
            P("y", lambda e, c=c, b=lc[par]: e.dma_start(
                out=b[:, :].rearrange("p (hl m) -> p hl m", m=E),
                in_=load_hl_scr[c].rearrange("hl p m -> p hl m")),
              waits=w, inc=d_w[par], inc_n=16)
            lc_mark[par] = d_w[par].n
            w = []
            if c >= 1:
                w.append((v_p, r_done[c - 1]))
            P("y", lambda e, c=c: e.dma_start(
                out=xc[:, :], in_=x_tm[c * 128:(c + 1) * 128, :]),
              waits=w, inc=d_x[0], inc_n=16)
            xc_mark[0] = d_x[0].n

        emit_p2_loads(0)

        def emit_p2_transposes(c):
            pt = p_attn2[c % 2]
            w = [(v_p, y_done[c])]

            def tr_run(eng, pt=pt):
                ins = None
                for j in range(NEC):
                    ins = eng.transpose(pt[:, j * 128:(j + 1) * 128],
                                        ry[:, j * 128:(j + 1) * 128], ident[:, :])
                return ins
            P("t", tr_run, waits=w, inc=t_p)
            tr_done[c] = t_p.n
            wv_ = [(t_p, tr_done[c])]
            if c >= 1:
                wv_.append((d_e[0], yf_dma.get(c - 1, 0)))
            P("s", lambda e, pt=pt: e.activation(yfh[:, :], pt[:, :], AF.Copy),
              waits=wv_, inc=s_p)
            yfh_done[c] = s_p.n
            P("v", lambda e, pt=pt: e.tensor_tensor(yfl[:, :], pt[:, :], yfh[:, :], ALU.subtract),
              waits=[(s_p, yfh_done[c])], inc=v_p)
            yfl_done[c] = v_p.n
            P("g", lambda e, c=c: e.dma_start(
                out=yfh_scr.rearrange("j p s -> p j s")[:, :, c * 128:(c + 1) * 128],
                in_=yfh[:, :].rearrange("p (j s) -> p j s", s=128)),
              waits=[(s_p, yfh_done[c])], inc=d_e[0], inc_n=16)
            P("g", lambda e, c=c: e.dma_start(
                out=yfl_scr.rearrange("j p s -> p j s")[:, :, c * 128:(c + 1) * 128],
                in_=yfl[:, :].rearrange("p (j s) -> p j s", s=128)),
              waits=[(v_p, yfl_done[c])], inc=d_e[0], inc_n=16)
            yf_dma[c] = d_e[0].n

        for c in range(NSC):
            par = c % 2
            waits = [(d_w[par], lc_mark[par]), (d_f, wo_mark)]
            if c >= 2:
                waits += [(s_p, yfh_done[c - 2]), (v_p, yfl_done[c - 2])]

            def attn_run(eng, b=lc[par], pa=p_attn2[c % 2]):
                ins = None
                WOL = NEC * E
                for et in range(4):
                    for j in range(NEC):
                        slh = slice(j * E + et * 512, j * E + (et + 1) * 512)
                        sll = slice(WOL + j * E + et * 512, WOL + j * E + (et + 1) * 512)
                        ins = eng.matmul(pa[:, et * 512:(et + 1) * 512],
                                         b[:, j * 128:(j + 1) * 128], wo_s[:, slh],
                                         start=(j == 0), stop=False)
                        ins = eng.matmul(pa[:, et * 512:(et + 1) * 512],
                                         b[:, E + j * 128:E + (j + 1) * 128], wo_s[:, slh],
                                         start=False, stop=False)
                        ins = eng.matmul(pa[:, et * 512:(et + 1) * 512],
                                         b[:, j * 128:(j + 1) * 128], wo_s[:, sll],
                                         start=False, stop=(j == NEC - 1))
                return ins
            P("t", attn_run, waits=waits, inc=t_p)
            attn_done[c] = t_p.n

            if c >= 1:
                emit_p2_transposes(c - 1)

            rw = [(t_p, attn_done[c]), (d_x[0], xc_mark[0])]
            if c >= 1:
                rw.append((t_p, tr_done[c - 1]))
            P("v", lambda e, pa=p_attn2[c % 2]: e.scalar_tensor_tensor(
                ry[:, :], pa[:, :], 1.0, xc[:, :], ALU.mult, ALU.add,
                accum_out=stats[:, 0:1]),
              waits=rw, inc=v_p)
            r_done[c] = v_p.n
            if c + 1 < NSC:
                emit_p2_loads(c + 1)
            P("s", lambda e: e.activation(garb[:, :], ry[:, :], AF.Square,
                                          accum_out=stats[:, 1:2]),
              waits=[(v_p, r_done[c])], inc=s_p)
            sq_mark = s_p.n
            P("v", lambda e: e.tensor_scalar_mul(stats[:, 2:4], stats[:, 0:2], 1.0 / E),
              waits=[(s_p, sq_mark)])
            P("v", lambda e: e.tensor_tensor(stats[:, 4:5], stats[:, 2:3], stats[:, 2:3], ALU.mult))
            P("v", lambda e: e.tensor_scalar(stats[:, 4:5], stats[:, 4:5], -1.0, EPS,
                                             ALU.mult, ALU.add))
            P("v", lambda e: e.tensor_tensor(stats[:, 5:6], stats[:, 4:5], stats[:, 3:4], ALU.add))
            P("s", lambda e: e.activation(stats[:, 7:8], stats[:, 5:6], AF.Sqrt),
              waits=[(v_p, v_p.n)])
            sqrt_mark = s_p.n
            P("v", lambda e: e.reciprocal(stats[:, 11:12], stats[:, 7:8]),
              waits=[(s_p, sqrt_mark)])
            P("v", lambda e: e.tensor_scalar(ry[:, :], ry[:, :], stats[:, 2:3], stats[:, 11:12],
                                             ALU.subtract, ALU.mult))
            y_done[c] = v_p.n

        emit_p2_transposes(NSC - 1)
        p2_stack.close()
        p2_psum.close()
        barrier()

        p3_stack = ExitStack()
        yfh_s = [p3_stack.enter_context(nc.sbuf_tensor(f"yfh_s{i}", [128, NEC * SB], BF16)) for i in range(2)]
        yfl_s = [p3_stack.enter_context(nc.sbuf_tensor(f"yfl_s{i}", [128, NEC * SB], BF16)) for i in range(2)]
        hh_h = p3_stack.enter_context(nc.sbuf_tensor("hh_h", [128, NHC * SB], BF16))
        hh_l = p3_stack.enter_context(nc.sbuf_tensor("hh_l", [128, NHC * SB], BF16))
        wgt_s = [p3_stack.enter_context(nc.sbuf_tensor(f"wgt{i}", [128, E], BF16)) for i in range(2)]
        wut_s = [p3_stack.enter_context(nc.sbuf_tensor(f"wut{i}", [128, E], BF16)) for i in range(2)]
        wdt_s = [p3_stack.enter_context(nc.sbuf_tensor(f"wdt{i}", [128, NHC * 128], BF16)) for i in range(2)]
        gsil = [p3_stack.enter_context(nc.sbuf_tensor(f"gsil{i}", [128, SB], F32)) for i in range(2)]
        hhf = [p3_stack.enter_context(nc.sbuf_tensor(f"hhf{i}", [128, SB], F32)) for i in range(2)]
        r2 = p3_stack.enter_context(nc.sbuf_tensor("r2", [128, NEC * SB], F32))
        r2h = p3_stack.enter_context(nc.sbuf_tensor("r2h", [128, NEC * SB], BF16))
        r2l = p3_stack.enter_context(nc.sbuf_tensor("r2l", [128, NEC * SB], BF16))
        r2sq = p3_stack.enter_context(nc.sbuf_tensor("r2sq", [128, NEC * SB], BF16))
        bc = p3_stack.enter_context(nc.sbuf_tensor("bc", [128, 2 * SB], F32))
        p3_psum = ExitStack()
        p_g = [p3_psum.enter_context(nc.psum_tensor(f"p3g{i}", [128, SB], F32)) for i in range(2)]
        p_u = [p3_psum.enter_context(nc.psum_tensor(f"p3u{i}", [128, SB], F32)) for i in range(2)]
        p_f = [p3_psum.enter_context(nc.psum_tensor(f"p3f{i}", [128, SB], F32)) for i in range(2)]
        p_stat = p3_psum.enter_context(nc.psum_tensor("p_stat", [128, 2 * SB], F32))
        p_bc = p3_psum.enter_context(nc.psum_tensor("p_bc", [128, 2 * SB], F32))

        yf_mark = [0]
        wgu_mark = [0, 0]
        wdt_mark = [0, 0]
        gu_done = {}
        silu_done = {}
        hhf_done = {}
        hhh_done = {}
        hhl_done = {}
        down_done = {}
        r2_done = {}
        out_dma = {}
        stat_done = {}

        def emit_ln2(sb):
                split_mark = split_marks_ln2[sb]

                def stat_run(eng):
                    first = True
                    for dc in range(NEC):
                        for part in (r2h, r2l):
                            eng.matmul(p_stat[0:1, 0:SB], onc[:, :],
                                       part[:, dc * SB:(dc + 1) * SB],
                                       start=first, stop=(dc == NEC - 1 and part is r2l))
                            first = False
                    ins = None
                    for dc in range(NEC):
                        ins = eng.matmul(p_stat[0:1, SB:2 * SB], onc[:, :],
                                         r2sq[:, dc * SB:(dc + 1) * SB],
                                         start=(dc == 0), stop=(dc == NEC - 1))
                    return ins
                P("t", stat_run, waits=[(v_p, split_mark)], inc=t_p)
                stat_done[sb] = t_p.n

                P("v", lambda e: e.tensor_scalar_mul(srow[0:1, 0:2 * SB], p_stat[0:1, 0:2 * SB], 1.0 / E),
                  waits=[(t_p, stat_done[sb])])
                P("v", lambda e: e.tensor_tensor(srow[0:1, 2 * SB:3 * SB], srow[0:1, 0:SB], srow[0:1, 0:SB], ALU.mult))
                P("v", lambda e: e.tensor_scalar(srow[0:1, 2 * SB:3 * SB], srow[0:1, 2 * SB:3 * SB],
                                                 -1.0, EPS, ALU.mult, ALU.add))
                P("v", lambda e: e.tensor_tensor(srow[0:1, SB:2 * SB], srow[0:1, 2 * SB:3 * SB],
                                                 srow[0:1, SB:2 * SB], ALU.add))
                P("s", lambda e: e.activation(srow[0:1, 2 * SB:3 * SB], srow[0:1, SB:2 * SB], AF.Sqrt),
                  waits=[(v_p, v_p.n)])
                P("v", lambda e: e.reciprocal(srow[0:1, SB:2 * SB], srow[0:1, 2 * SB:3 * SB]),
                  waits=[(s_p, s_p.n)])
                P("v", lambda e: e.tensor_copy(srow_bf[0:1, 0:2 * SB], srow[0:1, 0:2 * SB]))
                P("v", lambda e: e.tensor_tensor(srow[0:1, 2 * SB:4 * SB], srow[0:1, 0:2 * SB],
                                                 srow_bf[0:1, 0:2 * SB], ALU.subtract))
                P("v", lambda e: e.tensor_copy(srow_bf[0:1, 2 * SB:4 * SB], srow[0:1, 2 * SB:4 * SB]))
                rows_mark = v_p.n

                def bcast_run(eng):
                    eng.matmul(p_bc[:, 0:SB], onr[0:1, :], srow_bf[0:1, 0:SB], start=True, stop=False)
                    eng.matmul(p_bc[:, 0:SB], onr[0:1, :], srow_bf[0:1, 2 * SB:3 * SB], start=False, stop=True)
                    eng.matmul(p_bc[:, SB:2 * SB], onr[0:1, :], srow_bf[0:1, SB:2 * SB], start=True, stop=False)
                    return eng.matmul(p_bc[:, SB:2 * SB], onr[0:1, :], srow_bf[0:1, 3 * SB:4 * SB], start=False, stop=True)
                P("t", bcast_run, waits=[(v_p, rows_mark)], inc=t_p)
                bc_mark = t_p.n

                P("v", lambda e: e.tensor_copy(bc[:, :], p_bc[:, :]),
                  waits=[(t_p, bc_mark)], inc=v_p)
                bc_ev = v_p.n

                for dc in range(NEC):
                    P("v", lambda e, dc=dc: e.tensor_tensor(
                        r2[:, dc * SB:(dc + 1) * SB], r2[:, dc * SB:(dc + 1) * SB],
                        bc[:, 0:SB], ALU.subtract))
                    P("v", lambda e, dc=dc: e.tensor_tensor(
                        r2[:, dc * SB:(dc + 1) * SB], r2[:, dc * SB:(dc + 1) * SB],
                        bc[:, SB:2 * SB], ALU.mult))
                norm_mark = v_p.n

                P("g", lambda e, sb=sb: e.dma_start(
                    out=out_fm.rearrange("(j p) s -> p j s", p=128)[:, :, sb * SB:(sb + 1) * SB],
                    in_=r2[:, :].rearrange("p (j s) -> p j s", s=SB)),
                  waits=[(v_p, norm_mark)], inc=d_e[1], inc_n=16)
                out_dma[sb] = d_e[1].n

        pending_ln2 = []
        split_marks_ln2 = {}

        yf_marks = {}
        wgu_marks = {}
        wdt_marks = {}

        def emit_yf_loads(sb):
            sp3 = sb % 2
            w = []
            if sb >= 2:
                w.append((t_p, gu_done[(sb - 2, NHC - 1, "u")]))
                w.append((v_p, r2_done[(sb - 2, NEC - 1)]))
            for scr, dst in ((yfh_scr, yfh_s[sp3]), (yfl_scr, yfl_s[sp3])):
                P("y", lambda e, scr=scr, dst=dst, sb=sb: e.dma_start(
                    out=dst[:, :].rearrange("p (j s) -> p j s", s=SB),
                    in_=scr.rearrange("j p s -> p j s")[:, :, sb * SB:(sb + 1) * SB]),
                  waits=w, inc=d_x[sp3], inc_n=16)
                w = []
            yf_marks[sb] = d_x[sp3].n

        def emit_wgu_load(sb, hc):
            par = hc % 2
            w = []
            if hc >= 2:
                w.append((t_p, gu_done[(sb, hc - 2, "u")]))
            elif sb >= 1:
                w.append((t_p, gu_done[(sb - 1, NHC - 2 + hc, "u")]))
            for scr, buf in ((wgT_sw, wgt_s[par]), (wuT_sw, wut_s[par])):
                P("y", lambda e, scr=scr, b=buf, hc=hc: e.dma_start(
                    out=b[:, :], in_=scr[hc]),
                  waits=w, inc=d_w[par], inc_n=16)
                w = []
            wgu_marks[(sb, hc)] = d_w[par].n

        for sb in range(NSB):
            sp3 = sb % 2
            if sb not in yf_marks:
                emit_yf_loads(sb)

            for hc in range(NHC):
                par = hc % 2
                if (sb, hc) not in wgu_marks:
                    emit_wgu_load(sb, hc)

                for proj, wbuf, ps in (("g", wgt_s, p_g), ("u", wut_s, p_u)):
                    waits = [(d_w[par], wgu_marks[(sb, hc)]), (d_x[sp3], yf_marks[sb])]
                    if hc >= 2:
                        ctr, key = ((s_p, "silu") if proj == "g" else (v_p, "hhf"))
                        mark = silu_done[(sb, hc - 2)] if proj == "g" else hhf_done[(sb, hc - 2)]
                        waits.append((ctr, mark))

                    def gu_run(eng, wbuf=wbuf[par], ps=ps[par],
                               yfh=yfh_s[sp3], yfl=yfl_s[sp3]):
                        first = True
                        ins = None
                        for j in range(NEC):
                            for yb in (yfh, yfl):
                                ins = eng.matmul(ps[:, :], wbuf[:, j * 128:(j + 1) * 128],
                                                 yb[:, j * SB:(j + 1) * SB],
                                                 start=first, stop=(j == NEC - 1 and yb is yfl))
                                first = False
                        return ins
                    P("t", gu_run, waits=waits, inc=t_p)
                    gu_done[(sb, hc, proj)] = t_p.n

                P("s", lambda e, ps=p_g[par], o=gsil[par], hc=hc: e.activation(
                    o[:, :], ps[:, :], AF.Silu, scale=sg_t[:, hc:hc + 1]),
                  waits=[(t_p, gu_done[(sb, hc, "g")])], inc=s_p)
                silu_done[(sb, hc)] = s_p.n
                P("v", lambda e, ps=p_u[par], g=gsil[par], o=hhf[par], hc=hc: e.scalar_tensor_tensor(
                    o[:, :], ps[:, :], su_t[:, hc:hc + 1], g[:, :], ALU.mult, ALU.mult),
                  waits=[(t_p, gu_done[(sb, hc, "u")]), (s_p, silu_done[(sb, hc)])],
                  inc=v_p)
                hhf_done[(sb, hc)] = v_p.n
                w_extra = []
                if sb >= 1 and hc == 0:
                    w_extra = [(t_p, down_done[(sb - 1, 15)])]
                P("s", lambda e, src=hhf[par], hc=hc: e.activation(
                    hh_h[:, hc * SB:(hc + 1) * SB], src[:, :], AF.Copy),
                  waits=[(v_p, hhf_done[(sb, hc)])] + w_extra, inc=s_p)
                hhh_done[(sb, hc)] = s_p.n
                P("v", lambda e, src=hhf[par], hc=hc: e.tensor_tensor(
                    hh_l[:, hc * SB:(hc + 1) * SB], src[:, :],
                    hh_h[:, hc * SB:(hc + 1) * SB], ALU.subtract),
                  waits=[(s_p, hhh_done[(sb, hc)])], inc=v_p)
                hhl_done[(sb, hc)] = v_p.n
                if hc == 3 and pending_ln2:
                    emit_ln2(pending_ln2.pop(0))

            def emit_wdt_load(dc):
                par = dc % 2
                w = []
                if dc >= 2:
                    w.append((t_p, down_done[(sb, dc - 2)]))
                elif sb >= 1:
                    w.append((t_p, down_done[(sb - 1, dc + NEC - 2)]))
                dch = d_st[par]
                P("y", lambda e, b=wdt_s[par], dc=dc: e.dma_start(
                    out=b[:, :], in_=wdT_sw[dc]),
                  waits=w, inc=dch, inc_n=16)
                wdt_mark[par] = dch.n
                wdt_marks[(sb, dc)] = dch.n
            emit_wdt_load(0)
            emit_wdt_load(1)
            if sb + 1 < NSB:
                emit_yf_loads(sb + 1)
                emit_wgu_load(sb + 1, 0)
                emit_wgu_load(sb + 1, 1)
            for dc in range(NEC):
                par = dc % 2
                if (sb, dc) not in wdt_marks:
                    emit_wdt_load(dc)

                waits = [(d_st[par], wdt_marks[(sb, dc)]),
                         (v_p, hhl_done[(sb, NHC - 1)])]
                if dc >= 2:
                    waits.append((v_p, r2_done[(sb, dc - 2)]))

                def down_run(eng, b=wdt_s[par], ps=p_f[par]):
                    first = True
                    ins = None
                    for j in range(NHC):
                        for hb in (hh_h, hh_l):
                            ins = eng.matmul(ps[:, :], b[:, j * 128:(j + 1) * 128],
                                             hb[:, j * SB:(j + 1) * SB],
                                             start=first, stop=(j == NHC - 1 and hb is hh_l))
                            first = False
                    return ins
                P("t", down_run, waits=waits, inc=t_p)
                down_done[(sb, dc)] = t_p.n

                w = [(t_p, down_done[(sb, dc)])]
                if sb >= 1 and dc == 0:
                    w.append((d_e[1], out_dma[sb - 1]))
                P("v", lambda e, ps=p_f[par], dc=dc, yy=yfh_s[sp3]: e.scalar_tensor_tensor(
                    r2[:, dc * SB:(dc + 1) * SB], ps[:, :], sd_t[:, dc:dc + 1],
                    yy[:, dc * SB:(dc + 1) * SB], ALU.mult, ALU.add),
                  waits=w, inc=v_p)
                P("v", lambda e, dc=dc, yy=yfl_s[sp3]: e.tensor_tensor(
                    r2[:, dc * SB:(dc + 1) * SB], r2[:, dc * SB:(dc + 1) * SB],
                    yy[:, dc * SB:(dc + 1) * SB], ALU.add),
                  inc=v_p)
                r2_done[(sb, dc)] = v_p.n
                P("v", lambda e, dc=dc: e.tensor_copy(
                    r2h[:, dc * SB:(dc + 1) * SB], r2[:, dc * SB:(dc + 1) * SB]))
                P("v", lambda e, dc=dc: e.tensor_tensor(
                    r2l[:, dc * SB:(dc + 1) * SB], r2[:, dc * SB:(dc + 1) * SB],
                    r2h[:, dc * SB:(dc + 1) * SB], ALU.subtract))
                P("v", lambda e, dc=dc: e.tensor_tensor(
                    r2sq[:, dc * SB:(dc + 1) * SB], r2[:, dc * SB:(dc + 1) * SB],
                    r2[:, dc * SB:(dc + 1) * SB], ALU.mult))
                if dc == NEC - 1:
                    split_marks_ln2[sb] = v_p.n

            pending_ln2.append(sb)


        while pending_ln2:
            emit_ln2(pending_ln2.pop(0))

        p3_stack.close()
        p3_psum.close()

        final_out = d_e[1].n
        prog.streams["g"].append(lambda e, v=final_out: e.wait_ge(d_e[1].h, v))

        block = g_stack.enter_context(nc.Block())

        @block.tensor
        def _(eng):
            for f in prog.streams["t"]:
                f(eng)

        @block.vector
        def _(eng):
            for f in prog.streams["v"]:
                f(eng)

        @block.scalar
        def _(eng):
            for f in prog.streams["s"]:
                f(eng)

        @block.sync
        def _(eng):
            for f in prog.streams["y"]:
                f(eng)

        @block.gpsimd
        def _(eng):
            for f in prog.streams["g"]:
                f(eng)

    return nc


_NC_CACHE = None


def kernel(L0_attn_x, L0_attn_Wv, L0_attn_Wg, L0_attn_bg, L0_attn_Wq, L0_attn_bq,
           L0_attn_Wo, L0_ffn_wq_gate, L0_ffn_s_gate, L0_ffn_wq_up, L0_ffn_s_up,
           L0_ffn_wq_down, L0_ffn_s_down):
    global _NC_CACHE
    x = np.asarray(L0_attn_x, dtype=np.float32)
    B = x.shape[0]
    assert x.shape == (8, S, E)

    shared = dict(
        wg=np.ascontiguousarray(np.asarray(L0_attn_Wg, np.float32)),
        wv=np.ascontiguousarray(np.asarray(L0_attn_Wv, np.float32)),
        wq=np.ascontiguousarray(np.asarray(L0_attn_Wq, np.float32)),
        wo=np.ascontiguousarray(np.asarray(L0_attn_Wo, np.float32)),
        bg2d=np.ascontiguousarray(np.asarray(L0_attn_bg, np.float32).reshape(NEC, 128).T),
        bq2d=np.ascontiguousarray(np.asarray(L0_attn_bq, np.float32).reshape(NEC, 128).T),
        wgT_i=np.ascontiguousarray(np.asarray(L0_ffn_wq_gate, np.int32).T),
        wuT_i=np.ascontiguousarray(np.asarray(L0_ffn_wq_up, np.int32).T),
        wdT_i=np.ascontiguousarray(np.asarray(L0_ffn_wq_down, np.int32).T),
        sg2d=np.ascontiguousarray(np.asarray(L0_ffn_s_gate, np.float32).reshape(NHC, 128).T),
        su2d=np.ascontiguousarray(np.asarray(L0_ffn_s_up, np.float32).reshape(NHC, 128).T),
        sd2d=np.ascontiguousarray(np.asarray(L0_ffn_s_down, np.float32).reshape(NEC, 128).T),
        dec_i=np.full((128, SGRP), np.float32(DECAY), dtype=np.float32),
        ident_i=np.eye(128, dtype=np.float32),
        onc_i=np.ones((128, 1), dtype=ml_dtypes.bfloat16),
        onr_i=np.ones((1, 128), dtype=ml_dtypes.bfloat16),
    )
    in_maps = []
    for b in range(B):
        m = dict(shared)
        m["x_tm"] = np.ascontiguousarray(x[b])
        m["xT"] = np.ascontiguousarray(x[b].T)
        in_maps.append(m)

    if _NC_CACHE is None:
        _NC_CACHE = build_nc()
    res = run_bass_kernel_spmd(_NC_CACHE, in_maps, core_ids=list(range(8)))
    out = np.stack([np.asarray(r["out_fm"]).T for r in res.results], axis=0)
    return out
